# version 21
# speedup vs baseline: 1.5409x; 1.0486x over previous
import numpy as np
import ml_dtypes
import concourse.bass as bass
import concourse.mybir as mybir
from concourse.bass_utils import run_bass_kernel_spmd

F32 = mybir.dt.float32
BF16 = mybir.dt.bfloat16
I32 = mybir.dt.int32
AF = mybir.ActivationFunctionType
ALU = mybir.AluOpType

B_TOTAL, S_LEN, D = 16, 2048, 1024
N_HEADS, D_KV = 16, 64
HID = 4096
N_CORES = 8
B_CORE = B_TOTAL // N_CORES
TOK = B_CORE * S_LEN
NT = S_LEN // 128
EPS = 1e-5
SHIFT = 40.0
NUM_BUCKETS, MAX_DIST = 32, 128
U0 = 1920
MU = 3968
GN = 4096

ENGINES = ("sync", "vector", "scalar", "tensor", "gpsimd")


def _bucket_table():
    d = np.arange(-2048, 2048)
    n = -d
    ret = np.where(n < 0, 16, 0).astype(np.int64)
    n = np.abs(n)
    is_small = n < 8
    nf = np.maximum(n, 1).astype(np.float32)
    t = np.log(nf / np.float32(8.0)).astype(np.float32)
    t = (t / np.float32(np.log(MAX_DIST / 8.0))).astype(np.float32)
    t = (t * np.float32(8.0)).astype(np.float32)
    val = 8 + t.astype(np.int32)
    val = np.minimum(val, 15)
    return (ret + np.where(is_small, n, val)).astype(np.int64)


def _sat_range():
    bt = _bucket_table()
    lo_val = bt[0]
    d = -2048
    while bt[d + 1 + 2048] == lo_val:
        d += 1
    dminus = d
    hi_val = bt[2047 + 2048]
    d = 2047
    while bt[d - 1 + 2048] == hi_val:
        d -= 1
    dplus = d
    return dminus, dplus


def host_constants():
    bt = _bucket_table()
    gsel = np.zeros((NUM_BUCKETS, GN), dtype=np.float32)
    i = np.arange(GN)
    dd = 2047 - i
    gsel[bt[dd + 2048], i] = 1.0
    jmat = np.zeros((128, 128), dtype=ml_dtypes.bfloat16)
    jmat[np.arange(128), 127 - np.arange(128)] = 1.0
    sel2 = np.zeros((2, 128), dtype=np.float32)
    sel2[0, 0:64] = 1.0
    sel2[1, 64:128] = 1.0
    eblo = np.zeros((NUM_BUCKETS, 1), dtype=np.float32)
    eblo[bt[0], 0] = 1.0
    return {"gsel_c": gsel, "jmat_c": jmat, "sel2_c": sel2, "eblo_c": eblo}


class Rec:
    def __init__(self):
        self.ops = {e: [] for e in ENGINES}
        self.sem = {}

    def add_sem(self, name, handle):
        self.sem[name] = [handle, 0]

    def emit(self, eng, fn):
        self.ops[eng].append(fn)

    def inc(self, eng, sname):
        s = self.sem[sname]
        s[1] += 1
        h = s[0]
        if eng in ("vector", "scalar", "tensor"):
            self.emit(eng, lambda E, h=h: (E.drain(), E.nop().then_inc(h, 1)))
        else:
            self.emit(eng, lambda E, h=h: E.nop().then_inc(h, 1))
        return s[1]

    def dma(self, eng, sname, out, in_, transpose=False):
        s = self.sem[sname]
        s[1] += 16
        h = s[0]
        if transpose:
            self.emit(eng, lambda E, h=h, out=out, in_=in_:
                      E.dma_start_transpose(out, in_).then_inc(h, 16))
        else:
            self.emit(eng, lambda E, h=h, out=out, in_=in_:
                      E.dma_start(out=out, in_=in_).then_inc(h, 16))
        return s[1]

    def wait(self, eng, sname, val=None):
        s = self.sem[sname]
        v = s[1] if val is None else val
        if sname in ("dS", "dG"):
            v = s[1]
            issuer = "sync" if sname == "dS" else "gpsimd"
            if not hasattr(self, "_sw"):
                self._sw = {}
            if self._sw.get(sname, -1) < v:
                self._sw[sname] = v
                if eng != issuer:
                    self.emit(issuer, lambda E, h=s[0], v=v: E.wait_ge(h, v))
        self.emit(eng, lambda E, h=s[0], v=v: E.wait_ge(h, v))

    def val(self, sname):
        return self.sem[sname][1]

    def guard(self, eng, key):
        if key in getattr(self, "bguard", {}):
            sname, v = self.bguard[key]
            self.wait(eng, sname, v)

    def set_guard(self, key, sname, v):
        if not hasattr(self, "bguard"):
            self.bguard = {}
        self.bguard[key] = (sname, v)

    def barrier(self):
        self.wait("sync", "dS")
        self.wait("gpsimd", "dG")
        for e in ENGINES:
            self.inc(e, "ph_" + e)
        for e in ENGINES:
            for o in ENGINES:
                if o != e:
                    self.wait(e, "ph_" + o)


def build_nc(debug=False):
    nc = bass.Bass("TRN2", target_bir_lowering=False, debug=False)
    r = Rec()

    x_in = nc.dram_tensor("x_in", [TOK, D], F32, kind="ExternalInput").ap()
    wq_in = nc.dram_tensor("wq_in", [D, D], F32, kind="ExternalInput").ap()
    wk_in = nc.dram_tensor("wk_in", [D, D], F32, kind="ExternalInput").ap()
    wv_in = nc.dram_tensor("wv_in", [D, D], F32, kind="ExternalInput").ap()
    wo_in = nc.dram_tensor("wo_in", [D, D], F32, kind="ExternalInput").ap()
    rel_in = nc.dram_tensor("rel_in", [NUM_BUCKETS, N_HEADS], F32, kind="ExternalInput").ap()
    w1_in = nc.dram_tensor("w1_in", [HID, D], F32, kind="ExternalInput").ap()
    w2_in = nc.dram_tensor("w2_in", [D, HID], F32, kind="ExternalInput").ap()
    gsel_c = nc.dram_tensor("gsel_c", [NUM_BUCKETS, GN], F32, kind="ExternalInput").ap()
    jmat_c = nc.dram_tensor("jmat_c", [128, 128], BF16, kind="ExternalInput").ap()
    sel2_c = nc.dram_tensor("sel2_c", [2, 128], F32, kind="ExternalInput").ap()
    eblo_c = nc.dram_tensor("eblo_c", [NUM_BUCKETS, 1], F32, kind="ExternalInput").ap()
    y_out = nc.dram_tensor("y_out", [TOK, D], F32, kind="ExternalOutput").ap()

    KD = "ExternalOutput" if debug else "Internal"
    g_dram = nc.dram_tensor("g_dram", [N_HEADS, GN], BF16, kind=KD).ap()
    wqkv_dram = [nc.dram_tensor(f"w{n}_bf", [D, D], BF16, kind=KD).ap()
                 for n in ("q", "k", "v", "o")]
    w1q_dram = nc.dram_tensor("w1q_dram", [HID, D], BF16, kind=KD).ap()
    w2q_dram = nc.dram_tensor("w2q_dram", [D, HID], BF16, kind=KD).ap()
    ln1_dram = nc.dram_tensor("ln1_dram", [S_LEN, D], BF16, kind=KD).ap()
    vt_dram = nc.dram_tensor("vt_dram", [N_HEADS * 80, S_LEN], BF16, kind=KD).ap()
    ot_dram = nc.dram_tensor("ot_dram", [D, S_LEN], BF16, kind=KD).ap()
    yat_dram = nc.dram_tensor("yat_dram", [D, S_LEN], BF16, kind=KD).ap()
    yat2_dram = nc.dram_tensor("yat2_dram", [D, S_LEN], BF16, kind=KD).ap()
    x1_dram = nc.dram_tensor("x1_dram", [S_LEN, D], F32, kind=KD).ap()
    xq_dram = nc.dram_tensor("xq_dram", [S_LEN, D], BF16, kind=KD).ap()
    hq_dram = nc.dram_tensor("hq_dram", [S_LEN, HID], BF16, kind=KD).ap()

    import contextlib
    es = contextlib.ExitStack()
    with es:
        def sb(name, shape, dt):
            return es.enter_context(nc.sbuf_tensor(name, shape, dt))

        neg40c = sb("neg40c", [128, 1], F32)
        halfc = sb("halfc", [128, 1], F32)
        zeroc = sb("zeroc", [128, 1], F32)
        epscl = sb("epscl", [128, 1], F32)
        onec = sb("onec", [128, 1], F32)
        onesrow = sb("onesrow", [1, 128], F32)
        j_sb = sb("j_sb", [128, 128], BF16)
        sel2_sb = sb("sel2_sb", [2, 128], F32)
        rel_sb = sb("rel_sb", [NUM_BUCKETS, N_HEADS], F32)
        s1col = sb("s1col", [128, 1], F32)
        s2col = sb("s2col", [128, 1], F32)
        rscol = sb("rscol", [128, 1], F32)
        sxcols = sb("sxcols", [128, NT], F32)
        sc2cols = sb("sc2cols", [128, NT], F32)
        relrow = sb("relrow", [1, 2 * N_HEADS], F32)
        bcol_sb = sb("bcol_sb", [128, 2 * N_HEADS], F32)
        nrel15 = sb("nrel15", [128, 1], F32)
        eblo_sb = sb("eblo_sb", [NUM_BUCKETS, 1], F32)
        t11 = sb("t11", [1, 8], F32)
        t11b = sb("t11b", [1, 8], F32)
        t11c = sb("t11c", [1, 8], F32)
        stc = {n: sb("stc_" + n, [128, 1], F32) for n in
               ("s", "q", "mneg", "vq", "m2", "var", "veps", "rec", "r", "mr",
                "mcol", "mclip", "invm", "s127", "tmp", "tmp2")}
        qst_t = sb("qst_t", [128, 1024], F32)
        qst_c = sb("qst_c", [128, 1024], I32)
        qst_b = sb("qst_b", [128, 1024], F32)
        qst_o = sb("qst_o", [128, 1024], BF16)
        qst_u = sb("qst_u", [128, 1024], F32)
        xtile = [sb(f"xtile{i}", [128, D], F32) for i in range(2)]
        yatile = [sb(f"yatile{i}", [128, D], BF16) for i in range(2)]
        x1t = sb("x1t", [128, D], F32)
        lnt = sb("lnt", [128, D], BF16)
        lntf = sb("lntf", [128, D], F32)
        evst = [sb(f"evst{i}", [128, 512], BF16) for i in range(2)]
        wsum = sb("wsum", [128, 32], F32)
        wsumt = sb("wsumt", [128, 1], F32)
        AR_N = 61440
        AR = sb("AR", [128, AR_N], BF16)
        ARF_N = 9216
        ARF = sb("ARF", [128, ARF_N], F32)

        def carve(off, p, n, dt=BF16):
            if dt == BF16:
                return AR[0:p, off:off + n]
            return ARF[0:p, off:off + n]

        ln1T = [carve(2048 * i, 128, 2048) for i in range(8)]
        vtok = [carve(16384 + 1280 * i, 128, 1280) for i in range(16)]
        m_e = carve(36864, 128, MU)
        m_o = carve(40832, 128, MU)
        qTp = carve(44800, 128, 2048)
        kTp = carve(46848, 128, 2048)
        oTp = carve(48896, 128, 2048)
        ebuf = [carve(50944 + 512 * i, 128, 512) for i in range(4)]
        wslab = [carve(52992 + 1024 * i, 128, 1024) for i in range(2)]
        mpp = carve(50944, 128, MU)
        ew = [carve(50944, 128, 1024), carve(52992, 128, 1024),
              carve(56576, 128, 1024)]
        gstage = carve(55040, N_HEADS, 512)
        ones_big = carve(55552, 128, 1024)
        onesr2k = carve(56576, 16, 2048)
        w1qT = [carve(4096 * i, 128, 4096) for i in range(8)]
        xqT = [carve(36864 + 2048 * i, 128, 2048) for i in range(8)]
        hbuf = carve(53248, 128, 4096)
        hqbuf = carve(57344, 128, 4096)
        w2qT = [carve(1024 * i, 128, 1024) for i in range(32)]
        hqT = [carve(36864 + 512 * i, 128, 512) for i in range(32)]
        woslab = [carve(36864 + 1024 * i, 128, 1024) for i in range(8)]
        oTld = [carve(45056 + 512 * i, 128, 512) for i in range(8)]
        oTu = carve(0, 128, 2048, F32)
        recB = carve(2048, 128, 2048, F32)
        denst = carve(4096, 128, 1024, F32)
        pairdenP = carve(5120, 128, 32, F32)
        pairrecP = carve(5184, 128, 32, F32)
        pairrec = carve(7168, 2, 2048, F32)
        gselch = carve(0, NUM_BUCKETS, 512, F32)
        wchunk = carve(512, 128, 2048, F32)
        wchunk2 = carve(4096, 128, 2048, F32)
        qst_o2 = carve(50944, 128, 1024)

        PS = es.enter_context(nc.psum_tensor("PS", [128, 4096], F32))
        ps = [PS[:, 512 * i:512 * (i + 1)] for i in range(8)]

        for sname in ("dS", "dG", "V", "S", "T") + tuple("ph_" + e for e in ENGINES):
            r.add_sem(sname, es.enter_context(nc.semaphore(sname)))

        E = r.emit

        def VD():
            E("vector", lambda V: V.drain())

        def tinc():
            s = r.sem["T"]
            s[1] += 1
            return s[0], s[1]


        def floor_chain(t_ap, out_ap, K, post=None):
            E("vector", lambda V, t_ap=t_ap: V.scalar_tensor_tensor(
                out=qst_c[:], in0=t_ap, scalar=0.0, in1=t_ap,
                op0=ALU.add, op1=ALU.max))
            VD()
            for j in range(1, K + 1):
                E("vector", lambda V, t_ap=t_ap, j=j: V.scalar_tensor_tensor(
                    out=qst_b[:], in0=qst_c[:], scalar=float(-j + 1),
                    in1=t_ap, op0=ALU.add, op1=ALU.is_gt))
                if j == 1:
                    E("vector", lambda V, t_ap=t_ap, j=j: V.scalar_tensor_tensor(
                        out=qst_u[:], in0=qst_c[:], scalar=float(j),
                        in1=t_ap, op0=ALU.add, op1=ALU.is_le))
                    VD()
                    E("vector", lambda V: V.tensor_tensor(
                        out=qst_u[:], in0=qst_u[:], in1=qst_b[:], op=ALU.subtract))
                else:
                    VD()
                    E("vector", lambda V: V.tensor_tensor(
                        out=qst_u[:], in0=qst_u[:], in1=qst_b[:], op=ALU.subtract))
                    VD()
                    E("vector", lambda V, t_ap=t_ap, j=j: V.scalar_tensor_tensor(
                        out=qst_b[:], in0=qst_c[:], scalar=float(j),
                        in1=t_ap, op0=ALU.add, op1=ALU.is_le))
                    VD()
                    E("vector", lambda V: V.tensor_tensor(
                        out=qst_u[:], in0=qst_u[:], in1=qst_b[:], op=ALU.add))
                VD()
            E("vector", lambda V, out_ap=out_ap: V.tensor_tensor(
                out=out_ap, in0=qst_c[:], in1=qst_u[:], op=ALU.add))

        E("vector", lambda V: V.memset(neg40c[:], -SHIFT))
        E("vector", lambda V: V.memset(halfc[:], 0.5))
        E("vector", lambda V: V.memset(zeroc[:], 0.0))
        E("vector", lambda V: V.memset(epscl[:], EPS))
        E("vector", lambda V: V.memset(onec[:], 1.0))
        E("vector", lambda V: V.memset(onesrow[:], 1.0))
        E("vector", lambda V: V.memset(ones_big, 1.0))
        E("vector", lambda V: V.memset(onesr2k, 1.0))
        v0 = r.inc("vector", "V")
        r.dma("sync", "dS", j_sb[:], jmat_c)
        r.dma("sync", "dS", sel2_sb[:], sel2_c)
        ld0 = r.dma("sync", "dS", rel_sb[:], rel_in)
        r.wait("sync", "V", v0)
        for h in range(N_HEADS):
            r.dma("sync", "dS", vt_dram[80 * h + 64:80 * h + 80, :], onesr2k)

        for c in range(GN // 512):
            gl = r.dma("sync", "dS", gselch, gsel_c[:, 512 * c:512 * (c + 1)])
            r.wait("tensor", "dS", gl)
            if c == 0:
                r.wait("tensor", "dS", ld0)
            E("tensor", lambda T: T.matmul(PS[0:N_HEADS, 0:512], rel_sb[:], gselch,
                                           start=True, stop=True))
            tv = r.inc("tensor", "T")
            r.wait("scalar", "T", tv)
            E("scalar", lambda S: S.activation(out=gstage, in_=PS[0:N_HEADS, 0:512],
                                               func=AF.Identity, bias=zeroc[0:N_HEADS],
                                               scale=1.0))
            sc = r.inc("scalar", "S")
            r.wait("tensor", "S", sc)
            r.wait("sync", "S", sc)
            st_ = r.dma("sync", "dS", g_dram[:, 512 * c:512 * (c + 1)], gstage)
            r.wait("scalar", "dS", st_)
            r.wait("sync", "dS", st_)

        for w_src, w_dst in zip((wq_in, wk_in, wv_in, wo_in), wqkv_dram):
            for dc in range(8):
                sl = slice(128 * dc, 128 * (dc + 1))
                st = wslab[dc % 2]
                s_ = r.sem["dG"]
                s_[1] += 16
                E("gpsimd", lambda G, w_src=w_src, sl=sl, st=st, h=s_[0]:
                  G.dma_start(out=st, in_=w_src[sl, :]).then_inc(h, 16))
                r.wait("sync", "dG", s_[1])
                sv = r.dma("sync", "dS", w_dst[sl, :], st)
                r.wait("gpsimd", "dS", sv)

        wbufs = (wchunk, wchunk2)
        wbg = {0: None, 1: None}
        obufs = (qst_o, qst_o2)
        obg = {0: None, 1: None}
        wq_state = {"g": 0, "o": 0}

        def wquant(w_src, w_dst, nrows, ncols, scol):
            CW = 2048 if ncols % 2048 == 0 else 1024
            nch = nrows // 128
            ncc = ncols // CW
            E("vector", lambda V: V.memset(wsum[:], 0.0))
            r.inc("vector", "V")
            for ch in range(nch):
                for cc in range(ncc):
                    bi = wq_state["g"] % 2
                    wq_state["g"] += 1
                    buf = wbufs[bi]
                    if wbg[bi]:
                        for sn_, v_ in wbg[bi]:
                            r.wait("sync", sn_, v_)
                    ld = r.dma("sync", "dS", buf[:, 0:CW],
                               w_src[128 * ch:128 * (ch + 1), CW * cc:CW * (cc + 1)])
                    r.wait("vector", "dS", ld)
                    E("vector", lambda V, CW=CW, buf=buf: V.reduce_sum(
                        stc["tmp"][:], buf[:, 0:CW], axis=mybir.AxisListType.X,
                        apply_absolute_value=True))
                    VD()
                    if cc == 0:
                        E("vector", lambda V, ch=ch: V.tensor_tensor(
                            out=wsum[:, ch:ch + 1], in0=stc["tmp"][:], in1=zeroc[:],
                            op=ALU.add))
                    else:
                        E("vector", lambda V, ch=ch: V.tensor_tensor(
                            out=wsum[:, ch:ch + 1], in0=wsum[:, ch:ch + 1],
                            in1=stc["tmp"][:], op=ALU.add))
                    vv = r.inc("vector", "V")
                    wbg[bi] = [("V", vv)]
            E("vector", lambda V: V.drain())
            E("vector", lambda V, n=nch: V.reduce_sum(
                wsumt[:], wsum[:, 0:n], axis=mybir.AxisListType.X))
            vv = r.inc("vector", "V")
            r.wait("tensor", "V", vv)
            E("tensor", lambda T: T.matmul(PS[0:1, 0:1], wsumt[:], onec[:],
                                           start=True, stop=True))
            tv = r.inc("tensor", "T")
            r.wait("scalar", "T", tv)
            E("scalar", lambda S, n=nrows * ncols: S.activation(
                out=t11[0:1, 0:1], in_=PS[0:1, 0:1], func=AF.Identity,
                bias=zeroc[0:1], scale=1.0 / n))
            sc = r.inc("scalar", "S")
            r.wait("vector", "S", sc)
            E("vector", lambda V: V.scalar_tensor_tensor(
                out=t11b[0:1, 0:1], in0=t11[0:1, 0:1], scalar=EPS,
                in1=onec[0:1], op0=ALU.max, op1=ALU.mult))
            VD()
            E("vector", lambda V: V.reciprocal(t11[0:1, 0:1], t11b[0:1, 0:1]))
            vv = r.inc("vector", "V")
            r.wait("tensor", "V", vv)
            E("tensor", lambda T: T.matmul(PS[:, 512:513], onesrow[:], t11[0:1, 0:1],
                                           start=True, stop=True))
            tv = r.inc("tensor", "T")
            r.wait("scalar", "T", tv)
            E("scalar", lambda S: S.copy(rscol[:], PS[:, 512:513]))
            sc = r.inc("scalar", "S")
            r.wait("tensor", "S", sc)
            E("tensor", lambda T: T.matmul(PS[:, 512:513], onesrow[:], t11b[0:1, 0:1],
                                           start=True, stop=True))
            tv = r.inc("tensor", "T")
            r.wait("scalar", "T", tv)
            E("scalar", lambda S, scol=scol: S.copy(scol[:], PS[:, 512:513]))
            sc = r.inc("scalar", "S")
            r.wait("vector", "S", sc)
            for ch in range(nch):
                for cc in range(ncc):
                    bi = wq_state["g"] % 2
                    wq_state["g"] += 1
                    buf = wbufs[bi]
                    if wbg[bi]:
                        for sn_, v_ in wbg[bi]:
                            r.wait("sync", sn_, v_)
                    ld = r.dma("sync", "dS", buf[:, 0:CW],
                               w_src[128 * ch:128 * (ch + 1), CW * cc:CW * (cc + 1)])
                    r.wait("scalar", "dS", ld)
                    E("scalar", lambda S, CW=CW, buf=buf: S.activation(
                        out=buf[:, 0:CW], in_=buf[:, 0:CW], func=AF.Identity,
                        bias=halfc[:], scale=rscol[:]))
                    sc = r.inc("scalar", "S")
                    r.wait("vector", "S", sc)
                    for sub in range(CW // 1024):
                        ssl = slice(1024 * sub, 1024 * (sub + 1))
                        oi = wq_state["o"] % 2
                        wq_state["o"] += 1
                        ob = obufs[oi]
                        if obg[oi] is not None:
                            r.wait("vector", "dS", obg[oi])
                        floor_chain(buf[:, ssl], qst_t[:], 1)
                        VD()
                        E("vector", lambda V, ob=ob: V.scalar_tensor_tensor(
                            out=ob[:], in0=qst_t[:], scalar=-1.0, in1=ones_big,
                            op0=ALU.max, op1=ALU.min))
                        vv = r.inc("vector", "V")
                        r.wait("sync", "V", vv)
                        sv = r.dma("sync", "dS",
                                   w_dst[128 * ch:128 * (ch + 1),
                                         CW * cc + 1024 * sub:CW * cc + 1024 * (sub + 1)],
                                   ob[:])
                        obg[oi] = sv
                    wbg[bi] = [("S", sc), ("V", vv)]

        wquant(w1_in, w1q_dram, HID, D, s1col)
        wquant(w2_in, w2q_dram, D, HID, s2col)
        r.barrier()

        def ln_tile(x_src_ap, dst_dram, tt, quant):
            xt = xtile[tt % 2]
            r.guard("sync", f"xtile{tt % 2}")
            ld = r.dma("sync", "dS", xt[:], x_src_ap)
            r.wait("vector", "dS", ld)
            r.guard("vector", "stc")
            E("vector", lambda V, xt=xt: V.reduce_sum(
                stc["s"][:], xt[:], axis=mybir.AxisListType.X))
            E("vector", lambda V, xt=xt: V.tensor_tensor(
                out=qst_b[:], in0=xt[:], in1=xt[:], op=ALU.mult))
            E("vector", lambda V: V.drain())
            E("vector", lambda V: V.reduce_sum(
                stc["q"][:], qst_b[:], axis=mybir.AxisListType.X))
            VD()
            E("vector", lambda V: V.scalar_tensor_tensor(
                out=stc["mneg"][:], in0=stc["s"][:], scalar=-1.0 / D,
                in1=zeroc[:], op0=ALU.mult, op1=ALU.add))
            E("vector", lambda V: V.scalar_tensor_tensor(
                out=stc["vq"][:], in0=stc["q"][:], scalar=1.0 / D,
                in1=zeroc[:], op0=ALU.mult, op1=ALU.add))
            VD()
            E("vector", lambda V: V.tensor_tensor(
                out=stc["m2"][:], in0=stc["mneg"][:], in1=stc["mneg"][:], op=ALU.mult))
            VD()
            E("vector", lambda V: V.tensor_tensor(
                out=stc["var"][:], in0=stc["vq"][:], in1=stc["m2"][:], op=ALU.subtract))
            VD()
            E("vector", lambda V: V.tensor_tensor(
                out=stc["veps"][:], in0=stc["var"][:], in1=epscl[:], op=ALU.add))
            E("vector", lambda V: V.drain())
            E("vector", lambda V: V.reciprocal(
                stc["rec"][:], stc["veps"][:]))
            vv = r.inc("vector", "V")
            r.wait("scalar", "V", vv)
            E("scalar", lambda S: S.activation(
                out=stc["r"][:], in_=stc["rec"][:], func=AF.Sqrt,
                bias=zeroc[:], scale=1.0))
            sc = r.inc("scalar", "S")
            r.wait("vector", "S", sc)
            VD()
            E("vector", lambda V: V.tensor_tensor(
                out=stc["mr"][:], in0=stc["mneg"][:], in1=stc["r"][:], op=ALU.mult))
            vv = r.inc("vector", "V")
            r.wait("scalar", "V", vv)
            if not quant:
                E("scalar", lambda S, xt=xt: S.activation(
                    out=lnt[:], in_=xt[:], func=AF.Identity,
                    bias=stc["mr"][:], scale=stc["r"][:]))
                sc = r.inc("scalar", "S")
                r.set_guard(f"xtile{tt % 2}", "S", sc)
                r.set_guard("stc", "S", sc)
                r.wait("sync", "S", sc)
                sv = r.dma("sync", "dS", dst_dram[128 * tt:128 * (tt + 1), :], lnt[:])
                r.wait("scalar", "dS", sv)
                return
            E("scalar", lambda S, xt=xt: S.activation(
                out=lntf[:], in_=xt[:], func=AF.Identity,
                bias=stc["mr"][:], scale=stc["r"][:]))
            sc = r.inc("scalar", "S")
            r.set_guard(f"xtile{tt % 2}", "S", sc)
            r.wait("vector", "S", sc)
            E("vector", lambda V: V.reduce_max(
                stc["mcol"][:], lntf[:], axis=mybir.AxisListType.X,
                apply_absolute_value=True))
            VD()
            E("vector", lambda V: V.scalar_tensor_tensor(
                out=stc["mclip"][:], in0=stc["mcol"][:], scalar=EPS,
                in1=onec[:], op0=ALU.max, op1=ALU.mult))
            VD()
            E("vector", lambda V: V.reciprocal(
                stc["invm"][:], stc["mclip"][:]))
            VD()
            E("vector", lambda V: V.scalar_tensor_tensor(
                out=stc["s127"][:], in0=stc["invm"][:], scalar=127.0,
                in1=zeroc[:], op0=ALU.mult, op1=ALU.add))
            E("vector", lambda V: V.tensor_tensor(
                out=stc["tmp2"][:], in0=stc["mclip"][:], in1=s1col[:], op=ALU.mult))
            VD()
            E("vector", lambda V, tt=tt: V.scalar_tensor_tensor(
                out=sxcols[:, tt:tt + 1], in0=stc["tmp2"][:], scalar=1.0 / 127.0,
                in1=zeroc[:], op0=ALU.mult, op1=ALU.add))
            vv = r.inc("vector", "V")
            r.wait("scalar", "V", vv)
            E("scalar", lambda S: S.activation(
                out=qst_t[:], in_=lntf[:], func=AF.Identity,
                bias=halfc[:], scale=stc["s127"][:]))
            sc = r.inc("scalar", "S")
            r.set_guard("stc", "S", sc)
            r.wait("vector", "S", sc)
            floor_chain(qst_t[:], qst_o[:], 2)
            vv = r.inc("vector", "V")
            r.wait("sync", "V", vv)
            sv = r.dma("sync", "dS", dst_dram[128 * tt:128 * (tt + 1), :], qst_o[:])
            r.wait("vector", "dS", sv)
            r.wait("scalar", "dS", sv)

        def batch_program(b):
            def x_rows(tt):
                return x_in[b * S_LEN + 128 * tt: b * S_LEN + 128 * (tt + 1), :]

            for tt in range(NT):
                ln_tile(x_rows(tt), ln1_dram, tt, False)
            r.barrier()

            for dc in range(8):
                for tg in range(4):
                    r.dma("sync", "dS", ln1T[dc][:, 512 * tg:512 * (tg + 1)],
                          ln1_dram[512 * tg:512 * (tg + 1), 128 * dc:128 * (dc + 1)],
                          transpose=True)
            r.wait("sync", "dS")
            ln_ld = r.val("dS")

            r.wait("tensor", "dS", ln_ld)
            for hp in range(8):
                kv0 = 128 * hp
                wl = r.dma("sync", "dS",
                           wslab[hp % 2].rearrange("p (dc c) -> p dc c", c=128),
                           wqkv_dram[2][:, kv0:kv0 + 128].rearrange(
                               "(dc p) c -> p dc c", p=128))
                r.wait("tensor", "dS", wl)
                for tg in range(4):
                    bk = tg % 2
                    h_, tv = tinc()
                    for dc in range(8):
                        if dc < 7:
                            E("tensor", lambda T, hp=hp, tg=tg, dc=dc, bk=bk: T.matmul(
                                ps[bk], wslab[hp % 2][:, 128 * dc:128 * (dc + 1)],
                                ln1T[dc][:, 512 * tg:512 * (tg + 1)],
                                start=(dc == 0), stop=False))
                        else:
                            E("tensor", lambda T, hp=hp, tg=tg, dc=dc, bk=bk, h_=h_:
                              T.matmul(ps[bk], wslab[hp % 2][:, 128 * dc:128 * (dc + 1)],
                                       ln1T[dc][:, 512 * tg:512 * (tg + 1)],
                                       start=False, stop=True).then_inc(h_, 1))
                    r.wait("scalar", "T", tv)
                    st = evst[tg % 2]
                    E("scalar", lambda S, st=st, bk=bk: S.copy(st[:], ps[bk]))
                    sc = r.inc("scalar", "S")
                    r.wait("tensor", "S", sc)
                    r.wait("sync", "S", sc)
                    r.dma("sync", "dS",
                          bass.AP(tensor=vt_dram.tensor,
                                  offset=(160 * hp) * S_LEN + 512 * tg,
                                  ap=[[S_LEN, 64], [1, 512]]),
                          st[0:64, :])
                    sv = r.dma("sync", "dS",
                               bass.AP(tensor=vt_dram.tensor,
                                       offset=(160 * hp + 80) * S_LEN + 512 * tg,
                                       ap=[[S_LEN, 64], [1, 512]]),
                               st[64:128, :])
                    r.wait("scalar", "dS", sv)
                r.wait("sync", "T", r.val("T"))
            r.barrier()

            for tt in range(NT):
                r.dma("sync", "dS", vtok[tt],
                      vt_dram[:, 128 * tt:128 * (tt + 1)], transpose=True)
            vtok_ld = r.val("dS")
            r.wait("tensor", "dS", vtok_ld)

            for hp in range(8):
                kv0 = 128 * hp
                wlq = r.dma("sync", "dS",
                            wslab[0].rearrange("p (dc c) -> p dc c", c=128),
                            wqkv_dram[0][:, kv0:kv0 + 128].rearrange(
                                "(dc p) c -> p dc c", p=128))
                wlk = r.dma("sync", "dS",
                            wslab[1].rearrange("p (dc c) -> p dc c", c=128),
                            wqkv_dram[1][:, kv0:kv0 + 128].rearrange(
                                "(dc p) c -> p dc c", p=128))
                r.wait("tensor", "dS", wlk)
                for wi, dstp in ((0, qTp), (1, kTp)):
                    for tg in range(4):
                        bk = tg % 2
                        h_, tv = tinc()
                        for dc in range(8):
                            if dc < 7:
                                E("tensor", lambda T, wi=wi, tg=tg, dc=dc, bk=bk: T.matmul(
                                    ps[bk], wslab[wi][:, 128 * dc:128 * (dc + 1)],
                                    ln1T[dc][:, 512 * tg:512 * (tg + 1)],
                                    start=(dc == 0), stop=False))
                            else:
                                E("tensor", lambda T, wi=wi, tg=tg, dc=dc, bk=bk, h_=h_:
                                  T.matmul(ps[bk], wslab[wi][:, 128 * dc:128 * (dc + 1)],
                                           ln1T[dc][:, 512 * tg:512 * (tg + 1)],
                                           start=False, stop=True).then_inc(h_, 1))
                        r.wait("scalar", "T", tv)
                        E("scalar", lambda S, dstp=dstp, tg=tg, bk=bk: S.copy(
                            dstp[:, 512 * tg:512 * (tg + 1)], ps[bk]))
                        sc = r.inc("scalar", "S")
                        r.wait("tensor", "S", sc)

                r.wait("sync", "T", r.val("T"))
                for side, mbuf in ((0, m_e), (1, m_o)):
                    h = 2 * hp + side
                    ld = r.dma("sync", "dS", mbuf,
                               bass.AP(tensor=g_dram.tensor, offset=h * GN,
                                       ap=[[1, 128], [1, MU]]))
                r.wait("tensor", "dS", ld)

                sc_hist = {}
                vv_hist = {}
                pv_hist = {}
                dminus_, dplus_ = _sat_range()
                def tclass(kt, qg):
                    dlo = 128 * kt - 512 * qg - 511
                    dhi = 128 * kt - 512 * qg + 127
                    if dhi <= dminus_:
                        return "lo"
                    if dlo >= dplus_:
                        return "hi"
                    return "mid"
                for qg in range(4):
                    q_sl = slice(512 * qg, 512 * (qg + 1))
                    for kt in range(NT + 2):
                        if kt < NT:
                            j = kt % 3
                            if (qg, kt - 3) in sc_hist:
                                r.wait("tensor", "S", sc_hist[(qg, kt - 3)])
                            off = min(max(U0 - (128 * kt - 512 * qg), 0), MU - 512)
                            E("tensor", lambda T, kt=kt, q_sl=q_sl, j=j: T.matmul(
                                ps[2 * j], kTp[0:64, 128 * kt:128 * (kt + 1)],
                                qTp[0:64, q_sl], start=True, stop=False,
                                tile_position=(0, 0)))
                            E("tensor", lambda T, kt=kt, q_sl=q_sl, j=j: T.matmul(
                                ps[2 * j + 1], kTp[64:128, 128 * kt:128 * (kt + 1)],
                                qTp[64:128, q_sl], start=True, stop=False,
                                tile_position=(64, 0)))
                            E("tensor", lambda T, j=j, off=off: T.matmul(
                                ps[2 * j], j_sb[:], m_e[:, off:off + 512],
                                start=False, stop=True))
                            h_, tv_s = tinc()
                            E("tensor", lambda T, j=j, off=off, h_=h_: T.matmul(
                                ps[2 * j + 1], j_sb[:], m_o[:, off:off + 512],
                                start=False, stop=True).then_inc(h_, 1))
                            ewg = ew[kt % 3]
                            r.wait("scalar", "T", tv_s)
                            if (qg, kt - 3) in pv_hist:
                                r.wait("scalar", "T", pv_hist[(qg, kt - 3)])
                            E("scalar", lambda S, ewg=ewg, j=j: S.activation(
                                out=ewg[:, 0:1024],
                                in_=PS[:, 1024 * j:1024 * (j + 1)],
                                func=AF.Exp, bias=neg40c[:], scale=1.0))
                            sc_hist[(qg, kt)] = r.inc("scalar", "S")
                        if kt >= 2:
                            pkt = kt - 2
                            ewp = ew[pkt % 3]
                            r.wait("tensor", "S", sc_hist[(qg, pkt)])
                            E("tensor", lambda T, hp=hp, pkt=pkt, ewp=ewp: T.matmul(
                                PS[0:65, 3072:3584],
                                vtok[pkt][:, 160 * hp:160 * hp + 65],
                                ewp[:, 0:512],
                                start=(pkt == 0), stop=(pkt == NT - 1)))
                            h_, pvv = tinc()
                            E("tensor", lambda T, hp=hp, pkt=pkt, ewp=ewp, h_=h_:
                              T.matmul(PS[0:65, 3584:4096],
                                       vtok[pkt][:, 160 * hp + 80:160 * hp + 145],
                                       ewp[:, 512:1024],
                                       start=(pkt == 0), stop=(pkt == NT - 1))
                              .then_inc(h_, 1))
                            pv_hist[(qg, pkt)] = pvv
                    r.wait("scalar", "T", pv_hist[(qg, NT - 1)])
                    r.wait("scalar", "dG", r.val("dG"))
                    E("scalar", lambda S, q_sl=q_sl: S.copy(oTu[0:64, q_sl],
                                                            PS[0:64, 3072:3584]))
                    E("scalar", lambda S, q_sl=q_sl: S.copy(oTu[64:128, q_sl],
                                                            PS[0:64, 3584:4096]))
                    E("scalar", lambda S: S.copy(denst[64:65, 0:512], PS[64:65, 3072:3584]))
                    E("scalar", lambda S: S.copy(denst[64:65, 512:1024], PS[64:65, 3584:4096]))
                    sc = r.inc("scalar", "S")
                    sc_hist[(qg, NT - 1)] = sc
                    sc_hist[(qg, NT - 2)] = sc
                    sc_hist[(qg, NT - 3)] = sc
                    r.wait("tensor", "S", sc)
                    r.wait("gpsimd", "S", sc)
                    if qg == 0:
                        r.wait("gpsimd", "V", r.val("V"))
                    s_ = r.sem["dG"]
                    s_[1] += 32
                    E("gpsimd", lambda G, qg=qg, h=s_[0]: (
                        G.dma_start(out=pairdenP[:, 8 * qg:8 * qg + 4],
                                    in_=denst[64:65, 0:512]).then_inc(h, 16),
                        G.dma_start(out=pairdenP[:, 8 * qg + 4:8 * qg + 8],
                                    in_=denst[64:65, 512:1024]).then_inc(h, 16)))
                r.wait("vector", "dG", r.val("dG"))
                r.wait("vector", "dS")
                E("vector", lambda V: V.drain())
                E("vector", lambda V: V.reciprocal(
                    pairrecP[:], pairdenP[:]))
                vv = r.inc("vector", "V")
                r.wait("sync", "V", vv)
                for qg in range(4):
                    r.dma("sync", "dS", pairrec[0:1, 512 * qg:512 * (qg + 1)],
                          pairrecP[:, 8 * qg:8 * qg + 4])
                    sv_sc = r.dma("sync", "dS",
                                  pairrec[1:2, 512 * qg:512 * (qg + 1)],
                                  pairrecP[:, 8 * qg + 4:8 * qg + 8])
                r.wait("tensor", "dS", sv_sc)
                for qg in range(4):
                    E("tensor", lambda T, qg=qg: T.matmul(
                        ps[6], sel2_sb[:], pairrec[:, 512 * qg:512 * (qg + 1)],
                        start=True, stop=True))
                    tv = r.inc("tensor", "T")
                    r.wait("scalar", "T", tv)
                    E("scalar", lambda S, qg=qg: S.copy(
                        recB[:, 512 * qg:512 * (qg + 1)], ps[6]))
                    sc = r.inc("scalar", "S")
                    r.wait("tensor", "S", sc)
                r.wait("vector", "S", r.val("S"))
                E("vector", lambda V: V.tensor_tensor(
                    out=oTp, in0=oTu, in1=recB[:], op=ALU.mult))
                vv = r.inc("vector", "V")
                r.wait("scalar", "V", vv)
                r.wait("sync", "V", vv)
                sv = r.dma("sync", "dS", ot_dram[128 * hp:128 * (hp + 1), :], oTp)
                r.wait("vector", "dS", sv)
            r.barrier()

            for kvc in range(8):
                r.dma("sync", "dS",
                      woslab[kvc].rearrange("p (dc c) -> p dc c", c=128),
                      wqkv_dram[3][128 * kvc:128 * (kvc + 1), :].rearrange(
                          "p (dc c) -> p dc c", c=128))
            for tg in range(4):
                for kvc in range(8):
                    r.dma("sync", "dS", oTld[kvc],
                          ot_dram[128 * kvc:128 * (kvc + 1), 512 * tg:512 * (tg + 1)])
                ld = r.val("dS")
                r.wait("tensor", "dS", ld)
                for dct in range(8):
                    bk = dct % 2
                    h_, tv = tinc()
                    for kvc in range(8):
                        if kvc < 7:
                            E("tensor", lambda T, dct=dct, kvc=kvc, bk=bk: T.matmul(
                                ps[bk], woslab[kvc][:, 128 * dct:128 * (dct + 1)],
                                oTld[kvc], start=(kvc == 0), stop=False))
                        else:
                            E("tensor", lambda T, dct=dct, kvc=kvc, bk=bk, h_=h_:
                              T.matmul(ps[bk], woslab[kvc][:, 128 * dct:128 * (dct + 1)],
                                       oTld[kvc], start=False, stop=True).then_inc(h_, 1))
                    r.wait("scalar", "T", tv)
                    st = evst[dct % 2]
                    E("scalar", lambda S, st=st, bk=bk: S.copy(st[:], ps[bk]))
                    sc = r.inc("scalar", "S")
                    r.wait("vector", "S", sc)
                    E("vector", lambda V, st=st, bk=bk: V.tensor_tensor(
                        out=qst_o[:, 0:512], in0=ps[bk], in1=st[:],
                        op=ALU.subtract))
                    vv = r.inc("vector", "V")
                    r.wait("tensor", "V", vv)
                    r.wait("sync", "V", vv)
                    r.dma("sync", "dS",
                          yat_dram[128 * dct:128 * (dct + 1),
                                   512 * tg:512 * (tg + 1)], st[:])
                    sv = r.dma("sync", "dS",
                               yat2_dram[128 * dct:128 * (dct + 1),
                                         512 * tg:512 * (tg + 1)], qst_o[:, 0:512])
                    r.wait("scalar", "dS", sv)
                    r.wait("vector", "dS", sv)
                r.wait("sync", "T", r.val("T"))
            r.barrier()

            for tt in range(NT):
                xt, ya = xtile[tt % 2], yatile[tt % 2]
                r.guard("sync", f"xtile{tt % 2}")
                r.guard("sync", f"yatile{tt % 2}")
                r.guard("sync", "lntbuf")
                r.dma("sync", "dS", xt[:], x_rows(tt))
                ld = r.dma("sync", "dS", ya[:],
                           yat_dram[:, 128 * tt:128 * (tt + 1)], transpose=True)
                ld = r.dma("sync", "dS", lnt[:],
                           yat2_dram[:, 128 * tt:128 * (tt + 1)], transpose=True)
                r.wait("vector", "dS", ld)
                E("vector", lambda V, xt=xt, ya=ya: V.tensor_tensor(
                    out=x1t[:], in0=xt[:], in1=ya[:], op=ALU.add))
                VD()
                E("vector", lambda V: V.tensor_tensor(
                    out=x1t[:], in0=x1t[:], in1=lnt[:], op=ALU.add))
                vv = r.inc("vector", "V")
                r.set_guard(f"xtile{tt % 2}", "V", vv)
                r.set_guard(f"yatile{tt % 2}", "V", vv)
                r.set_guard("lntbuf", "V", vv)
                r.wait("sync", "V", vv)
                sv = r.dma("sync", "dS", x1_dram[128 * tt:128 * (tt + 1), :], x1t[:])
                r.wait("vector", "dS", sv)
            r.barrier()

            for tt in range(NT):
                ln_tile(x1_dram[128 * tt:128 * (tt + 1), :], xq_dram, tt, True)
            r.barrier()

            for dc in range(8):
                for hg in range(8):
                    r.dma("sync", "dS", w1qT[dc][:, 512 * hg:512 * (hg + 1)],
                          w1q_dram[512 * hg:512 * (hg + 1), 128 * dc:128 * (dc + 1)],
                          transpose=True)
            for dc in range(8):
                for tg in range(4):
                    r.dma("sync", "dS", xqT[dc][:, 512 * tg:512 * (tg + 1)],
                          xq_dram[512 * tg:512 * (tg + 1), 128 * dc:128 * (dc + 1)],
                          transpose=True)
            mm1_ld = r.val("dS")
            r.wait("tensor", "dS", mm1_ld)
            for tt in range(NT):
                h_, tv = tinc()
                for dc in range(8):
                    for hg in range(8):
                        if dc == 7 and hg == 7:
                            E("tensor", lambda T, tt=tt, dc=dc, hg=hg, h_=h_: T.matmul(
                                ps[hg], xqT[dc][:, 128 * tt:128 * (tt + 1)],
                                w1qT[dc][:, 512 * hg:512 * (hg + 1)],
                                start=False, stop=True).then_inc(h_, 1))
                        else:
                            E("tensor", lambda T, tt=tt, dc=dc, hg=hg: T.matmul(
                                ps[hg], xqT[dc][:, 128 * tt:128 * (tt + 1)],
                                w1qT[dc][:, 512 * hg:512 * (hg + 1)],
                                start=(dc == 0), stop=(dc == 7)))
                r.wait("scalar", "T", tv)
                for hg in range(8):
                    E("scalar", lambda S, tt=tt, hg=hg: S.activation(
                        out=hbuf[:, 512 * hg:512 * (hg + 1)], in_=ps[hg],
                        func=AF.Gelu, bias=zeroc[:], scale=sxcols[:, tt:tt + 1]))
                sc = r.inc("scalar", "S")
                r.wait("tensor", "S", sc)
                r.wait("vector", "S", sc)
                r.guard("vector", "stc")
                E("vector", lambda V: V.reduce_max(
                    stc["mcol"][:], hbuf, axis=mybir.AxisListType.X,
                    apply_absolute_value=True))
                VD()
                E("vector", lambda V: V.scalar_tensor_tensor(
                    out=stc["mclip"][:], in0=stc["mcol"][:], scalar=EPS,
                    in1=onec[:], op0=ALU.max, op1=ALU.mult))
                VD()
                E("vector", lambda V: V.reciprocal(
                    stc["invm"][:], stc["mclip"][:]))
                VD()
                E("vector", lambda V: V.scalar_tensor_tensor(
                    out=stc["s127"][:], in0=stc["invm"][:], scalar=127.0,
                    in1=zeroc[:], op0=ALU.mult, op1=ALU.add))
                E("vector", lambda V: V.tensor_tensor(
                    out=stc["tmp2"][:], in0=stc["mclip"][:], in1=s2col[:], op=ALU.mult))
                VD()
                E("vector", lambda V, tt=tt: V.scalar_tensor_tensor(
                    out=sc2cols[:, tt:tt + 1], in0=stc["tmp2"][:], scalar=1.0 / 127.0,
                    in1=zeroc[:], op0=ALU.mult, op1=ALU.add))
                vv = r.inc("vector", "V")
                r.wait("scalar", "V", vv)
                for cc in range(4):
                    csl = slice(1024 * cc, 1024 * (cc + 1))
                    E("scalar", lambda S, csl=csl: S.activation(
                        out=qst_t[:], in_=hbuf[:, csl], func=AF.Identity,
                        bias=halfc[:], scale=stc["s127"][:]))
                    sc = r.inc("scalar", "S")
                    if csl.stop == HID:
                        r.set_guard("stc", "S", sc)
                    r.wait("vector", "S", sc)
                    floor_chain(qst_t[:], hqbuf[:, csl], 1)
                    vv = r.inc("vector", "V")
                    r.wait("scalar", "V", vv)
                r.wait("sync", "V", r.val("V"))
                sv = r.dma("sync", "dS", hq_dram[128 * tt:128 * (tt + 1), :], hqbuf)
                r.wait("vector", "dS", sv)
                r.wait("scalar", "dS", sv)
            r.barrier()

            for hc in range(32):
                for dg in range(2):
                    r.dma("sync", "dS", w2qT[hc][:, 512 * dg:512 * (dg + 1)],
                          w2q_dram[512 * dg:512 * (dg + 1), 128 * hc:128 * (hc + 1)],
                          transpose=True)
            w2_ld = r.val("dS")
            for tg in range(4):
                for hc in range(32):
                    r.dma("sync", "dS", hqT[hc],
                          hq_dram[512 * tg:512 * (tg + 1), 128 * hc:128 * (hc + 1)],
                          transpose=True)
                slab_ld = r.val("dS")
                r.wait("tensor", "dS", max(w2_ld, slab_ld))
                for ttl in range(4):
                    tt = 4 * tg + ttl
                    for dg in range(2):
                        bk = (2 * ttl + dg) % 2
                        h_, tv = tinc()
                        for hc in range(32):
                            if hc < 31:
                                E("tensor", lambda T, ttl=ttl, dg=dg, hc=hc, bk=bk:
                                  T.matmul(ps[bk],
                                           hqT[hc][:, 128 * ttl:128 * (ttl + 1)],
                                           w2qT[hc][:, 512 * dg:512 * (dg + 1)],
                                           start=(hc == 0), stop=False))
                            else:
                                E("tensor", lambda T, ttl=ttl, dg=dg, hc=hc, bk=bk, h_=h_:
                                  T.matmul(ps[bk],
                                           hqT[hc][:, 128 * ttl:128 * (ttl + 1)],
                                           w2qT[hc][:, 512 * dg:512 * (dg + 1)],
                                           start=False, stop=True).then_inc(h_, 1))
                        xt = xtile[(2 * ttl + dg) % 2]
                        r.guard("sync", f"xtile{(2 * ttl + dg) % 2}")
                        ld = r.dma("sync", "dS", xt[:, 0:512],
                                   x1_dram[128 * tt:128 * (tt + 1),
                                           512 * dg:512 * (dg + 1)])
                        r.wait("vector", "dS", ld)
                        r.wait("vector", "T", tv)
                        E("vector", lambda V, tt=tt, bk=bk, xt=xt:
                          V.scalar_tensor_tensor(
                              out=qst_t[:, 0:512], in0=ps[bk],
                              scalar=sc2cols[:, tt:tt + 1], in1=xt[:, 0:512],
                              op0=ALU.mult, op1=ALU.add))
                        vv = r.inc("vector", "V")
                        r.set_guard(f"xtile{(2 * ttl + dg) % 2}", "V", vv)
                        r.wait("tensor", "V", vv)
                        r.wait("sync", "V", vv)
                        sv = r.dma("sync", "dS",
                                   y_out[b * S_LEN + 128 * tt:
                                         b * S_LEN + 128 * (tt + 1),
                                         512 * dg:512 * (dg + 1)], qst_t[:, 0:512])
                        r.wait("vector", "dS", sv)
                r.wait("sync", "T", r.val("T"))
            r.barrier()

        for b in range(B_CORE):
            batch_program(b)

        r.wait("sync", "dS")
        r.wait("sync", "dG")

        with nc.Block() as block:
            @block.sync
            def _(eng):
                for fn in r.ops["sync"]:
                    fn(eng)

            @block.vector
            def _(eng):
                for fn in r.ops["vector"]:
                    fn(eng)

            @block.scalar
            def _(eng):
                for fn in r.ops["scalar"]:
                    fn(eng)

            @block.tensor
            def _(eng):
                for fn in r.ops["tensor"]:
                    fn(eng)

            @block.gpsimd
            def _(eng):
                for fn in r.ops["gpsimd"]:
                    fn(eng)

    return nc


_NC_CACHE = None


def kernel(L0_attn_inputs, L0_attn_Wq, L0_attn_Wk, L0_attn_Wv, L0_attn_Wo,
           L0_attn_rel_emb, L0_ffn_w1, L0_ffn_w2):
    global _NC_CACHE
    x = np.ascontiguousarray(np.asarray(L0_attn_inputs, dtype=np.float32))
    consts = host_constants()
    if _NC_CACHE is None:
        _NC_CACHE = build_nc()
    nc = _NC_CACHE
    shared = {
        "wq_in": np.ascontiguousarray(L0_attn_Wq, dtype=np.float32),
        "wk_in": np.ascontiguousarray(L0_attn_Wk, dtype=np.float32),
        "wv_in": np.ascontiguousarray(L0_attn_Wv, dtype=np.float32),
        "wo_in": np.ascontiguousarray(L0_attn_Wo, dtype=np.float32),
        "rel_in": np.ascontiguousarray(L0_attn_rel_emb, dtype=np.float32),
        "w1_in": np.ascontiguousarray(L0_ffn_w1, dtype=np.float32),
        "w2_in": np.ascontiguousarray(L0_ffn_w2, dtype=np.float32),
        **consts,
    }
    in_maps = []
    for c in range(N_CORES):
        xi = x[B_CORE * c: B_CORE * (c + 1)].reshape(TOK, D)
        in_maps.append({"x_in": np.ascontiguousarray(xi), **shared})
    res = run_bass_kernel_spmd(nc, in_maps, core_ids=list(range(N_CORES)))
    out = np.concatenate(
        [np.asarray(r["y_out"]).reshape(B_CORE, S_LEN, D) for r in res.results],
        axis=0)
    return out.astype(np.float32)


# version 22
# speedup vs baseline: 1.5699x; 1.0188x over previous
import numpy as np
import ml_dtypes
import concourse.bass as bass
import concourse.mybir as mybir
from concourse.bass_utils import run_bass_kernel_spmd

F32 = mybir.dt.float32
BF16 = mybir.dt.bfloat16
I32 = mybir.dt.int32
AF = mybir.ActivationFunctionType
ALU = mybir.AluOpType

B_TOTAL, S_LEN, D = 16, 2048, 1024
N_HEADS, D_KV = 16, 64
HID = 4096
N_CORES = 8
B_CORE = B_TOTAL // N_CORES
TOK = B_CORE * S_LEN
NT = S_LEN // 128
EPS = 1e-5
SHIFT = 40.0
NUM_BUCKETS, MAX_DIST = 32, 128
U0 = 1920
MU = 3968
GN = 4096

ENGINES = ("sync", "vector", "scalar", "tensor", "gpsimd")


def _bucket_table():
    d = np.arange(-2048, 2048)
    n = -d
    ret = np.where(n < 0, 16, 0).astype(np.int64)
    n = np.abs(n)
    is_small = n < 8
    nf = np.maximum(n, 1).astype(np.float32)
    t = np.log(nf / np.float32(8.0)).astype(np.float32)
    t = (t / np.float32(np.log(MAX_DIST / 8.0))).astype(np.float32)
    t = (t * np.float32(8.0)).astype(np.float32)
    val = 8 + t.astype(np.int32)
    val = np.minimum(val, 15)
    return (ret + np.where(is_small, n, val)).astype(np.int64)


def _sat_range():
    bt = _bucket_table()
    lo_val = bt[0]
    d = -2048
    while bt[d + 1 + 2048] == lo_val:
        d += 1
    dminus = d
    hi_val = bt[2047 + 2048]
    d = 2047
    while bt[d - 1 + 2048] == hi_val:
        d -= 1
    dplus = d
    return dminus, dplus


def host_constants():
    bt = _bucket_table()
    gsel = np.zeros((NUM_BUCKETS, GN), dtype=np.float32)
    i = np.arange(GN)
    dd = 2047 - i
    gsel[bt[dd + 2048], i] = 1.0
    jmat = np.zeros((128, 128), dtype=ml_dtypes.bfloat16)
    jmat[np.arange(128), 127 - np.arange(128)] = 1.0
    sel2 = np.zeros((2, 128), dtype=np.float32)
    sel2[0, 0:64] = 1.0
    sel2[1, 64:128] = 1.0
    eblo = np.zeros((NUM_BUCKETS, 1), dtype=np.float32)
    eblo[bt[0], 0] = 1.0
    return {"gsel_c": gsel, "jmat_c": jmat, "sel2_c": sel2, "eblo_c": eblo}


class Rec:
    def __init__(self):
        self.ops = {e: [] for e in ENGINES}
        self.sem = {}

    def add_sem(self, name, handle):
        self.sem[name] = [handle, 0]

    def emit(self, eng, fn):
        self.ops[eng].append(fn)

    def inc(self, eng, sname):
        s = self.sem[sname]
        s[1] += 1
        h = s[0]
        if eng in ("vector", "scalar", "tensor"):
            self.emit(eng, lambda E, h=h: (E.drain(), E.nop().then_inc(h, 1)))
        else:
            self.emit(eng, lambda E, h=h: E.nop().then_inc(h, 1))
        return s[1]

    def dma(self, eng, sname, out, in_, transpose=False):
        s = self.sem[sname]
        s[1] += 16
        h = s[0]
        if transpose:
            self.emit(eng, lambda E, h=h, out=out, in_=in_:
                      E.dma_start_transpose(out, in_).then_inc(h, 16))
        else:
            self.emit(eng, lambda E, h=h, out=out, in_=in_:
                      E.dma_start(out=out, in_=in_).then_inc(h, 16))
        return s[1]

    def wait(self, eng, sname, val=None):
        s = self.sem[sname]
        v = s[1] if val is None else val
        if sname in ("dS", "dG"):
            v = s[1]
            issuer = "sync" if sname == "dS" else "gpsimd"
            if not hasattr(self, "_sw"):
                self._sw = {}
            if self._sw.get(sname, -1) < v:
                self._sw[sname] = v
                if eng != issuer:
                    self.emit(issuer, lambda E, h=s[0], v=v: E.wait_ge(h, v))
        self.emit(eng, lambda E, h=s[0], v=v: E.wait_ge(h, v))

    def val(self, sname):
        return self.sem[sname][1]

    def guard(self, eng, key):
        if key in getattr(self, "bguard", {}):
            sname, v = self.bguard[key]
            self.wait(eng, sname, v)

    def set_guard(self, key, sname, v):
        if not hasattr(self, "bguard"):
            self.bguard = {}
        self.bguard[key] = (sname, v)

    def barrier(self):
        self.wait("sync", "dS")
        self.wait("gpsimd", "dG")
        for e in ENGINES:
            self.inc(e, "ph_" + e)
        for e in ENGINES:
            for o in ENGINES:
                if o != e:
                    self.wait(e, "ph_" + o)


def build_nc(debug=False):
    nc = bass.Bass("TRN2", target_bir_lowering=False, debug=False)
    r = Rec()

    x_in = nc.dram_tensor("x_in", [TOK, D], F32, kind="ExternalInput").ap()
    wq_in = nc.dram_tensor("wq_in", [D, D], F32, kind="ExternalInput").ap()
    wk_in = nc.dram_tensor("wk_in", [D, D], F32, kind="ExternalInput").ap()
    wv_in = nc.dram_tensor("wv_in", [D, D], F32, kind="ExternalInput").ap()
    wo_in = nc.dram_tensor("wo_in", [D, D], F32, kind="ExternalInput").ap()
    rel_in = nc.dram_tensor("rel_in", [NUM_BUCKETS, N_HEADS], F32, kind="ExternalInput").ap()
    w1_in = nc.dram_tensor("w1_in", [HID, D], F32, kind="ExternalInput").ap()
    w2_in = nc.dram_tensor("w2_in", [D, HID], F32, kind="ExternalInput").ap()
    gsel_c = nc.dram_tensor("gsel_c", [NUM_BUCKETS, GN], F32, kind="ExternalInput").ap()
    jmat_c = nc.dram_tensor("jmat_c", [128, 128], BF16, kind="ExternalInput").ap()
    sel2_c = nc.dram_tensor("sel2_c", [2, 128], F32, kind="ExternalInput").ap()
    eblo_c = nc.dram_tensor("eblo_c", [NUM_BUCKETS, 1], F32, kind="ExternalInput").ap()
    y_out = nc.dram_tensor("y_out", [TOK, D], F32, kind="ExternalOutput").ap()

    KD = "ExternalOutput" if debug else "Internal"
    g_dram = nc.dram_tensor("g_dram", [N_HEADS, GN], BF16, kind=KD).ap()
    wqkv_dram = [nc.dram_tensor(f"w{n}_bf", [D, D], BF16, kind=KD).ap()
                 for n in ("q", "k", "v", "o")]
    w1q_dram = nc.dram_tensor("w1q_dram", [HID, D], BF16, kind=KD).ap()
    w2q_dram = nc.dram_tensor("w2q_dram", [D, HID], BF16, kind=KD).ap()
    ln1_dram = nc.dram_tensor("ln1_dram", [S_LEN, D], BF16, kind=KD).ap()
    vt_dram = nc.dram_tensor("vt_dram", [N_HEADS * 80, S_LEN], BF16, kind=KD).ap()
    ot_dram = nc.dram_tensor("ot_dram", [D, S_LEN], BF16, kind=KD).ap()
    yat_dram = nc.dram_tensor("yat_dram", [D, S_LEN], BF16, kind=KD).ap()
    yat2_dram = nc.dram_tensor("yat2_dram", [D, S_LEN], BF16, kind=KD).ap()
    x1_dram = nc.dram_tensor("x1_dram", [S_LEN, D], F32, kind=KD).ap()
    xq_dram = nc.dram_tensor("xq_dram", [S_LEN, D], BF16, kind=KD).ap()
    hq_dram = nc.dram_tensor("hq_dram", [S_LEN, HID], BF16, kind=KD).ap()

    import contextlib
    es = contextlib.ExitStack()
    with es:
        def sb(name, shape, dt):
            return es.enter_context(nc.sbuf_tensor(name, shape, dt))

        neg40c = sb("neg40c", [128, 1], F32)
        halfc = sb("halfc", [128, 1], F32)
        zeroc = sb("zeroc", [128, 1], F32)
        epscl = sb("epscl", [128, 1], F32)
        onec = sb("onec", [128, 1], F32)
        onesrow = sb("onesrow", [1, 128], F32)
        j_sb = sb("j_sb", [128, 128], BF16)
        sel2_sb = sb("sel2_sb", [2, 128], F32)
        rel_sb = sb("rel_sb", [NUM_BUCKETS, N_HEADS], F32)
        s1col = sb("s1col", [128, 1], F32)
        s2col = sb("s2col", [128, 1], F32)
        rscol = sb("rscol", [128, 1], F32)
        sxcols = sb("sxcols", [128, NT], F32)
        sc2cols = sb("sc2cols", [128, NT], F32)
        relrow = sb("relrow", [1, 2 * N_HEADS], F32)
        bcol_sb = sb("bcol_sb", [128, 2 * N_HEADS], F32)
        nrel15 = sb("nrel15", [128, 1], F32)
        eblo_sb = sb("eblo_sb", [NUM_BUCKETS, 1], F32)
        t11 = sb("t11", [1, 8], F32)
        t11b = sb("t11b", [1, 8], F32)
        t11c = sb("t11c", [1, 8], F32)
        stc = {n: sb("stc_" + n, [128, 1], F32) for n in
               ("s", "q", "mneg", "vq", "m2", "var", "veps", "rec", "r", "mr",
                "mcol", "mclip", "invm", "s127", "tmp", "tmp2")}
        qst_t = sb("qst_t", [128, 1024], F32)
        qst_c = sb("qst_c", [128, 1024], I32)
        qst_b = sb("qst_b", [128, 1024], F32)
        qst_o = sb("qst_o", [128, 1024], BF16)
        qst_u = sb("qst_u", [128, 1024], F32)
        xtile = [sb(f"xtile{i}", [128, D], F32) for i in range(2)]
        yatile = [sb(f"yatile{i}", [128, D], BF16) for i in range(2)]
        x1t = sb("x1t", [128, D], F32)
        lnt = sb("lnt", [128, D], BF16)
        lntf = sb("lntf", [128, D], F32)
        evst = [sb(f"evst{i}", [128, 512], BF16) for i in range(2)]
        wsum = sb("wsum", [128, 32], F32)
        wsumt = sb("wsumt", [128, 1], F32)
        AR_N = 61440
        AR = sb("AR", [128, AR_N], BF16)
        ARF_N = 9216
        ARF = sb("ARF", [128, ARF_N], F32)

        def carve(off, p, n, dt=BF16):
            if dt == BF16:
                return AR[0:p, off:off + n]
            return ARF[0:p, off:off + n]

        ln1T = [carve(2048 * i, 128, 2048) for i in range(8)]
        vtok = [carve(16384 + 1280 * i, 128, 1280) for i in range(16)]
        m_e = carve(36864, 128, MU)
        m_o = carve(40832, 128, MU)
        qTp = carve(44800, 128, 2048)
        kTp = carve(46848, 128, 2048)
        oTp = carve(48896, 128, 2048)
        ebuf = [carve(50944 + 512 * i, 128, 512) for i in range(4)]
        wslab = [carve(52992 + 1024 * i, 128, 1024) for i in range(2)]
        mpp = carve(50944, 128, MU)
        ew = [carve(50944, 128, 1024), carve(52992, 128, 1024),
              carve(56576, 128, 1024)]
        gstage = carve(55040, N_HEADS, 512)
        ones_big = carve(55552, 128, 1024)
        onesr2k = carve(56576, 16, 2048)
        w1qT = [carve(4096 * i, 128, 4096) for i in range(8)]
        xqT = [carve(36864 + 2048 * i, 128, 2048) for i in range(8)]
        hbuf = carve(53248, 128, 4096)
        hqbuf = carve(57344, 128, 4096)
        w2qT = [carve(1024 * i, 128, 1024) for i in range(32)]
        hqT = [carve(36864 + 512 * i, 128, 512) for i in range(32)]
        woslab = [carve(36864 + 1024 * i, 128, 1024) for i in range(8)]
        oTld = [carve(45056 + 512 * i, 128, 512) for i in range(8)]
        oTu = carve(0, 128, 2048, F32)
        recB = carve(2048, 128, 2048, F32)
        denst = carve(4096, 128, 1024, F32)
        pairdenP = carve(5120, 128, 32, F32)
        pairrecP = carve(5184, 128, 32, F32)
        pairrec = carve(7168, 2, 2048, F32)
        gselch = carve(0, NUM_BUCKETS, 512, F32)
        wchunk = carve(512, 128, 2048, F32)
        wchunk2 = carve(4096, 128, 2048, F32)
        qst_o2 = carve(50944, 128, 1024)

        PS = es.enter_context(nc.psum_tensor("PS", [128, 4096], F32))
        ps = [PS[:, 512 * i:512 * (i + 1)] for i in range(8)]

        for sname in ("dS", "dG", "V", "S", "T") + tuple("ph_" + e for e in ENGINES):
            r.add_sem(sname, es.enter_context(nc.semaphore(sname)))

        E = r.emit

        def VD():
            E("vector", lambda V: V.drain())

        def tinc():
            s = r.sem["T"]
            s[1] += 1
            return s[0], s[1]


        def floor_chain(t_ap, out_ap, K, post=None):
            E("vector", lambda V, t_ap=t_ap: V.scalar_tensor_tensor(
                out=qst_c[:], in0=t_ap, scalar=0.0, in1=t_ap,
                op0=ALU.add, op1=ALU.max))
            VD()
            for j in range(1, K + 1):
                E("vector", lambda V, t_ap=t_ap, j=j: V.scalar_tensor_tensor(
                    out=qst_b[:], in0=qst_c[:], scalar=float(-j + 1),
                    in1=t_ap, op0=ALU.add, op1=ALU.is_gt))
                if j == 1:
                    E("vector", lambda V, t_ap=t_ap, j=j: V.scalar_tensor_tensor(
                        out=qst_u[:], in0=qst_c[:], scalar=float(j),
                        in1=t_ap, op0=ALU.add, op1=ALU.is_le))
                    VD()
                    E("vector", lambda V: V.tensor_tensor(
                        out=qst_u[:], in0=qst_u[:], in1=qst_b[:], op=ALU.subtract))
                else:
                    VD()
                    E("vector", lambda V: V.tensor_tensor(
                        out=qst_u[:], in0=qst_u[:], in1=qst_b[:], op=ALU.subtract))
                    VD()
                    E("vector", lambda V, t_ap=t_ap, j=j: V.scalar_tensor_tensor(
                        out=qst_b[:], in0=qst_c[:], scalar=float(j),
                        in1=t_ap, op0=ALU.add, op1=ALU.is_le))
                    VD()
                    E("vector", lambda V: V.tensor_tensor(
                        out=qst_u[:], in0=qst_u[:], in1=qst_b[:], op=ALU.add))
                VD()
            E("vector", lambda V, out_ap=out_ap: V.tensor_tensor(
                out=out_ap, in0=qst_c[:], in1=qst_u[:], op=ALU.add))

        E("vector", lambda V: V.memset(neg40c[:], -SHIFT))
        E("vector", lambda V: V.memset(halfc[:], 0.5))
        E("vector", lambda V: V.memset(zeroc[:], 0.0))
        E("vector", lambda V: V.memset(epscl[:], EPS))
        E("vector", lambda V: V.memset(onec[:], 1.0))
        E("vector", lambda V: V.memset(onesrow[:], 1.0))
        E("vector", lambda V: V.memset(ones_big, 1.0))
        E("vector", lambda V: V.memset(onesr2k, 1.0))
        v0 = r.inc("vector", "V")
        r.dma("sync", "dS", j_sb[:], jmat_c)
        r.dma("sync", "dS", sel2_sb[:], sel2_c)
        ld0 = r.dma("sync", "dS", rel_sb[:], rel_in)
        r.wait("sync", "V", v0)
        for h in range(N_HEADS):
            r.dma("sync", "dS", vt_dram[80 * h + 64:80 * h + 80, :], onesr2k)

        for c in range(GN // 512):
            gl = r.dma("sync", "dS", gselch, gsel_c[:, 512 * c:512 * (c + 1)])
            r.wait("tensor", "dS", gl)
            if c == 0:
                r.wait("tensor", "dS", ld0)
            E("tensor", lambda T: T.matmul(PS[0:N_HEADS, 0:512], rel_sb[:], gselch,
                                           start=True, stop=True))
            tv = r.inc("tensor", "T")
            r.wait("scalar", "T", tv)
            E("scalar", lambda S: S.activation(out=gstage, in_=PS[0:N_HEADS, 0:512],
                                               func=AF.Identity, bias=zeroc[0:N_HEADS],
                                               scale=1.0))
            sc = r.inc("scalar", "S")
            r.wait("tensor", "S", sc)
            r.wait("sync", "S", sc)
            st_ = r.dma("sync", "dS", g_dram[:, 512 * c:512 * (c + 1)], gstage)
            r.wait("scalar", "dS", st_)
            r.wait("sync", "dS", st_)

        for w_src, w_dst in zip((wq_in, wk_in, wv_in, wo_in), wqkv_dram):
            for dc in range(8):
                sl = slice(128 * dc, 128 * (dc + 1))
                st = wslab[dc % 2]
                s_ = r.sem["dG"]
                s_[1] += 16
                E("gpsimd", lambda G, w_src=w_src, sl=sl, st=st, h=s_[0]:
                  G.dma_start(out=st, in_=w_src[sl, :]).then_inc(h, 16))
                r.wait("sync", "dG", s_[1])
                sv = r.dma("sync", "dS", w_dst[sl, :], st)
                r.wait("gpsimd", "dS", sv)

        wbufs = (wchunk, wchunk2)
        wbg = {0: None, 1: None}
        obufs = (qst_o, qst_o2)
        obg = {0: None, 1: None}
        wq_state = {"g": 0, "o": 0}

        def wquant(w_src, w_dst, nrows, ncols, scol):
            CW = 2048 if ncols % 2048 == 0 else 1024
            nch = nrows // 128
            ncc = ncols // CW
            E("vector", lambda V: V.memset(wsum[:], 0.0))
            r.inc("vector", "V")
            for ch in range(nch):
                for cc in range(ncc):
                    bi = wq_state["g"] % 2
                    wq_state["g"] += 1
                    buf = wbufs[bi]
                    if wbg[bi]:
                        for sn_, v_ in wbg[bi]:
                            r.wait("sync", sn_, v_)
                    ld = r.dma("sync", "dS", buf[:, 0:CW],
                               w_src[128 * ch:128 * (ch + 1), CW * cc:CW * (cc + 1)])
                    r.wait("vector", "dS", ld)
                    E("vector", lambda V, CW=CW, buf=buf: V.reduce_sum(
                        stc["tmp"][:], buf[:, 0:CW], axis=mybir.AxisListType.X,
                        apply_absolute_value=True))
                    VD()
                    if cc == 0:
                        E("vector", lambda V, ch=ch: V.tensor_tensor(
                            out=wsum[:, ch:ch + 1], in0=stc["tmp"][:], in1=zeroc[:],
                            op=ALU.add))
                    else:
                        E("vector", lambda V, ch=ch: V.tensor_tensor(
                            out=wsum[:, ch:ch + 1], in0=wsum[:, ch:ch + 1],
                            in1=stc["tmp"][:], op=ALU.add))
                    vv = r.inc("vector", "V")
                    wbg[bi] = [("V", vv)]
            E("vector", lambda V: V.drain())
            E("vector", lambda V, n=nch: V.reduce_sum(
                wsumt[:], wsum[:, 0:n], axis=mybir.AxisListType.X))
            vv = r.inc("vector", "V")
            r.wait("tensor", "V", vv)
            E("tensor", lambda T: T.matmul(PS[0:1, 0:1], wsumt[:], onec[:],
                                           start=True, stop=True))
            tv = r.inc("tensor", "T")
            r.wait("scalar", "T", tv)
            E("scalar", lambda S, n=nrows * ncols: S.activation(
                out=t11[0:1, 0:1], in_=PS[0:1, 0:1], func=AF.Identity,
                bias=zeroc[0:1], scale=1.0 / n))
            sc = r.inc("scalar", "S")
            r.wait("vector", "S", sc)
            E("vector", lambda V: V.scalar_tensor_tensor(
                out=t11b[0:1, 0:1], in0=t11[0:1, 0:1], scalar=EPS,
                in1=onec[0:1], op0=ALU.max, op1=ALU.mult))
            VD()
            E("vector", lambda V: V.reciprocal(t11[0:1, 0:1], t11b[0:1, 0:1]))
            vv = r.inc("vector", "V")
            r.wait("tensor", "V", vv)
            E("tensor", lambda T: T.matmul(PS[:, 512:513], onesrow[:], t11[0:1, 0:1],
                                           start=True, stop=True))
            tv = r.inc("tensor", "T")
            r.wait("scalar", "T", tv)
            E("scalar", lambda S: S.copy(rscol[:], PS[:, 512:513]))
            sc = r.inc("scalar", "S")
            r.wait("tensor", "S", sc)
            E("tensor", lambda T: T.matmul(PS[:, 512:513], onesrow[:], t11b[0:1, 0:1],
                                           start=True, stop=True))
            tv = r.inc("tensor", "T")
            r.wait("scalar", "T", tv)
            E("scalar", lambda S, scol=scol: S.copy(scol[:], PS[:, 512:513]))
            sc = r.inc("scalar", "S")
            r.wait("vector", "S", sc)
            for ch in range(nch):
                for cc in range(ncc):
                    bi = wq_state["g"] % 2
                    wq_state["g"] += 1
                    buf = wbufs[bi]
                    if wbg[bi]:
                        for sn_, v_ in wbg[bi]:
                            r.wait("sync", sn_, v_)
                    ld = r.dma("sync", "dS", buf[:, 0:CW],
                               w_src[128 * ch:128 * (ch + 1), CW * cc:CW * (cc + 1)])
                    r.wait("scalar", "dS", ld)
                    E("scalar", lambda S, CW=CW, buf=buf: S.activation(
                        out=buf[:, 0:CW], in_=buf[:, 0:CW], func=AF.Identity,
                        bias=zeroc[:], scale=rscol[:]))
                    sc = r.inc("scalar", "S")
                    r.wait("vector", "S", sc)
                    for sub in range(CW // 1024):
                        ssl = slice(1024 * sub, 1024 * (sub + 1))
                        oi = wq_state["o"] % 2
                        wq_state["o"] += 1
                        ob = obufs[oi]
                        if obg[oi] is not None:
                            r.wait("vector", "dS", obg[oi])
                        E("vector", lambda V, buf=buf, ssl=ssl: V.scalar_tensor_tensor(
                            out=qst_b[:], in0=buf[:, ssl], scalar=0.5,
                            in1=ones_big, op0=ALU.is_ge, op1=ALU.mult))
                        E("vector", lambda V, buf=buf, ssl=ssl: V.scalar_tensor_tensor(
                            out=qst_u[:], in0=buf[:, ssl], scalar=-0.5,
                            in1=ones_big, op0=ALU.is_lt, op1=ALU.mult))
                        VD()
                        E("vector", lambda V, ob=ob: V.tensor_tensor(
                            out=ob[:], in0=qst_b[:], in1=qst_u[:], op=ALU.subtract))
                        vv = r.inc("vector", "V")
                        r.wait("sync", "V", vv)
                        sv = r.dma("sync", "dS",
                                   w_dst[128 * ch:128 * (ch + 1),
                                         CW * cc + 1024 * sub:CW * cc + 1024 * (sub + 1)],
                                   ob[:])
                        obg[oi] = sv
                    wbg[bi] = [("S", sc), ("V", vv)]

        wquant(w1_in, w1q_dram, HID, D, s1col)
        wquant(w2_in, w2q_dram, D, HID, s2col)
        r.barrier()

        def ln_tile(x_src_ap, dst_dram, tt, quant):
            xt = xtile[tt % 2]
            r.guard("sync", f"xtile{tt % 2}")
            ld = r.dma("sync", "dS", xt[:], x_src_ap)
            r.wait("vector", "dS", ld)
            r.guard("vector", "stc")
            E("vector", lambda V, xt=xt: V.reduce_sum(
                stc["s"][:], xt[:], axis=mybir.AxisListType.X))
            E("vector", lambda V, xt=xt: V.tensor_tensor(
                out=qst_b[:], in0=xt[:], in1=xt[:], op=ALU.mult))
            E("vector", lambda V: V.drain())
            E("vector", lambda V: V.reduce_sum(
                stc["q"][:], qst_b[:], axis=mybir.AxisListType.X))
            VD()
            E("vector", lambda V: V.scalar_tensor_tensor(
                out=stc["mneg"][:], in0=stc["s"][:], scalar=-1.0 / D,
                in1=zeroc[:], op0=ALU.mult, op1=ALU.add))
            E("vector", lambda V: V.scalar_tensor_tensor(
                out=stc["vq"][:], in0=stc["q"][:], scalar=1.0 / D,
                in1=zeroc[:], op0=ALU.mult, op1=ALU.add))
            VD()
            E("vector", lambda V: V.tensor_tensor(
                out=stc["m2"][:], in0=stc["mneg"][:], in1=stc["mneg"][:], op=ALU.mult))
            VD()
            E("vector", lambda V: V.tensor_tensor(
                out=stc["var"][:], in0=stc["vq"][:], in1=stc["m2"][:], op=ALU.subtract))
            VD()
            E("vector", lambda V: V.tensor_tensor(
                out=stc["veps"][:], in0=stc["var"][:], in1=epscl[:], op=ALU.add))
            E("vector", lambda V: V.drain())
            E("vector", lambda V: V.reciprocal(
                stc["rec"][:], stc["veps"][:]))
            vv = r.inc("vector", "V")
            r.wait("scalar", "V", vv)
            E("scalar", lambda S: S.activation(
                out=stc["r"][:], in_=stc["rec"][:], func=AF.Sqrt,
                bias=zeroc[:], scale=1.0))
            sc = r.inc("scalar", "S")
            r.wait("vector", "S", sc)
            VD()
            E("vector", lambda V: V.tensor_tensor(
                out=stc["mr"][:], in0=stc["mneg"][:], in1=stc["r"][:], op=ALU.mult))
            vv = r.inc("vector", "V")
            r.wait("scalar", "V", vv)
            if not quant:
                E("scalar", lambda S, xt=xt: S.activation(
                    out=lnt[:], in_=xt[:], func=AF.Identity,
                    bias=stc["mr"][:], scale=stc["r"][:]))
                sc = r.inc("scalar", "S")
                r.set_guard(f"xtile{tt % 2}", "S", sc)
                r.set_guard("stc", "S", sc)
                r.wait("sync", "S", sc)
                sv = r.dma("sync", "dS", dst_dram[128 * tt:128 * (tt + 1), :], lnt[:])
                r.wait("scalar", "dS", sv)
                return
            E("scalar", lambda S, xt=xt: S.activation(
                out=lntf[:], in_=xt[:], func=AF.Identity,
                bias=stc["mr"][:], scale=stc["r"][:]))
            sc = r.inc("scalar", "S")
            r.set_guard(f"xtile{tt % 2}", "S", sc)
            r.wait("vector", "S", sc)
            E("vector", lambda V: V.reduce_max(
                stc["mcol"][:], lntf[:], axis=mybir.AxisListType.X,
                apply_absolute_value=True))
            VD()
            E("vector", lambda V: V.scalar_tensor_tensor(
                out=stc["mclip"][:], in0=stc["mcol"][:], scalar=EPS,
                in1=onec[:], op0=ALU.max, op1=ALU.mult))
            VD()
            E("vector", lambda V: V.reciprocal(
                stc["invm"][:], stc["mclip"][:]))
            VD()
            E("vector", lambda V: V.scalar_tensor_tensor(
                out=stc["s127"][:], in0=stc["invm"][:], scalar=127.0,
                in1=zeroc[:], op0=ALU.mult, op1=ALU.add))
            E("vector", lambda V: V.tensor_tensor(
                out=stc["tmp2"][:], in0=stc["mclip"][:], in1=s1col[:], op=ALU.mult))
            VD()
            E("vector", lambda V, tt=tt: V.scalar_tensor_tensor(
                out=sxcols[:, tt:tt + 1], in0=stc["tmp2"][:], scalar=1.0 / 127.0,
                in1=zeroc[:], op0=ALU.mult, op1=ALU.add))
            vv = r.inc("vector", "V")
            r.wait("scalar", "V", vv)
            E("scalar", lambda S: S.activation(
                out=qst_t[:], in_=lntf[:], func=AF.Identity,
                bias=halfc[:], scale=stc["s127"][:]))
            sc = r.inc("scalar", "S")
            r.set_guard("stc", "S", sc)
            r.wait("vector", "S", sc)
            floor_chain(qst_t[:], qst_o[:], 2)
            vv = r.inc("vector", "V")
            r.wait("sync", "V", vv)
            sv = r.dma("sync", "dS", dst_dram[128 * tt:128 * (tt + 1), :], qst_o[:])
            r.wait("vector", "dS", sv)
            r.wait("scalar", "dS", sv)

        def batch_program(b):
            def x_rows(tt):
                return x_in[b * S_LEN + 128 * tt: b * S_LEN + 128 * (tt + 1), :]

            for tt in range(NT):
                ln_tile(x_rows(tt), ln1_dram, tt, False)
            r.barrier()

            for dc in range(8):
                for tg in range(4):
                    r.dma("sync", "dS", ln1T[dc][:, 512 * tg:512 * (tg + 1)],
                          ln1_dram[512 * tg:512 * (tg + 1), 128 * dc:128 * (dc + 1)],
                          transpose=True)
            r.wait("sync", "dS")
            ln_ld = r.val("dS")

            r.wait("tensor", "dS", ln_ld)
            for hp in range(8):
                kv0 = 128 * hp
                wl = r.dma("sync", "dS",
                           wslab[hp % 2].rearrange("p (dc c) -> p dc c", c=128),
                           wqkv_dram[2][:, kv0:kv0 + 128].rearrange(
                               "(dc p) c -> p dc c", p=128))
                r.wait("tensor", "dS", wl)
                for tg in range(4):
                    bk = tg % 2
                    h_, tv = tinc()
                    for dc in range(8):
                        if dc < 7:
                            E("tensor", lambda T, hp=hp, tg=tg, dc=dc, bk=bk: T.matmul(
                                ps[bk], wslab[hp % 2][:, 128 * dc:128 * (dc + 1)],
                                ln1T[dc][:, 512 * tg:512 * (tg + 1)],
                                start=(dc == 0), stop=False))
                        else:
                            E("tensor", lambda T, hp=hp, tg=tg, dc=dc, bk=bk, h_=h_:
                              T.matmul(ps[bk], wslab[hp % 2][:, 128 * dc:128 * (dc + 1)],
                                       ln1T[dc][:, 512 * tg:512 * (tg + 1)],
                                       start=False, stop=True).then_inc(h_, 1))
                    r.wait("scalar", "T", tv)
                    st = evst[tg % 2]
                    E("scalar", lambda S, st=st, bk=bk: S.copy(st[:], ps[bk]))
                    sc = r.inc("scalar", "S")
                    r.wait("tensor", "S", sc)
                    r.wait("sync", "S", sc)
                    r.dma("sync", "dS",
                          bass.AP(tensor=vt_dram.tensor,
                                  offset=(160 * hp) * S_LEN + 512 * tg,
                                  ap=[[S_LEN, 64], [1, 512]]),
                          st[0:64, :])
                    sv = r.dma("sync", "dS",
                               bass.AP(tensor=vt_dram.tensor,
                                       offset=(160 * hp + 80) * S_LEN + 512 * tg,
                                       ap=[[S_LEN, 64], [1, 512]]),
                               st[64:128, :])
                    r.wait("scalar", "dS", sv)
                r.wait("sync", "T", r.val("T"))
            r.barrier()

            for tt in range(NT):
                r.dma("sync", "dS", vtok[tt],
                      vt_dram[:, 128 * tt:128 * (tt + 1)], transpose=True)
            vtok_ld = r.val("dS")
            r.wait("tensor", "dS", vtok_ld)

            for hp in range(8):
                kv0 = 128 * hp
                wlq = r.dma("sync", "dS",
                            wslab[0].rearrange("p (dc c) -> p dc c", c=128),
                            wqkv_dram[0][:, kv0:kv0 + 128].rearrange(
                                "(dc p) c -> p dc c", p=128))
                wlk = r.dma("sync", "dS",
                            wslab[1].rearrange("p (dc c) -> p dc c", c=128),
                            wqkv_dram[1][:, kv0:kv0 + 128].rearrange(
                                "(dc p) c -> p dc c", p=128))
                r.wait("tensor", "dS", wlk)
                for wi, dstp in ((0, qTp), (1, kTp)):
                    for tg in range(4):
                        bk = tg % 2
                        h_, tv = tinc()
                        for dc in range(8):
                            if dc < 7:
                                E("tensor", lambda T, wi=wi, tg=tg, dc=dc, bk=bk: T.matmul(
                                    ps[bk], wslab[wi][:, 128 * dc:128 * (dc + 1)],
                                    ln1T[dc][:, 512 * tg:512 * (tg + 1)],
                                    start=(dc == 0), stop=False))
                            else:
                                E("tensor", lambda T, wi=wi, tg=tg, dc=dc, bk=bk, h_=h_:
                                  T.matmul(ps[bk], wslab[wi][:, 128 * dc:128 * (dc + 1)],
                                           ln1T[dc][:, 512 * tg:512 * (tg + 1)],
                                           start=False, stop=True).then_inc(h_, 1))
                        r.wait("scalar", "T", tv)
                        E("scalar", lambda S, dstp=dstp, tg=tg, bk=bk: S.copy(
                            dstp[:, 512 * tg:512 * (tg + 1)], ps[bk]))
                        sc = r.inc("scalar", "S")
                        r.wait("tensor", "S", sc)

                r.wait("sync", "T", r.val("T"))
                for side, mbuf in ((0, m_e), (1, m_o)):
                    h = 2 * hp + side
                    ld = r.dma("sync", "dS", mbuf,
                               bass.AP(tensor=g_dram.tensor, offset=h * GN,
                                       ap=[[1, 128], [1, MU]]))
                r.wait("tensor", "dS", ld)

                sc_hist = {}
                vv_hist = {}
                pv_hist = {}
                dminus_, dplus_ = _sat_range()
                def tclass(kt, qg):
                    dlo = 128 * kt - 512 * qg - 511
                    dhi = 128 * kt - 512 * qg + 127
                    if dhi <= dminus_:
                        return "lo"
                    if dlo >= dplus_:
                        return "hi"
                    return "mid"
                for qg in range(4):
                    q_sl = slice(512 * qg, 512 * (qg + 1))
                    for kt in range(NT + 2):
                        if kt < NT:
                            j = kt % 3
                            if (qg, kt - 3) in sc_hist:
                                r.wait("tensor", "S", sc_hist[(qg, kt - 3)])
                            off = min(max(U0 - (128 * kt - 512 * qg), 0), MU - 512)
                            E("tensor", lambda T, kt=kt, q_sl=q_sl, j=j: T.matmul(
                                ps[2 * j], kTp[0:64, 128 * kt:128 * (kt + 1)],
                                qTp[0:64, q_sl], start=True, stop=False,
                                tile_position=(0, 0)))
                            E("tensor", lambda T, kt=kt, q_sl=q_sl, j=j: T.matmul(
                                ps[2 * j + 1], kTp[64:128, 128 * kt:128 * (kt + 1)],
                                qTp[64:128, q_sl], start=True, stop=False,
                                tile_position=(64, 0)))
                            E("tensor", lambda T, j=j, off=off: T.matmul(
                                ps[2 * j], j_sb[:], m_e[:, off:off + 512],
                                start=False, stop=True))
                            h_, tv_s = tinc()
                            E("tensor", lambda T, j=j, off=off, h_=h_: T.matmul(
                                ps[2 * j + 1], j_sb[:], m_o[:, off:off + 512],
                                start=False, stop=True).then_inc(h_, 1))
                            ewg = ew[kt % 3]
                            r.wait("scalar", "T", tv_s)
                            if (qg, kt - 3) in pv_hist:
                                r.wait("scalar", "T", pv_hist[(qg, kt - 3)])
                            E("scalar", lambda S, ewg=ewg, j=j: S.activation(
                                out=ewg[:, 0:1024],
                                in_=PS[:, 1024 * j:1024 * (j + 1)],
                                func=AF.Exp, bias=neg40c[:], scale=1.0))
                            sc_hist[(qg, kt)] = r.inc("scalar", "S")
                        if kt >= 2:
                            pkt = kt - 2
                            ewp = ew[pkt % 3]
                            r.wait("tensor", "S", sc_hist[(qg, pkt)])
                            E("tensor", lambda T, hp=hp, pkt=pkt, ewp=ewp: T.matmul(
                                PS[0:65, 3072:3584],
                                vtok[pkt][:, 160 * hp:160 * hp + 65],
                                ewp[:, 0:512],
                                start=(pkt == 0), stop=(pkt == NT - 1)))
                            h_, pvv = tinc()
                            E("tensor", lambda T, hp=hp, pkt=pkt, ewp=ewp, h_=h_:
                              T.matmul(PS[0:65, 3584:4096],
                                       vtok[pkt][:, 160 * hp + 80:160 * hp + 145],
                                       ewp[:, 512:1024],
                                       start=(pkt == 0), stop=(pkt == NT - 1))
                              .then_inc(h_, 1))
                            pv_hist[(qg, pkt)] = pvv
                    r.wait("scalar", "T", pv_hist[(qg, NT - 1)])
                    r.wait("scalar", "dG", r.val("dG"))
                    E("scalar", lambda S, q_sl=q_sl: S.copy(oTu[0:64, q_sl],
                                                            PS[0:64, 3072:3584]))
                    E("scalar", lambda S, q_sl=q_sl: S.copy(oTu[64:128, q_sl],
                                                            PS[0:64, 3584:4096]))
                    E("scalar", lambda S: S.copy(denst[64:65, 0:512], PS[64:65, 3072:3584]))
                    E("scalar", lambda S: S.copy(denst[64:65, 512:1024], PS[64:65, 3584:4096]))
                    sc = r.inc("scalar", "S")
                    sc_hist[(qg, NT - 1)] = sc
                    sc_hist[(qg, NT - 2)] = sc
                    sc_hist[(qg, NT - 3)] = sc
                    r.wait("tensor", "S", sc)
                    r.wait("gpsimd", "S", sc)
                    if qg == 0:
                        r.wait("gpsimd", "V", r.val("V"))
                    s_ = r.sem["dG"]
                    s_[1] += 32
                    E("gpsimd", lambda G, qg=qg, h=s_[0]: (
                        G.dma_start(out=pairdenP[:, 8 * qg:8 * qg + 4],
                                    in_=denst[64:65, 0:512]).then_inc(h, 16),
                        G.dma_start(out=pairdenP[:, 8 * qg + 4:8 * qg + 8],
                                    in_=denst[64:65, 512:1024]).then_inc(h, 16)))
                r.wait("vector", "dG", r.val("dG"))
                r.wait("vector", "dS")
                E("vector", lambda V: V.drain())
                E("vector", lambda V: V.reciprocal(
                    pairrecP[:], pairdenP[:]))
                vv = r.inc("vector", "V")
                r.wait("sync", "V", vv)
                for qg in range(4):
                    r.dma("sync", "dS", pairrec[0:1, 512 * qg:512 * (qg + 1)],
                          pairrecP[:, 8 * qg:8 * qg + 4])
                    sv_sc = r.dma("sync", "dS",
                                  pairrec[1:2, 512 * qg:512 * (qg + 1)],
                                  pairrecP[:, 8 * qg + 4:8 * qg + 8])
                r.wait("tensor", "dS", sv_sc)
                for qg in range(4):
                    E("tensor", lambda T, qg=qg: T.matmul(
                        ps[6], sel2_sb[:], pairrec[:, 512 * qg:512 * (qg + 1)],
                        start=True, stop=True))
                    tv = r.inc("tensor", "T")
                    r.wait("scalar", "T", tv)
                    E("scalar", lambda S, qg=qg: S.copy(
                        recB[:, 512 * qg:512 * (qg + 1)], ps[6]))
                    sc = r.inc("scalar", "S")
                    r.wait("tensor", "S", sc)
                r.wait("vector", "S", r.val("S"))
                E("vector", lambda V: V.tensor_tensor(
                    out=oTp, in0=oTu, in1=recB[:], op=ALU.mult))
                vv = r.inc("vector", "V")
                r.wait("scalar", "V", vv)
                r.wait("sync", "V", vv)
                sv = r.dma("sync", "dS", ot_dram[128 * hp:128 * (hp + 1), :], oTp)
                r.wait("vector", "dS", sv)
            r.barrier()

            for kvc in range(8):
                r.dma("sync", "dS",
                      woslab[kvc].rearrange("p (dc c) -> p dc c", c=128),
                      wqkv_dram[3][128 * kvc:128 * (kvc + 1), :].rearrange(
                          "p (dc c) -> p dc c", c=128))
            for tg in range(4):
                for kvc in range(8):
                    r.dma("sync", "dS", oTld[kvc],
                          ot_dram[128 * kvc:128 * (kvc + 1), 512 * tg:512 * (tg + 1)])
                ld = r.val("dS")
                r.wait("tensor", "dS", ld)
                for dct in range(8):
                    bk = dct % 2
                    h_, tv = tinc()
                    for kvc in range(8):
                        if kvc < 7:
                            E("tensor", lambda T, dct=dct, kvc=kvc, bk=bk: T.matmul(
                                ps[bk], woslab[kvc][:, 128 * dct:128 * (dct + 1)],
                                oTld[kvc], start=(kvc == 0), stop=False))
                        else:
                            E("tensor", lambda T, dct=dct, kvc=kvc, bk=bk, h_=h_:
                              T.matmul(ps[bk], woslab[kvc][:, 128 * dct:128 * (dct + 1)],
                                       oTld[kvc], start=False, stop=True).then_inc(h_, 1))
                    r.wait("scalar", "T", tv)
                    st = evst[dct % 2]
                    E("scalar", lambda S, st=st, bk=bk: S.copy(st[:], ps[bk]))
                    sc = r.inc("scalar", "S")
                    r.wait("vector", "S", sc)
                    E("vector", lambda V, st=st, bk=bk: V.tensor_tensor(
                        out=qst_o[:, 0:512], in0=ps[bk], in1=st[:],
                        op=ALU.subtract))
                    vv = r.inc("vector", "V")
                    r.wait("tensor", "V", vv)
                    r.wait("sync", "V", vv)
                    r.dma("sync", "dS",
                          yat_dram[128 * dct:128 * (dct + 1),
                                   512 * tg:512 * (tg + 1)], st[:])
                    sv = r.dma("sync", "dS",
                               yat2_dram[128 * dct:128 * (dct + 1),
                                         512 * tg:512 * (tg + 1)], qst_o[:, 0:512])
                    r.wait("scalar", "dS", sv)
                    r.wait("vector", "dS", sv)
                r.wait("sync", "T", r.val("T"))
            r.barrier()

            for tt in range(NT):
                xt, ya = xtile[tt % 2], yatile[tt % 2]
                r.guard("sync", f"xtile{tt % 2}")
                r.guard("sync", f"yatile{tt % 2}")
                r.guard("sync", "lntbuf")
                r.dma("sync", "dS", xt[:], x_rows(tt))
                ld = r.dma("sync", "dS", ya[:],
                           yat_dram[:, 128 * tt:128 * (tt + 1)], transpose=True)
                ld = r.dma("sync", "dS", lnt[:],
                           yat2_dram[:, 128 * tt:128 * (tt + 1)], transpose=True)
                r.wait("vector", "dS", ld)
                E("vector", lambda V, xt=xt, ya=ya: V.tensor_tensor(
                    out=x1t[:], in0=xt[:], in1=ya[:], op=ALU.add))
                VD()
                E("vector", lambda V: V.tensor_tensor(
                    out=x1t[:], in0=x1t[:], in1=lnt[:], op=ALU.add))
                vv = r.inc("vector", "V")
                r.set_guard(f"xtile{tt % 2}", "V", vv)
                r.set_guard(f"yatile{tt % 2}", "V", vv)
                r.set_guard("lntbuf", "V", vv)
                r.wait("sync", "V", vv)
                sv = r.dma("sync", "dS", x1_dram[128 * tt:128 * (tt + 1), :], x1t[:])
                r.wait("vector", "dS", sv)
            r.barrier()

            for tt in range(NT):
                ln_tile(x1_dram[128 * tt:128 * (tt + 1), :], xq_dram, tt, True)
            r.barrier()

            for dc in range(8):
                for hg in range(8):
                    r.dma("sync", "dS", w1qT[dc][:, 512 * hg:512 * (hg + 1)],
                          w1q_dram[512 * hg:512 * (hg + 1), 128 * dc:128 * (dc + 1)],
                          transpose=True)
            for dc in range(8):
                for tg in range(4):
                    r.dma("sync", "dS", xqT[dc][:, 512 * tg:512 * (tg + 1)],
                          xq_dram[512 * tg:512 * (tg + 1), 128 * dc:128 * (dc + 1)],
                          transpose=True)
            mm1_ld = r.val("dS")
            r.wait("tensor", "dS", mm1_ld)
            for tt in range(NT):
                h_, tv = tinc()
                for dc in range(8):
                    for hg in range(8):
                        if dc == 7 and hg == 7:
                            E("tensor", lambda T, tt=tt, dc=dc, hg=hg, h_=h_: T.matmul(
                                ps[hg], xqT[dc][:, 128 * tt:128 * (tt + 1)],
                                w1qT[dc][:, 512 * hg:512 * (hg + 1)],
                                start=False, stop=True).then_inc(h_, 1))
                        else:
                            E("tensor", lambda T, tt=tt, dc=dc, hg=hg: T.matmul(
                                ps[hg], xqT[dc][:, 128 * tt:128 * (tt + 1)],
                                w1qT[dc][:, 512 * hg:512 * (hg + 1)],
                                start=(dc == 0), stop=(dc == 7)))
                r.wait("scalar", "T", tv)
                for hg in range(8):
                    E("scalar", lambda S, tt=tt, hg=hg: S.activation(
                        out=hbuf[:, 512 * hg:512 * (hg + 1)], in_=ps[hg],
                        func=AF.Gelu, bias=zeroc[:], scale=sxcols[:, tt:tt + 1]))
                sc = r.inc("scalar", "S")
                r.wait("tensor", "S", sc)
                r.wait("vector", "S", sc)
                r.guard("vector", "stc")
                E("vector", lambda V: V.reduce_max(
                    stc["mcol"][:], hbuf, axis=mybir.AxisListType.X,
                    apply_absolute_value=True))
                VD()
                E("vector", lambda V: V.scalar_tensor_tensor(
                    out=stc["mclip"][:], in0=stc["mcol"][:], scalar=EPS,
                    in1=onec[:], op0=ALU.max, op1=ALU.mult))
                VD()
                E("vector", lambda V: V.reciprocal(
                    stc["invm"][:], stc["mclip"][:]))
                VD()
                E("vector", lambda V: V.scalar_tensor_tensor(
                    out=stc["s127"][:], in0=stc["invm"][:], scalar=127.0,
                    in1=zeroc[:], op0=ALU.mult, op1=ALU.add))
                E("vector", lambda V: V.tensor_tensor(
                    out=stc["tmp2"][:], in0=stc["mclip"][:], in1=s2col[:], op=ALU.mult))
                VD()
                E("vector", lambda V, tt=tt: V.scalar_tensor_tensor(
                    out=sc2cols[:, tt:tt + 1], in0=stc["tmp2"][:], scalar=1.0 / 127.0,
                    in1=zeroc[:], op0=ALU.mult, op1=ALU.add))
                vv = r.inc("vector", "V")
                r.wait("scalar", "V", vv)
                for cc in range(4):
                    csl = slice(1024 * cc, 1024 * (cc + 1))
                    E("scalar", lambda S, csl=csl: S.activation(
                        out=qst_t[:], in_=hbuf[:, csl], func=AF.Identity,
                        bias=halfc[:], scale=stc["s127"][:]))
                    sc = r.inc("scalar", "S")
                    if csl.stop == HID:
                        r.set_guard("stc", "S", sc)
                    r.wait("vector", "S", sc)
                    floor_chain(qst_t[:], hqbuf[:, csl], 1)
                    vv = r.inc("vector", "V")
                    r.wait("scalar", "V", vv)
                r.wait("sync", "V", r.val("V"))
                sv = r.dma("sync", "dS", hq_dram[128 * tt:128 * (tt + 1), :], hqbuf)
                r.wait("vector", "dS", sv)
                r.wait("scalar", "dS", sv)
            r.barrier()

            for hc in range(32):
                for dg in range(2):
                    r.dma("sync", "dS", w2qT[hc][:, 512 * dg:512 * (dg + 1)],
                          w2q_dram[512 * dg:512 * (dg + 1), 128 * hc:128 * (hc + 1)],
                          transpose=True)
            w2_ld = r.val("dS")
            for tg in range(4):
                for hc in range(32):
                    r.dma("sync", "dS", hqT[hc],
                          hq_dram[512 * tg:512 * (tg + 1), 128 * hc:128 * (hc + 1)],
                          transpose=True)
                slab_ld = r.val("dS")
                r.wait("tensor", "dS", max(w2_ld, slab_ld))
                for ttl in range(4):
                    tt = 4 * tg + ttl
                    for dg in range(2):
                        bk = (2 * ttl + dg) % 2
                        h_, tv = tinc()
                        for hc in range(32):
                            if hc < 31:
                                E("tensor", lambda T, ttl=ttl, dg=dg, hc=hc, bk=bk:
                                  T.matmul(ps[bk],
                                           hqT[hc][:, 128 * ttl:128 * (ttl + 1)],
                                           w2qT[hc][:, 512 * dg:512 * (dg + 1)],
                                           start=(hc == 0), stop=False))
                            else:
                                E("tensor", lambda T, ttl=ttl, dg=dg, hc=hc, bk=bk, h_=h_:
                                  T.matmul(ps[bk],
                                           hqT[hc][:, 128 * ttl:128 * (ttl + 1)],
                                           w2qT[hc][:, 512 * dg:512 * (dg + 1)],
                                           start=False, stop=True).then_inc(h_, 1))
                        xt = xtile[(2 * ttl + dg) % 2]
                        r.guard("sync", f"xtile{(2 * ttl + dg) % 2}")
                        ld = r.dma("sync", "dS", xt[:, 0:512],
                                   x1_dram[128 * tt:128 * (tt + 1),
                                           512 * dg:512 * (dg + 1)])
                        r.wait("vector", "dS", ld)
                        r.wait("vector", "T", tv)
                        E("vector", lambda V, tt=tt, bk=bk, xt=xt:
                          V.scalar_tensor_tensor(
                              out=qst_t[:, 0:512], in0=ps[bk],
                              scalar=sc2cols[:, tt:tt + 1], in1=xt[:, 0:512],
                              op0=ALU.mult, op1=ALU.add))
                        vv = r.inc("vector", "V")
                        r.set_guard(f"xtile{(2 * ttl + dg) % 2}", "V", vv)
                        r.wait("tensor", "V", vv)
                        r.wait("sync", "V", vv)
                        sv = r.dma("sync", "dS",
                                   y_out[b * S_LEN + 128 * tt:
                                         b * S_LEN + 128 * (tt + 1),
                                         512 * dg:512 * (dg + 1)], qst_t[:, 0:512])
                        r.wait("vector", "dS", sv)
                r.wait("sync", "T", r.val("T"))
            r.barrier()

        for b in range(B_CORE):
            batch_program(b)

        r.wait("sync", "dS")
        r.wait("sync", "dG")

        with nc.Block() as block:
            @block.sync
            def _(eng):
                for fn in r.ops["sync"]:
                    fn(eng)

            @block.vector
            def _(eng):
                for fn in r.ops["vector"]:
                    fn(eng)

            @block.scalar
            def _(eng):
                for fn in r.ops["scalar"]:
                    fn(eng)

            @block.tensor
            def _(eng):
                for fn in r.ops["tensor"]:
                    fn(eng)

            @block.gpsimd
            def _(eng):
                for fn in r.ops["gpsimd"]:
                    fn(eng)

    return nc


_NC_CACHE = None


def kernel(L0_attn_inputs, L0_attn_Wq, L0_attn_Wk, L0_attn_Wv, L0_attn_Wo,
           L0_attn_rel_emb, L0_ffn_w1, L0_ffn_w2):
    global _NC_CACHE
    x = np.ascontiguousarray(np.asarray(L0_attn_inputs, dtype=np.float32))
    consts = host_constants()
    if _NC_CACHE is None:
        _NC_CACHE = build_nc()
    nc = _NC_CACHE
    shared = {
        "wq_in": np.ascontiguousarray(L0_attn_Wq, dtype=np.float32),
        "wk_in": np.ascontiguousarray(L0_attn_Wk, dtype=np.float32),
        "wv_in": np.ascontiguousarray(L0_attn_Wv, dtype=np.float32),
        "wo_in": np.ascontiguousarray(L0_attn_Wo, dtype=np.float32),
        "rel_in": np.ascontiguousarray(L0_attn_rel_emb, dtype=np.float32),
        "w1_in": np.ascontiguousarray(L0_ffn_w1, dtype=np.float32),
        "w2_in": np.ascontiguousarray(L0_ffn_w2, dtype=np.float32),
        **consts,
    }
    in_maps = []
    for c in range(N_CORES):
        xi = x[B_CORE * c: B_CORE * (c + 1)].reshape(TOK, D)
        in_maps.append({"x_in": np.ascontiguousarray(xi), **shared})
    res = run_bass_kernel_spmd(nc, in_maps, core_ids=list(range(N_CORES)))
    out = np.concatenate(
        [np.asarray(r["y_out"]).reshape(B_CORE, S_LEN, D) for r in res.results],
        axis=0)
    return out.astype(np.float32)


# version 23
# speedup vs baseline: 1.6020x; 1.0204x over previous
import numpy as np
import ml_dtypes
import concourse.bass as bass
import concourse.mybir as mybir
from concourse.bass_utils import run_bass_kernel_spmd

F32 = mybir.dt.float32
BF16 = mybir.dt.bfloat16
I32 = mybir.dt.int32
AF = mybir.ActivationFunctionType
ALU = mybir.AluOpType

B_TOTAL, S_LEN, D = 16, 2048, 1024
N_HEADS, D_KV = 16, 64
HID = 4096
N_CORES = 8
B_CORE = B_TOTAL // N_CORES
TOK = B_CORE * S_LEN
NT = S_LEN // 128
EPS = 1e-5
SHIFT = 40.0
NUM_BUCKETS, MAX_DIST = 32, 128
U0 = 1920
MU = 3968
GN = 4096

ENGINES = ("sync", "vector", "scalar", "tensor", "gpsimd")


def _bucket_table():
    d = np.arange(-2048, 2048)
    n = -d
    ret = np.where(n < 0, 16, 0).astype(np.int64)
    n = np.abs(n)
    is_small = n < 8
    nf = np.maximum(n, 1).astype(np.float32)
    t = np.log(nf / np.float32(8.0)).astype(np.float32)
    t = (t / np.float32(np.log(MAX_DIST / 8.0))).astype(np.float32)
    t = (t * np.float32(8.0)).astype(np.float32)
    val = 8 + t.astype(np.int32)
    val = np.minimum(val, 15)
    return (ret + np.where(is_small, n, val)).astype(np.int64)


def _sat_range():
    bt = _bucket_table()
    lo_val = bt[0]
    d = -2048
    while bt[d + 1 + 2048] == lo_val:
        d += 1
    dminus = d
    hi_val = bt[2047 + 2048]
    d = 2047
    while bt[d - 1 + 2048] == hi_val:
        d -= 1
    dplus = d
    return dminus, dplus


def host_constants():
    bt = _bucket_table()
    gsel = np.zeros((NUM_BUCKETS, GN), dtype=np.float32)
    i = np.arange(GN)
    dd = 2047 - i
    gsel[bt[dd + 2048], i] = 1.0
    jmat = np.zeros((128, 128), dtype=ml_dtypes.bfloat16)
    jmat[np.arange(128), 127 - np.arange(128)] = 1.0
    sel2 = np.zeros((2, 128), dtype=np.float32)
    sel2[0, 0:64] = 1.0
    sel2[1, 64:128] = 1.0
    eblo = np.zeros((NUM_BUCKETS, 1), dtype=np.float32)
    eblo[bt[0], 0] = 1.0
    return {"gsel_c": gsel, "jmat_c": jmat, "sel2_c": sel2, "eblo_c": eblo}


class Rec:
    def __init__(self):
        self.ops = {e: [] for e in ENGINES}
        self.sem = {}

    def add_sem(self, name, handle):
        self.sem[name] = [handle, 0]

    def emit(self, eng, fn):
        self.ops[eng].append(fn)

    def inc(self, eng, sname):
        s = self.sem[sname]
        s[1] += 1
        h = s[0]
        if eng in ("vector", "scalar", "tensor"):
            self.emit(eng, lambda E, h=h: (E.drain(), E.nop().then_inc(h, 1)))
        else:
            self.emit(eng, lambda E, h=h: E.nop().then_inc(h, 1))
        return s[1]

    def dma(self, eng, sname, out, in_, transpose=False):
        s = self.sem[sname]
        s[1] += 16
        h = s[0]
        if transpose:
            self.emit(eng, lambda E, h=h, out=out, in_=in_:
                      E.dma_start_transpose(out, in_).then_inc(h, 16))
        else:
            self.emit(eng, lambda E, h=h, out=out, in_=in_:
                      E.dma_start(out=out, in_=in_).then_inc(h, 16))
        return s[1]

    def wait(self, eng, sname, val=None):
        s = self.sem[sname]
        v = s[1] if val is None else val
        if sname in ("dS", "dG"):
            v = s[1]
            issuer = "sync" if sname == "dS" else "gpsimd"
            if not hasattr(self, "_sw"):
                self._sw = {}
            if self._sw.get(sname, -1) < v:
                self._sw[sname] = v
                if eng != issuer:
                    self.emit(issuer, lambda E, h=s[0], v=v: E.wait_ge(h, v))
        self.emit(eng, lambda E, h=s[0], v=v: E.wait_ge(h, v))

    def val(self, sname):
        return self.sem[sname][1]

    def guard(self, eng, key):
        if key in getattr(self, "bguard", {}):
            sname, v = self.bguard[key]
            self.wait(eng, sname, v)

    def set_guard(self, key, sname, v):
        if not hasattr(self, "bguard"):
            self.bguard = {}
        self.bguard[key] = (sname, v)

    def barrier(self):
        self.wait("sync", "dS")
        self.wait("gpsimd", "dG")
        for e in ENGINES:
            self.inc(e, "ph_" + e)
        for e in ENGINES:
            for o in ENGINES:
                if o != e:
                    self.wait(e, "ph_" + o)


def build_nc(debug=False):
    nc = bass.Bass("TRN2", target_bir_lowering=False, debug=False)
    r = Rec()

    x_in = nc.dram_tensor("x_in", [TOK, D], F32, kind="ExternalInput").ap()
    wq_in = nc.dram_tensor("wq_in", [D, D], F32, kind="ExternalInput").ap()
    wk_in = nc.dram_tensor("wk_in", [D, D], F32, kind="ExternalInput").ap()
    wv_in = nc.dram_tensor("wv_in", [D, D], F32, kind="ExternalInput").ap()
    wo_in = nc.dram_tensor("wo_in", [D, D], F32, kind="ExternalInput").ap()
    rel_in = nc.dram_tensor("rel_in", [NUM_BUCKETS, N_HEADS], F32, kind="ExternalInput").ap()
    w1_in = nc.dram_tensor("w1_in", [HID, D], F32, kind="ExternalInput").ap()
    w2_in = nc.dram_tensor("w2_in", [D, HID], F32, kind="ExternalInput").ap()
    gsel_c = nc.dram_tensor("gsel_c", [NUM_BUCKETS, GN], F32, kind="ExternalInput").ap()
    jmat_c = nc.dram_tensor("jmat_c", [128, 128], BF16, kind="ExternalInput").ap()
    sel2_c = nc.dram_tensor("sel2_c", [2, 128], F32, kind="ExternalInput").ap()
    eblo_c = nc.dram_tensor("eblo_c", [NUM_BUCKETS, 1], F32, kind="ExternalInput").ap()
    y_out = nc.dram_tensor("y_out", [TOK, D], F32, kind="ExternalOutput").ap()

    KD = "ExternalOutput" if debug else "Internal"
    g_dram = nc.dram_tensor("g_dram", [N_HEADS, GN], BF16, kind=KD).ap()
    wqkv_dram = [nc.dram_tensor(f"w{n}_bf", [D, D], BF16, kind=KD).ap()
                 for n in ("q", "k", "v", "o")]
    w1q_dram = nc.dram_tensor("w1q_dram", [HID, D], BF16, kind=KD).ap()
    w2q_dram = nc.dram_tensor("w2q_dram", [D, HID], BF16, kind=KD).ap()
    ln1_dram = nc.dram_tensor("ln1_dram", [S_LEN, D], BF16, kind=KD).ap()
    vt_dram = nc.dram_tensor("vt_dram", [N_HEADS * 80, S_LEN], BF16, kind=KD).ap()
    ot_dram = nc.dram_tensor("ot_dram", [D, S_LEN], BF16, kind=KD).ap()
    yat_dram = nc.dram_tensor("yat_dram", [D, S_LEN], BF16, kind=KD).ap()
    yat2_dram = nc.dram_tensor("yat2_dram", [D, S_LEN], BF16, kind=KD).ap()
    x1_dram = nc.dram_tensor("x1_dram", [S_LEN, D], F32, kind=KD).ap()
    xq_dram = nc.dram_tensor("xq_dram", [S_LEN, D], BF16, kind=KD).ap()
    hq_dram = nc.dram_tensor("hq_dram", [S_LEN, HID], BF16, kind=KD).ap()

    import contextlib
    es = contextlib.ExitStack()
    with es:
        def sb(name, shape, dt):
            return es.enter_context(nc.sbuf_tensor(name, shape, dt))

        neg40c = sb("neg40c", [128, 1], F32)
        halfc = sb("halfc", [128, 1], F32)
        zeroc = sb("zeroc", [128, 1], F32)
        epscl = sb("epscl", [128, 1], F32)
        onec = sb("onec", [128, 1], F32)
        onesrow = sb("onesrow", [1, 128], F32)
        j_sb = sb("j_sb", [128, 128], BF16)
        sel2_sb = sb("sel2_sb", [2, 128], F32)
        rel_sb = sb("rel_sb", [NUM_BUCKETS, N_HEADS], F32)
        s1col = sb("s1col", [128, 1], F32)
        s2col = sb("s2col", [128, 1], F32)
        rscol = sb("rscol", [128, 1], F32)
        sxcols = sb("sxcols", [128, NT], F32)
        sc2cols = sb("sc2cols", [128, NT], F32)
        relrow = sb("relrow", [1, 2 * N_HEADS], F32)
        bcol_sb = sb("bcol_sb", [128, 2 * N_HEADS], F32)
        nrel15 = sb("nrel15", [128, 1], F32)
        eblo_sb = sb("eblo_sb", [NUM_BUCKETS, 1], F32)
        t11 = sb("t11", [1, 8], F32)
        t11b = sb("t11b", [1, 8], F32)
        t11c = sb("t11c", [1, 8], F32)
        stc = {n: sb("stc_" + n, [128, 1], F32) for n in
               ("s", "q", "mneg", "vq", "m2", "var", "veps", "rec", "r", "mr",
                "mcol", "mclip", "invm", "s127", "tmp", "tmp2")}
        qst_t = sb("qst_t", [128, 1024], F32)
        qst_c = sb("qst_c", [128, 1024], I32)
        qst_b = sb("qst_b", [128, 1024], F32)
        qst_o = sb("qst_o", [128, 1024], BF16)
        qst_u = sb("qst_u", [128, 1024], F32)
        xtile = [sb(f"xtile{i}", [128, D], F32) for i in range(2)]
        yatile = [sb(f"yatile{i}", [128, D], BF16) for i in range(2)]
        x1t = sb("x1t", [128, D], F32)
        lnt = sb("lnt", [128, D], BF16)
        lntf = sb("lntf", [128, D], F32)
        evst = [sb(f"evst{i}", [128, 512], BF16) for i in range(2)]
        wsum = sb("wsum", [128, 32], F32)
        wsumt = sb("wsumt", [128, 1], F32)
        AR_N = 61440
        AR = sb("AR", [128, AR_N], BF16)
        ARF_N = 9216
        ARF = sb("ARF", [128, ARF_N], F32)

        def carve(off, p, n, dt=BF16):
            if dt == BF16:
                return AR[0:p, off:off + n]
            return ARF[0:p, off:off + n]

        ln1T = [carve(2048 * i, 128, 2048) for i in range(8)]
        vtok = [carve(16384 + 1280 * i, 128, 1280) for i in range(16)]
        m_e = carve(36864, 128, MU)
        m_o = carve(40832, 128, MU)
        qTp = carve(44800, 128, 2048)
        kTp = carve(46848, 128, 2048)
        oTp = carve(48896, 128, 2048)
        ebuf = [carve(50944 + 512 * i, 128, 512) for i in range(4)]
        wslab = [carve(52992 + 1024 * i, 128, 1024) for i in range(2)]
        mpp = carve(50944, 128, MU)
        ew = [carve(50944, 128, 1024), carve(52992, 128, 1024),
              carve(56576, 128, 1024)]
        gstage = carve(55040, N_HEADS, 512)
        ones_big = carve(55552, 128, 1024)
        onesr2k = carve(56576, 16, 2048)
        w1qT = [carve(4096 * i, 128, 4096) for i in range(8)]
        xqT = [carve(36864 + 2048 * i, 128, 2048) for i in range(8)]
        hbuf = carve(53248, 128, 4096)
        hqbuf = carve(57344, 128, 4096)
        w2qT = [carve(1024 * i, 128, 1024) for i in range(32)]
        hqT = [carve(36864 + 512 * i, 128, 512) for i in range(32)]
        woslab = [carve(36864 + 1024 * i, 128, 1024) for i in range(8)]
        oTld = [carve(45056 + 512 * i, 128, 512) for i in range(8)]
        oTu = carve(0, 128, 2048, F32)
        recB = carve(2048, 128, 2048, F32)
        denst = carve(4096, 128, 1024, F32)
        pairdenP = carve(5120, 128, 32, F32)
        pairrecP = carve(5184, 128, 32, F32)
        pairrec = carve(7168, 2, 2048, F32)
        gselch = carve(0, NUM_BUCKETS, 512, F32)
        wchunk = carve(512, 128, 2048, F32)
        wchunk2 = carve(4096, 128, 2048, F32)
        qst_o2 = carve(50944, 128, 1024)

        PS = es.enter_context(nc.psum_tensor("PS", [128, 4096], F32))
        ps = [PS[:, 512 * i:512 * (i + 1)] for i in range(8)]

        for sname in ("dS", "dG", "V", "S", "T") + tuple("ph_" + e for e in ENGINES):
            r.add_sem(sname, es.enter_context(nc.semaphore(sname)))

        E = r.emit

        def VD():
            E("vector", lambda V: V.drain())

        def tinc():
            s = r.sem["T"]
            s[1] += 1
            return s[0], s[1]


        def floor_chain(t_ap, out_ap, K, post=None):
            E("vector", lambda V, t_ap=t_ap: V.scalar_tensor_tensor(
                out=qst_c[:], in0=t_ap, scalar=0.0, in1=t_ap,
                op0=ALU.add, op1=ALU.max))
            VD()
            for j in range(1, K + 1):
                E("vector", lambda V, t_ap=t_ap, j=j: V.scalar_tensor_tensor(
                    out=qst_b[:], in0=qst_c[:], scalar=float(-j + 1),
                    in1=t_ap, op0=ALU.add, op1=ALU.is_gt))
                if j == 1:
                    E("vector", lambda V, t_ap=t_ap, j=j: V.scalar_tensor_tensor(
                        out=qst_u[:], in0=qst_c[:], scalar=float(j),
                        in1=t_ap, op0=ALU.add, op1=ALU.is_le))
                    VD()
                    E("vector", lambda V: V.tensor_tensor(
                        out=qst_u[:], in0=qst_u[:], in1=qst_b[:], op=ALU.subtract))
                else:
                    VD()
                    E("vector", lambda V: V.tensor_tensor(
                        out=qst_u[:], in0=qst_u[:], in1=qst_b[:], op=ALU.subtract))
                    VD()
                    E("vector", lambda V, t_ap=t_ap, j=j: V.scalar_tensor_tensor(
                        out=qst_b[:], in0=qst_c[:], scalar=float(j),
                        in1=t_ap, op0=ALU.add, op1=ALU.is_le))
                    VD()
                    E("vector", lambda V: V.tensor_tensor(
                        out=qst_u[:], in0=qst_u[:], in1=qst_b[:], op=ALU.add))
                VD()
            E("vector", lambda V, out_ap=out_ap: V.tensor_tensor(
                out=out_ap, in0=qst_c[:], in1=qst_u[:], op=ALU.add))

        E("vector", lambda V: V.memset(neg40c[:], -SHIFT))
        E("vector", lambda V: V.memset(halfc[:], 0.5))
        E("vector", lambda V: V.memset(zeroc[:], 0.0))
        E("vector", lambda V: V.memset(epscl[:], EPS))
        E("vector", lambda V: V.memset(onec[:], 1.0))
        E("vector", lambda V: V.memset(onesrow[:], 1.0))
        E("vector", lambda V: V.memset(ones_big, 1.0))
        E("vector", lambda V: V.memset(onesr2k, 1.0))
        v0 = r.inc("vector", "V")
        r.dma("sync", "dS", j_sb[:], jmat_c)
        r.dma("sync", "dS", sel2_sb[:], sel2_c)
        ld0 = r.dma("sync", "dS", rel_sb[:], rel_in)
        r.wait("sync", "V", v0)
        for h in range(N_HEADS):
            r.dma("sync", "dS", vt_dram[80 * h + 64:80 * h + 80, :], onesr2k)

        for c in range(GN // 512):
            gl = r.dma("sync", "dS", gselch, gsel_c[:, 512 * c:512 * (c + 1)])
            r.wait("tensor", "dS", gl)
            if c == 0:
                r.wait("tensor", "dS", ld0)
            E("tensor", lambda T: T.matmul(PS[0:N_HEADS, 0:512], rel_sb[:], gselch,
                                           start=True, stop=True))
            tv = r.inc("tensor", "T")
            r.wait("scalar", "T", tv)
            E("scalar", lambda S: S.activation(out=gstage, in_=PS[0:N_HEADS, 0:512],
                                               func=AF.Identity, bias=zeroc[0:N_HEADS],
                                               scale=1.0))
            sc = r.inc("scalar", "S")
            r.wait("tensor", "S", sc)
            r.wait("sync", "S", sc)
            st_ = r.dma("sync", "dS", g_dram[:, 512 * c:512 * (c + 1)], gstage)
            r.wait("scalar", "dS", st_)
            r.wait("sync", "dS", st_)

        for w_src, w_dst in zip((wq_in, wk_in, wv_in, wo_in), wqkv_dram):
            for dc in range(8):
                sl = slice(128 * dc, 128 * (dc + 1))
                st = wslab[dc % 2]
                s_ = r.sem["dG"]
                s_[1] += 16
                E("gpsimd", lambda G, w_src=w_src, sl=sl, st=st, h=s_[0]:
                  G.dma_start(out=st, in_=w_src[sl, :]).then_inc(h, 16))
                r.wait("sync", "dG", s_[1])
                sv = r.dma("sync", "dS", w_dst[sl, :], st)
                r.wait("gpsimd", "dS", sv)

        wbufs = (wchunk, wchunk2)
        wbg = {0: None, 1: None}
        obufs = (qst_o, qst_o2)
        obg = {0: None, 1: None}
        wq_state = {"g": 0, "o": 0}

        def wquant(w_src, w_dst, nrows, ncols, scol):
            CW = 2048 if ncols % 2048 == 0 else 1024
            nch = nrows // 128
            ncc = ncols // CW
            E("vector", lambda V: V.memset(wsum[:], 0.0))
            r.inc("vector", "V")
            for ch in range(nch):
                for cc in range(ncc):
                    bi = wq_state["g"] % 2
                    wq_state["g"] += 1
                    buf = wbufs[bi]
                    if wbg[bi]:
                        for sn_, v_ in wbg[bi]:
                            r.wait("sync", sn_, v_)
                    ld = r.dma("sync", "dS", buf[:, 0:CW],
                               w_src[128 * ch:128 * (ch + 1), CW * cc:CW * (cc + 1)])
                    r.wait("vector", "dS", ld)
                    E("vector", lambda V, CW=CW, buf=buf: V.reduce_sum(
                        stc["tmp"][:], buf[:, 0:CW], axis=mybir.AxisListType.X,
                        apply_absolute_value=True))
                    VD()
                    if cc == 0:
                        E("vector", lambda V, ch=ch: V.tensor_tensor(
                            out=wsum[:, ch:ch + 1], in0=stc["tmp"][:], in1=zeroc[:],
                            op=ALU.add))
                    else:
                        E("vector", lambda V, ch=ch: V.tensor_tensor(
                            out=wsum[:, ch:ch + 1], in0=wsum[:, ch:ch + 1],
                            in1=stc["tmp"][:], op=ALU.add))
                    vv = r.inc("vector", "V")
                    wbg[bi] = [("V", vv)]
            E("vector", lambda V: V.drain())
            E("vector", lambda V, n=nch: V.reduce_sum(
                wsumt[:], wsum[:, 0:n], axis=mybir.AxisListType.X))
            vv = r.inc("vector", "V")
            r.wait("tensor", "V", vv)
            E("tensor", lambda T: T.matmul(PS[0:1, 0:1], wsumt[:], onec[:],
                                           start=True, stop=True))
            tv = r.inc("tensor", "T")
            r.wait("scalar", "T", tv)
            E("scalar", lambda S, n=nrows * ncols: S.activation(
                out=t11[0:1, 0:1], in_=PS[0:1, 0:1], func=AF.Identity,
                bias=zeroc[0:1], scale=1.0 / n))
            sc = r.inc("scalar", "S")
            r.wait("vector", "S", sc)
            E("vector", lambda V: V.scalar_tensor_tensor(
                out=t11b[0:1, 0:1], in0=t11[0:1, 0:1], scalar=EPS,
                in1=onec[0:1], op0=ALU.max, op1=ALU.mult))
            VD()
            E("vector", lambda V: V.reciprocal(t11[0:1, 0:1], t11b[0:1, 0:1]))
            vv = r.inc("vector", "V")
            r.wait("tensor", "V", vv)
            E("tensor", lambda T: T.matmul(PS[:, 512:513], onesrow[:], t11[0:1, 0:1],
                                           start=True, stop=True))
            tv = r.inc("tensor", "T")
            r.wait("scalar", "T", tv)
            E("scalar", lambda S: S.copy(rscol[:], PS[:, 512:513]))
            sc = r.inc("scalar", "S")
            r.wait("tensor", "S", sc)
            E("tensor", lambda T: T.matmul(PS[:, 512:513], onesrow[:], t11b[0:1, 0:1],
                                           start=True, stop=True))
            tv = r.inc("tensor", "T")
            r.wait("scalar", "T", tv)
            E("scalar", lambda S, scol=scol: S.copy(scol[:], PS[:, 512:513]))
            sc = r.inc("scalar", "S")
            r.wait("vector", "S", sc)
            for ch in range(nch):
                for cc in range(ncc):
                    bi = wq_state["g"] % 2
                    wq_state["g"] += 1
                    buf = wbufs[bi]
                    if wbg[bi]:
                        for sn_, v_ in wbg[bi]:
                            r.wait("sync", sn_, v_)
                    ld = r.dma("sync", "dS", buf[:, 0:CW],
                               w_src[128 * ch:128 * (ch + 1), CW * cc:CW * (cc + 1)])
                    r.wait("scalar", "dS", ld)
                    E("scalar", lambda S, CW=CW, buf=buf: S.activation(
                        out=buf[:, 0:CW], in_=buf[:, 0:CW], func=AF.Identity,
                        bias=zeroc[:], scale=rscol[:]))
                    sc = r.inc("scalar", "S")
                    r.wait("vector", "S", sc)
                    for sub in range(CW // 1024):
                        ssl = slice(1024 * sub, 1024 * (sub + 1))
                        oi = wq_state["o"] % 2
                        wq_state["o"] += 1
                        ob = obufs[oi]
                        if obg[oi] is not None:
                            r.wait("vector", "dS", obg[oi])
                        E("vector", lambda V, buf=buf, ssl=ssl: V.scalar_tensor_tensor(
                            out=qst_b[:], in0=buf[:, ssl], scalar=0.5,
                            in1=ones_big, op0=ALU.is_ge, op1=ALU.mult))
                        E("vector", lambda V, buf=buf, ssl=ssl: V.scalar_tensor_tensor(
                            out=qst_u[:], in0=buf[:, ssl], scalar=-0.5,
                            in1=ones_big, op0=ALU.is_lt, op1=ALU.mult))
                        VD()
                        E("vector", lambda V, ob=ob: V.tensor_tensor(
                            out=ob[:], in0=qst_b[:], in1=qst_u[:], op=ALU.subtract))
                        vv = r.inc("vector", "V")
                        r.wait("sync", "V", vv)
                        sv = r.dma("sync", "dS",
                                   w_dst[128 * ch:128 * (ch + 1),
                                         CW * cc + 1024 * sub:CW * cc + 1024 * (sub + 1)],
                                   ob[:])
                        obg[oi] = sv
                    wbg[bi] = [("S", sc), ("V", vv)]

        wquant(w1_in, w1q_dram, HID, D, s1col)
        wquant(w2_in, w2q_dram, D, HID, s2col)
        r.barrier()

        def ln_tile(x_src_ap, dst_dram, tt, quant):
            xt = xtile[tt % 2]
            r.guard("sync", f"xtile{tt % 2}")
            ld = r.dma("sync", "dS", xt[:], x_src_ap)
            r.wait("vector", "dS", ld)
            r.guard("vector", "stc")
            E("vector", lambda V, xt=xt: V.reduce_sum(
                stc["s"][:], xt[:], axis=mybir.AxisListType.X))
            E("vector", lambda V, xt=xt: V.tensor_tensor(
                out=qst_b[:], in0=xt[:], in1=xt[:], op=ALU.mult))
            E("vector", lambda V: V.drain())
            E("vector", lambda V: V.reduce_sum(
                stc["q"][:], qst_b[:], axis=mybir.AxisListType.X))
            VD()
            E("vector", lambda V: V.scalar_tensor_tensor(
                out=stc["mneg"][:], in0=stc["s"][:], scalar=-1.0 / D,
                in1=zeroc[:], op0=ALU.mult, op1=ALU.add))
            E("vector", lambda V: V.scalar_tensor_tensor(
                out=stc["vq"][:], in0=stc["q"][:], scalar=1.0 / D,
                in1=zeroc[:], op0=ALU.mult, op1=ALU.add))
            VD()
            E("vector", lambda V: V.tensor_tensor(
                out=stc["m2"][:], in0=stc["mneg"][:], in1=stc["mneg"][:], op=ALU.mult))
            VD()
            E("vector", lambda V: V.tensor_tensor(
                out=stc["var"][:], in0=stc["vq"][:], in1=stc["m2"][:], op=ALU.subtract))
            VD()
            E("vector", lambda V: V.tensor_tensor(
                out=stc["veps"][:], in0=stc["var"][:], in1=epscl[:], op=ALU.add))
            E("vector", lambda V: V.drain())
            E("vector", lambda V: V.reciprocal(
                stc["rec"][:], stc["veps"][:]))
            vv = r.inc("vector", "V")
            r.wait("scalar", "V", vv)
            E("scalar", lambda S: S.activation(
                out=stc["r"][:], in_=stc["rec"][:], func=AF.Sqrt,
                bias=zeroc[:], scale=1.0))
            sc = r.inc("scalar", "S")
            r.wait("vector", "S", sc)
            VD()
            E("vector", lambda V: V.tensor_tensor(
                out=stc["mr"][:], in0=stc["mneg"][:], in1=stc["r"][:], op=ALU.mult))
            vv = r.inc("vector", "V")
            r.wait("scalar", "V", vv)
            if not quant:
                E("scalar", lambda S, xt=xt: S.activation(
                    out=lnt[:], in_=xt[:], func=AF.Identity,
                    bias=stc["mr"][:], scale=stc["r"][:]))
                sc = r.inc("scalar", "S")
                r.set_guard(f"xtile{tt % 2}", "S", sc)
                r.set_guard("stc", "S", sc)
                r.wait("sync", "S", sc)
                sv = r.dma("sync", "dS", dst_dram[128 * tt:128 * (tt + 1), :], lnt[:])
                r.wait("scalar", "dS", sv)
                return
            E("scalar", lambda S, xt=xt: S.activation(
                out=lntf[:], in_=xt[:], func=AF.Identity,
                bias=stc["mr"][:], scale=stc["r"][:]))
            sc = r.inc("scalar", "S")
            r.set_guard(f"xtile{tt % 2}", "S", sc)
            r.wait("vector", "S", sc)
            E("vector", lambda V: V.reduce_max(
                stc["mcol"][:], lntf[:], axis=mybir.AxisListType.X,
                apply_absolute_value=True))
            VD()
            E("vector", lambda V: V.scalar_tensor_tensor(
                out=stc["mclip"][:], in0=stc["mcol"][:], scalar=EPS,
                in1=onec[:], op0=ALU.max, op1=ALU.mult))
            VD()
            E("vector", lambda V: V.reciprocal(
                stc["invm"][:], stc["mclip"][:]))
            VD()
            E("vector", lambda V: V.scalar_tensor_tensor(
                out=stc["s127"][:], in0=stc["invm"][:], scalar=127.0,
                in1=zeroc[:], op0=ALU.mult, op1=ALU.add))
            E("vector", lambda V: V.tensor_tensor(
                out=stc["tmp2"][:], in0=stc["mclip"][:], in1=s1col[:], op=ALU.mult))
            VD()
            E("vector", lambda V, tt=tt: V.scalar_tensor_tensor(
                out=sxcols[:, tt:tt + 1], in0=stc["tmp2"][:], scalar=1.0 / 127.0,
                in1=zeroc[:], op0=ALU.mult, op1=ALU.add))
            vv = r.inc("vector", "V")
            r.wait("scalar", "V", vv)
            E("scalar", lambda S: S.activation(
                out=qst_t[:], in_=lntf[:], func=AF.Identity,
                bias=halfc[:], scale=stc["s127"][:]))
            sc = r.inc("scalar", "S")
            r.set_guard("stc", "S", sc)
            r.wait("vector", "S", sc)
            floor_chain(qst_t[:], qst_o[:], 2)
            vv = r.inc("vector", "V")
            r.wait("sync", "V", vv)
            sv = r.dma("sync", "dS", dst_dram[128 * tt:128 * (tt + 1), :], qst_o[:])
            r.wait("vector", "dS", sv)
            r.wait("scalar", "dS", sv)

        def batch_program(b):
            def x_rows(tt):
                return x_in[b * S_LEN + 128 * tt: b * S_LEN + 128 * (tt + 1), :]

            for tt in range(NT):
                ln_tile(x_rows(tt), ln1_dram, tt, False)
            r.barrier()

            for dc in range(8):
                for tg in range(4):
                    r.dma("sync", "dS", ln1T[dc][:, 512 * tg:512 * (tg + 1)],
                          ln1_dram[512 * tg:512 * (tg + 1), 128 * dc:128 * (dc + 1)],
                          transpose=True)
            r.wait("sync", "dS")
            ln_ld = r.val("dS")

            r.wait("tensor", "dS", ln_ld)
            for hp in range(8):
                kv0 = 128 * hp
                wl = r.dma("sync", "dS",
                           wslab[hp % 2].rearrange("p (dc c) -> p dc c", c=128),
                           wqkv_dram[2][:, kv0:kv0 + 128].rearrange(
                               "(dc p) c -> p dc c", p=128))
                r.wait("tensor", "dS", wl)
                for tg in range(4):
                    bk = tg % 2
                    h_, tv = tinc()
                    for dc in range(8):
                        if dc < 7:
                            E("tensor", lambda T, hp=hp, tg=tg, dc=dc, bk=bk: T.matmul(
                                ps[bk], wslab[hp % 2][:, 128 * dc:128 * (dc + 1)],
                                ln1T[dc][:, 512 * tg:512 * (tg + 1)],
                                start=(dc == 0), stop=False))
                        else:
                            E("tensor", lambda T, hp=hp, tg=tg, dc=dc, bk=bk, h_=h_:
                              T.matmul(ps[bk], wslab[hp % 2][:, 128 * dc:128 * (dc + 1)],
                                       ln1T[dc][:, 512 * tg:512 * (tg + 1)],
                                       start=False, stop=True).then_inc(h_, 1))
                    r.wait("scalar", "T", tv)
                    st = evst[tg % 2]
                    E("scalar", lambda S, st=st, bk=bk: S.copy(st[:], ps[bk]))
                    sc = r.inc("scalar", "S")
                    r.wait("tensor", "S", sc)
                    r.wait("sync", "S", sc)
                    r.dma("sync", "dS",
                          bass.AP(tensor=vt_dram.tensor,
                                  offset=(160 * hp) * S_LEN + 512 * tg,
                                  ap=[[S_LEN, 64], [1, 512]]),
                          st[0:64, :])
                    sv = r.dma("sync", "dS",
                               bass.AP(tensor=vt_dram.tensor,
                                       offset=(160 * hp + 80) * S_LEN + 512 * tg,
                                       ap=[[S_LEN, 64], [1, 512]]),
                               st[64:128, :])
                    r.wait("scalar", "dS", sv)
                r.wait("sync", "T", r.val("T"))
            r.barrier()

            for tt in range(NT):
                r.dma("sync", "dS", vtok[tt],
                      vt_dram[:, 128 * tt:128 * (tt + 1)], transpose=True)
            vtok_ld = r.val("dS")
            r.wait("tensor", "dS", vtok_ld)

            for hp in range(8):
                kv0 = 128 * hp
                wlq = r.dma("sync", "dS",
                            wslab[0].rearrange("p (dc c) -> p dc c", c=128),
                            wqkv_dram[0][:, kv0:kv0 + 128].rearrange(
                                "(dc p) c -> p dc c", p=128))
                wlk = r.dma("sync", "dS",
                            wslab[1].rearrange("p (dc c) -> p dc c", c=128),
                            wqkv_dram[1][:, kv0:kv0 + 128].rearrange(
                                "(dc p) c -> p dc c", p=128))
                r.wait("tensor", "dS", wlk)
                for wi, dstp in ((0, qTp), (1, kTp)):
                    for tg in range(4):
                        bk = tg % 2
                        h_, tv = tinc()
                        for dc in range(8):
                            if dc < 7:
                                E("tensor", lambda T, wi=wi, tg=tg, dc=dc, bk=bk: T.matmul(
                                    ps[bk], wslab[wi][:, 128 * dc:128 * (dc + 1)],
                                    ln1T[dc][:, 512 * tg:512 * (tg + 1)],
                                    start=(dc == 0), stop=False))
                            else:
                                E("tensor", lambda T, wi=wi, tg=tg, dc=dc, bk=bk, h_=h_:
                                  T.matmul(ps[bk], wslab[wi][:, 128 * dc:128 * (dc + 1)],
                                           ln1T[dc][:, 512 * tg:512 * (tg + 1)],
                                           start=False, stop=True).then_inc(h_, 1))
                        r.wait("scalar", "T", tv)
                        E("scalar", lambda S, dstp=dstp, tg=tg, bk=bk: S.copy(
                            dstp[:, 512 * tg:512 * (tg + 1)], ps[bk]))
                        sc = r.inc("scalar", "S")
                        r.wait("tensor", "S", sc)

                r.wait("sync", "T", r.val("T"))
                for side, mbuf in ((0, m_e), (1, m_o)):
                    h = 2 * hp + side
                    ld = r.dma("sync", "dS", mbuf,
                               bass.AP(tensor=g_dram.tensor, offset=h * GN,
                                       ap=[[1, 128], [1, MU]]))
                r.wait("tensor", "dS", ld)

                sc_hist = {}
                vv_hist = {}
                pv_hist = {}
                pv_guard = None
                dminus_, dplus_ = _sat_range()
                def tclass(kt, qg):
                    dlo = 128 * kt - 512 * qg - 511
                    dhi = 128 * kt - 512 * qg + 127
                    if dhi <= dminus_:
                        return "lo"
                    if dlo >= dplus_:
                        return "hi"
                    return "mid"
                for qg in range(4):
                    q_sl = slice(512 * qg, 512 * (qg + 1))
                    for kt in range(NT + 2):
                        if kt < NT:
                            j = kt % 3
                            if (qg, kt - 3) in sc_hist:
                                r.wait("tensor", "S", sc_hist[(qg, kt - 3)])
                            off = min(max(U0 - (128 * kt - 512 * qg), 0), MU - 512)
                            E("tensor", lambda T, kt=kt, q_sl=q_sl, j=j: T.matmul(
                                ps[2 * j], kTp[0:64, 128 * kt:128 * (kt + 1)],
                                qTp[0:64, q_sl], start=True, stop=False,
                                tile_position=(0, 0)))
                            E("tensor", lambda T, kt=kt, q_sl=q_sl, j=j: T.matmul(
                                ps[2 * j + 1], kTp[64:128, 128 * kt:128 * (kt + 1)],
                                qTp[64:128, q_sl], start=True, stop=False,
                                tile_position=(64, 0)))
                            E("tensor", lambda T, j=j, off=off: T.matmul(
                                ps[2 * j], j_sb[:], m_e[:, off:off + 512],
                                start=False, stop=True))
                            h_, tv_s = tinc()
                            E("tensor", lambda T, j=j, off=off, h_=h_: T.matmul(
                                ps[2 * j + 1], j_sb[:], m_o[:, off:off + 512],
                                start=False, stop=True).then_inc(h_, 1))
                            ewg = ew[kt % 3]
                            r.wait("scalar", "T", tv_s)
                            if (qg, kt - 3) in pv_hist:
                                r.wait("scalar", "T", pv_hist[(qg, kt - 3)])
                            E("scalar", lambda S, ewg=ewg, j=j: S.activation(
                                out=ewg[:, 0:1024],
                                in_=PS[:, 1024 * j:1024 * (j + 1)],
                                func=AF.Exp, bias=neg40c[:], scale=1.0))
                            sc_hist[(qg, kt)] = r.inc("scalar", "S")
                        if kt >= 2:
                            pkt = kt - 2
                            ewp = ew[pkt % 3]
                            if pkt == 0 and pv_guard is not None:
                                r.wait("tensor", "S", pv_guard)
                            r.wait("tensor", "S", sc_hist[(qg, pkt)])
                            E("tensor", lambda T, hp=hp, pkt=pkt, ewp=ewp: T.matmul(
                                PS[0:65, 3072:3584],
                                vtok[pkt][:, 160 * hp:160 * hp + 65],
                                ewp[:, 0:512],
                                start=(pkt == 0), stop=(pkt == NT - 1)))
                            h_, pvv = tinc()
                            E("tensor", lambda T, hp=hp, pkt=pkt, ewp=ewp, h_=h_:
                              T.matmul(PS[0:65, 3584:4096],
                                       vtok[pkt][:, 160 * hp + 80:160 * hp + 145],
                                       ewp[:, 512:1024],
                                       start=(pkt == 0), stop=(pkt == NT - 1))
                              .then_inc(h_, 1))
                            pv_hist[(qg, pkt)] = pvv
                    r.wait("scalar", "T", pv_hist[(qg, NT - 1)])
                    r.wait("scalar", "dG", r.val("dG"))
                    E("scalar", lambda S, q_sl=q_sl: S.copy(oTu[0:64, q_sl],
                                                            PS[0:64, 3072:3584]))
                    E("scalar", lambda S, q_sl=q_sl: S.copy(oTu[64:128, q_sl],
                                                            PS[0:64, 3584:4096]))
                    E("scalar", lambda S: S.copy(denst[64:65, 0:512], PS[64:65, 3072:3584]))
                    E("scalar", lambda S: S.copy(denst[64:65, 512:1024], PS[64:65, 3584:4096]))
                    sc = r.inc("scalar", "S")
                    sc_hist[(qg, NT - 1)] = sc
                    sc_hist[(qg, NT - 2)] = sc
                    sc_hist[(qg, NT - 3)] = sc
                    pv_guard = sc
                    r.wait("gpsimd", "S", sc)
                    if qg == 0:
                        r.wait("gpsimd", "V", r.val("V"))
                    s_ = r.sem["dG"]
                    s_[1] += 32
                    E("gpsimd", lambda G, qg=qg, h=s_[0]: (
                        G.dma_start(out=pairdenP[:, 8 * qg:8 * qg + 4],
                                    in_=denst[64:65, 0:512]).then_inc(h, 16),
                        G.dma_start(out=pairdenP[:, 8 * qg + 4:8 * qg + 8],
                                    in_=denst[64:65, 512:1024]).then_inc(h, 16)))
                r.wait("vector", "dG", r.val("dG"))
                r.wait("vector", "dS")
                E("vector", lambda V: V.drain())
                E("vector", lambda V: V.reciprocal(
                    pairrecP[:], pairdenP[:]))
                vv = r.inc("vector", "V")
                r.wait("sync", "V", vv)
                for qg in range(4):
                    r.dma("sync", "dS", pairrec[0:1, 512 * qg:512 * (qg + 1)],
                          pairrecP[:, 8 * qg:8 * qg + 4])
                    sv_sc = r.dma("sync", "dS",
                                  pairrec[1:2, 512 * qg:512 * (qg + 1)],
                                  pairrecP[:, 8 * qg + 4:8 * qg + 8])
                r.wait("tensor", "dS", sv_sc)
                for qg in range(4):
                    E("tensor", lambda T, qg=qg: T.matmul(
                        ps[6], sel2_sb[:], pairrec[:, 512 * qg:512 * (qg + 1)],
                        start=True, stop=True))
                    tv = r.inc("tensor", "T")
                    r.wait("scalar", "T", tv)
                    E("scalar", lambda S, qg=qg: S.copy(
                        recB[:, 512 * qg:512 * (qg + 1)], ps[6]))
                    sc = r.inc("scalar", "S")
                    r.wait("tensor", "S", sc)
                r.wait("vector", "S", r.val("S"))
                E("vector", lambda V: V.tensor_tensor(
                    out=oTp, in0=oTu, in1=recB[:], op=ALU.mult))
                vv = r.inc("vector", "V")
                r.wait("scalar", "V", vv)
                r.wait("sync", "V", vv)
                sv = r.dma("sync", "dS", ot_dram[128 * hp:128 * (hp + 1), :], oTp)
                r.wait("vector", "dS", sv)
            r.barrier()

            for kvc in range(8):
                r.dma("sync", "dS",
                      woslab[kvc].rearrange("p (dc c) -> p dc c", c=128),
                      wqkv_dram[3][128 * kvc:128 * (kvc + 1), :].rearrange(
                          "p (dc c) -> p dc c", c=128))
            for tg in range(4):
                for kvc in range(8):
                    r.dma("sync", "dS", oTld[kvc],
                          ot_dram[128 * kvc:128 * (kvc + 1), 512 * tg:512 * (tg + 1)])
                ld = r.val("dS")
                r.wait("tensor", "dS", ld)
                for dct in range(8):
                    bk = dct % 2
                    h_, tv = tinc()
                    for kvc in range(8):
                        if kvc < 7:
                            E("tensor", lambda T, dct=dct, kvc=kvc, bk=bk: T.matmul(
                                ps[bk], woslab[kvc][:, 128 * dct:128 * (dct + 1)],
                                oTld[kvc], start=(kvc == 0), stop=False))
                        else:
                            E("tensor", lambda T, dct=dct, kvc=kvc, bk=bk, h_=h_:
                              T.matmul(ps[bk], woslab[kvc][:, 128 * dct:128 * (dct + 1)],
                                       oTld[kvc], start=False, stop=True).then_inc(h_, 1))
                    r.wait("scalar", "T", tv)
                    st = evst[dct % 2]
                    E("scalar", lambda S, st=st, bk=bk: S.copy(st[:], ps[bk]))
                    sc = r.inc("scalar", "S")
                    r.wait("vector", "S", sc)
                    E("vector", lambda V, st=st, bk=bk: V.tensor_tensor(
                        out=qst_o[:, 0:512], in0=ps[bk], in1=st[:],
                        op=ALU.subtract))
                    vv = r.inc("vector", "V")
                    r.wait("tensor", "V", vv)
                    r.wait("sync", "V", vv)
                    r.dma("sync", "dS",
                          yat_dram[128 * dct:128 * (dct + 1),
                                   512 * tg:512 * (tg + 1)], st[:])
                    sv = r.dma("sync", "dS",
                               yat2_dram[128 * dct:128 * (dct + 1),
                                         512 * tg:512 * (tg + 1)], qst_o[:, 0:512])
                    r.wait("scalar", "dS", sv)
                    r.wait("vector", "dS", sv)
                r.wait("sync", "T", r.val("T"))
            r.barrier()

            for tt in range(NT):
                xt, ya = xtile[tt % 2], yatile[tt % 2]
                r.guard("sync", f"xtile{tt % 2}")
                r.guard("sync", f"yatile{tt % 2}")
                r.guard("sync", "lntbuf")
                r.dma("sync", "dS", xt[:], x_rows(tt))
                ld = r.dma("sync", "dS", ya[:],
                           yat_dram[:, 128 * tt:128 * (tt + 1)], transpose=True)
                ld = r.dma("sync", "dS", lnt[:],
                           yat2_dram[:, 128 * tt:128 * (tt + 1)], transpose=True)
                r.wait("vector", "dS", ld)
                E("vector", lambda V, xt=xt, ya=ya: V.tensor_tensor(
                    out=x1t[:], in0=xt[:], in1=ya[:], op=ALU.add))
                VD()
                E("vector", lambda V: V.tensor_tensor(
                    out=x1t[:], in0=x1t[:], in1=lnt[:], op=ALU.add))
                vv = r.inc("vector", "V")
                r.set_guard(f"xtile{tt % 2}", "V", vv)
                r.set_guard(f"yatile{tt % 2}", "V", vv)
                r.set_guard("lntbuf", "V", vv)
                r.wait("sync", "V", vv)
                sv = r.dma("sync", "dS", x1_dram[128 * tt:128 * (tt + 1), :], x1t[:])
                r.wait("vector", "dS", sv)
            r.barrier()

            for tt in range(NT):
                ln_tile(x1_dram[128 * tt:128 * (tt + 1), :], xq_dram, tt, True)
            r.barrier()

            for dc in range(8):
                for hg in range(8):
                    r.dma("sync", "dS", w1qT[dc][:, 512 * hg:512 * (hg + 1)],
                          w1q_dram[512 * hg:512 * (hg + 1), 128 * dc:128 * (dc + 1)],
                          transpose=True)
            for dc in range(8):
                for tg in range(4):
                    r.dma("sync", "dS", xqT[dc][:, 512 * tg:512 * (tg + 1)],
                          xq_dram[512 * tg:512 * (tg + 1), 128 * dc:128 * (dc + 1)],
                          transpose=True)
            mm1_ld = r.val("dS")
            r.wait("tensor", "dS", mm1_ld)
            for tt in range(NT):
                h_, tv = tinc()
                for dc in range(8):
                    for hg in range(8):
                        if dc == 7 and hg == 7:
                            E("tensor", lambda T, tt=tt, dc=dc, hg=hg, h_=h_: T.matmul(
                                ps[hg], xqT[dc][:, 128 * tt:128 * (tt + 1)],
                                w1qT[dc][:, 512 * hg:512 * (hg + 1)],
                                start=False, stop=True).then_inc(h_, 1))
                        else:
                            E("tensor", lambda T, tt=tt, dc=dc, hg=hg: T.matmul(
                                ps[hg], xqT[dc][:, 128 * tt:128 * (tt + 1)],
                                w1qT[dc][:, 512 * hg:512 * (hg + 1)],
                                start=(dc == 0), stop=(dc == 7)))
                r.wait("scalar", "T", tv)
                for hg in range(8):
                    E("scalar", lambda S, tt=tt, hg=hg: S.activation(
                        out=hbuf[:, 512 * hg:512 * (hg + 1)], in_=ps[hg],
                        func=AF.Gelu, bias=zeroc[:], scale=sxcols[:, tt:tt + 1]))
                sc = r.inc("scalar", "S")
                r.wait("tensor", "S", sc)
                r.wait("vector", "S", sc)
                r.guard("vector", "stc")
                E("vector", lambda V: V.reduce_max(
                    stc["mcol"][:], hbuf, axis=mybir.AxisListType.X,
                    apply_absolute_value=True))
                VD()
                E("vector", lambda V: V.scalar_tensor_tensor(
                    out=stc["mclip"][:], in0=stc["mcol"][:], scalar=EPS,
                    in1=onec[:], op0=ALU.max, op1=ALU.mult))
                VD()
                E("vector", lambda V: V.reciprocal(
                    stc["invm"][:], stc["mclip"][:]))
                VD()
                E("vector", lambda V: V.scalar_tensor_tensor(
                    out=stc["s127"][:], in0=stc["invm"][:], scalar=127.0,
                    in1=zeroc[:], op0=ALU.mult, op1=ALU.add))
                E("vector", lambda V: V.tensor_tensor(
                    out=stc["tmp2"][:], in0=stc["mclip"][:], in1=s2col[:], op=ALU.mult))
                VD()
                E("vector", lambda V, tt=tt: V.scalar_tensor_tensor(
                    out=sc2cols[:, tt:tt + 1], in0=stc["tmp2"][:], scalar=1.0 / 127.0,
                    in1=zeroc[:], op0=ALU.mult, op1=ALU.add))
                vv = r.inc("vector", "V")
                r.wait("scalar", "V", vv)
                for cc in range(4):
                    csl = slice(1024 * cc, 1024 * (cc + 1))
                    E("scalar", lambda S, csl=csl: S.activation(
                        out=qst_t[:], in_=hbuf[:, csl], func=AF.Identity,
                        bias=halfc[:], scale=stc["s127"][:]))
                    sc = r.inc("scalar", "S")
                    if csl.stop == HID:
                        r.set_guard("stc", "S", sc)
                    r.wait("vector", "S", sc)
                    floor_chain(qst_t[:], hqbuf[:, csl], 1)
                    vv = r.inc("vector", "V")
                    r.wait("scalar", "V", vv)
                r.wait("sync", "V", r.val("V"))
                sv = r.dma("sync", "dS", hq_dram[128 * tt:128 * (tt + 1), :], hqbuf)
                r.wait("vector", "dS", sv)
                r.wait("scalar", "dS", sv)
            r.barrier()

            for hc in range(32):
                for dg in range(2):
                    r.dma("sync", "dS", w2qT[hc][:, 512 * dg:512 * (dg + 1)],
                          w2q_dram[512 * dg:512 * (dg + 1), 128 * hc:128 * (hc + 1)],
                          transpose=True)
            w2_ld = r.val("dS")
            for tg in range(4):
                for hc in range(32):
                    r.dma("sync", "dS", hqT[hc],
                          hq_dram[512 * tg:512 * (tg + 1), 128 * hc:128 * (hc + 1)],
                          transpose=True)
                slab_ld = r.val("dS")
                r.wait("tensor", "dS", max(w2_ld, slab_ld))
                for ttl in range(4):
                    tt = 4 * tg + ttl
                    for dg in range(2):
                        bk = (2 * ttl + dg) % 2
                        h_, tv = tinc()
                        for hc in range(32):
                            if hc < 31:
                                E("tensor", lambda T, ttl=ttl, dg=dg, hc=hc, bk=bk:
                                  T.matmul(ps[bk],
                                           hqT[hc][:, 128 * ttl:128 * (ttl + 1)],
                                           w2qT[hc][:, 512 * dg:512 * (dg + 1)],
                                           start=(hc == 0), stop=False))
                            else:
                                E("tensor", lambda T, ttl=ttl, dg=dg, hc=hc, bk=bk, h_=h_:
                                  T.matmul(ps[bk],
                                           hqT[hc][:, 128 * ttl:128 * (ttl + 1)],
                                           w2qT[hc][:, 512 * dg:512 * (dg + 1)],
                                           start=False, stop=True).then_inc(h_, 1))
                        xt = xtile[(2 * ttl + dg) % 2]
                        r.guard("sync", f"xtile{(2 * ttl + dg) % 2}")
                        ld = r.dma("sync", "dS", xt[:, 0:512],
                                   x1_dram[128 * tt:128 * (tt + 1),
                                           512 * dg:512 * (dg + 1)])
                        r.wait("vector", "dS", ld)
                        r.wait("vector", "T", tv)
                        E("vector", lambda V, tt=tt, bk=bk, xt=xt:
                          V.scalar_tensor_tensor(
                              out=qst_t[:, 0:512], in0=ps[bk],
                              scalar=sc2cols[:, tt:tt + 1], in1=xt[:, 0:512],
                              op0=ALU.mult, op1=ALU.add))
                        vv = r.inc("vector", "V")
                        r.set_guard(f"xtile{(2 * ttl + dg) % 2}", "V", vv)
                        r.wait("tensor", "V", vv)
                        r.wait("sync", "V", vv)
                        sv = r.dma("sync", "dS",
                                   y_out[b * S_LEN + 128 * tt:
                                         b * S_LEN + 128 * (tt + 1),
                                         512 * dg:512 * (dg + 1)], qst_t[:, 0:512])
                        r.wait("vector", "dS", sv)
                r.wait("sync", "T", r.val("T"))
            r.barrier()

        for b in range(B_CORE):
            batch_program(b)

        r.wait("sync", "dS")
        r.wait("sync", "dG")

        with nc.Block() as block:
            @block.sync
            def _(eng):
                for fn in r.ops["sync"]:
                    fn(eng)

            @block.vector
            def _(eng):
                for fn in r.ops["vector"]:
                    fn(eng)

            @block.scalar
            def _(eng):
                for fn in r.ops["scalar"]:
                    fn(eng)

            @block.tensor
            def _(eng):
                for fn in r.ops["tensor"]:
                    fn(eng)

            @block.gpsimd
            def _(eng):
                for fn in r.ops["gpsimd"]:
                    fn(eng)

    return nc


_NC_CACHE = None


def kernel(L0_attn_inputs, L0_attn_Wq, L0_attn_Wk, L0_attn_Wv, L0_attn_Wo,
           L0_attn_rel_emb, L0_ffn_w1, L0_ffn_w2):
    global _NC_CACHE
    x = np.ascontiguousarray(np.asarray(L0_attn_inputs, dtype=np.float32))
    consts = host_constants()
    if _NC_CACHE is None:
        _NC_CACHE = build_nc()
    nc = _NC_CACHE
    shared = {
        "wq_in": np.ascontiguousarray(L0_attn_Wq, dtype=np.float32),
        "wk_in": np.ascontiguousarray(L0_attn_Wk, dtype=np.float32),
        "wv_in": np.ascontiguousarray(L0_attn_Wv, dtype=np.float32),
        "wo_in": np.ascontiguousarray(L0_attn_Wo, dtype=np.float32),
        "rel_in": np.ascontiguousarray(L0_attn_rel_emb, dtype=np.float32),
        "w1_in": np.ascontiguousarray(L0_ffn_w1, dtype=np.float32),
        "w2_in": np.ascontiguousarray(L0_ffn_w2, dtype=np.float32),
        **consts,
    }
    in_maps = []
    for c in range(N_CORES):
        xi = x[B_CORE * c: B_CORE * (c + 1)].reshape(TOK, D)
        in_maps.append({"x_in": np.ascontiguousarray(xi), **shared})
    res = run_bass_kernel_spmd(nc, in_maps, core_ids=list(range(N_CORES)))
    out = np.concatenate(
        [np.asarray(r["y_out"]).reshape(B_CORE, S_LEN, D) for r in res.results],
        axis=0)
    return out.astype(np.float32)


# version 24
# speedup vs baseline: 1.6213x; 1.0121x over previous
import numpy as np
import ml_dtypes
import concourse.bass as bass
import concourse.mybir as mybir
from concourse.bass_utils import run_bass_kernel_spmd

F32 = mybir.dt.float32
BF16 = mybir.dt.bfloat16
I32 = mybir.dt.int32
AF = mybir.ActivationFunctionType
ALU = mybir.AluOpType

B_TOTAL, S_LEN, D = 16, 2048, 1024
N_HEADS, D_KV = 16, 64
HID = 4096
N_CORES = 8
B_CORE = B_TOTAL // N_CORES
TOK = B_CORE * S_LEN
NT = S_LEN // 128
EPS = 1e-5
SHIFT = 40.0
NUM_BUCKETS, MAX_DIST = 32, 128
U0 = 1920
MU = 3968
GN = 4096

ENGINES = ("sync", "vector", "scalar", "tensor", "gpsimd")


def _bucket_table():
    d = np.arange(-2048, 2048)
    n = -d
    ret = np.where(n < 0, 16, 0).astype(np.int64)
    n = np.abs(n)
    is_small = n < 8
    nf = np.maximum(n, 1).astype(np.float32)
    t = np.log(nf / np.float32(8.0)).astype(np.float32)
    t = (t / np.float32(np.log(MAX_DIST / 8.0))).astype(np.float32)
    t = (t * np.float32(8.0)).astype(np.float32)
    val = 8 + t.astype(np.int32)
    val = np.minimum(val, 15)
    return (ret + np.where(is_small, n, val)).astype(np.int64)


def _sat_range():
    bt = _bucket_table()
    lo_val = bt[0]
    d = -2048
    while bt[d + 1 + 2048] == lo_val:
        d += 1
    dminus = d
    hi_val = bt[2047 + 2048]
    d = 2047
    while bt[d - 1 + 2048] == hi_val:
        d -= 1
    dplus = d
    return dminus, dplus


def host_constants():
    bt = _bucket_table()
    gsel = np.zeros((NUM_BUCKETS, GN), dtype=np.float32)
    i = np.arange(GN)
    dd = 2047 - i
    gsel[bt[dd + 2048], i] = 1.0
    jmat = np.zeros((128, 128), dtype=ml_dtypes.bfloat16)
    jmat[np.arange(128), 127 - np.arange(128)] = 1.0
    sel2 = np.zeros((2, 128), dtype=np.float32)
    sel2[0, 0:64] = 1.0
    sel2[1, 64:128] = 1.0
    eblo = np.zeros((NUM_BUCKETS, 1), dtype=np.float32)
    eblo[bt[0], 0] = 1.0
    return {"gsel_c": gsel, "jmat_c": jmat, "sel2_c": sel2, "eblo_c": eblo}


class Rec:
    def __init__(self):
        self.ops = {e: [] for e in ENGINES}
        self.sem = {}

    def add_sem(self, name, handle):
        self.sem[name] = [handle, 0]

    def emit(self, eng, fn):
        self.ops[eng].append(fn)

    def inc(self, eng, sname):
        s = self.sem[sname]
        s[1] += 1
        h = s[0]
        if eng in ("vector", "scalar", "tensor"):
            self.emit(eng, lambda E, h=h: (E.drain(), E.nop().then_inc(h, 1)))
        else:
            self.emit(eng, lambda E, h=h: E.nop().then_inc(h, 1))
        return s[1]

    def dma(self, eng, sname, out, in_, transpose=False):
        s = self.sem[sname]
        s[1] += 16
        h = s[0]
        if transpose:
            self.emit(eng, lambda E, h=h, out=out, in_=in_:
                      E.dma_start_transpose(out, in_).then_inc(h, 16))
        else:
            self.emit(eng, lambda E, h=h, out=out, in_=in_:
                      E.dma_start(out=out, in_=in_).then_inc(h, 16))
        return s[1]

    def wait(self, eng, sname, val=None):
        s = self.sem[sname]
        v = s[1] if val is None else val
        if sname in ("dS", "dG"):
            v = s[1]
            issuer = "sync" if sname == "dS" else "gpsimd"
            if not hasattr(self, "_sw"):
                self._sw = {}
            if self._sw.get(sname, -1) < v:
                self._sw[sname] = v
                if eng != issuer:
                    self.emit(issuer, lambda E, h=s[0], v=v: E.wait_ge(h, v))
        self.emit(eng, lambda E, h=s[0], v=v: E.wait_ge(h, v))

    def val(self, sname):
        return self.sem[sname][1]

    def guard(self, eng, key):
        if key in getattr(self, "bguard", {}):
            sname, v = self.bguard[key]
            self.wait(eng, sname, v)

    def set_guard(self, key, sname, v):
        if not hasattr(self, "bguard"):
            self.bguard = {}
        self.bguard[key] = (sname, v)

    def barrier(self):
        self.wait("sync", "dS")
        self.wait("gpsimd", "dG")
        for e in ENGINES:
            self.inc(e, "ph_" + e)
        for e in ENGINES:
            for o in ENGINES:
                if o != e:
                    self.wait(e, "ph_" + o)


def build_nc(debug=False):
    nc = bass.Bass("TRN2", target_bir_lowering=False, debug=False)
    r = Rec()

    x_in = nc.dram_tensor("x_in", [TOK, D], F32, kind="ExternalInput").ap()
    wq_in = nc.dram_tensor("wq_in", [D, D], F32, kind="ExternalInput").ap()
    wk_in = nc.dram_tensor("wk_in", [D, D], F32, kind="ExternalInput").ap()
    wv_in = nc.dram_tensor("wv_in", [D, D], F32, kind="ExternalInput").ap()
    wo_in = nc.dram_tensor("wo_in", [D, D], F32, kind="ExternalInput").ap()
    rel_in = nc.dram_tensor("rel_in", [NUM_BUCKETS, N_HEADS], F32, kind="ExternalInput").ap()
    w1_in = nc.dram_tensor("w1_in", [HID, D], F32, kind="ExternalInput").ap()
    w2_in = nc.dram_tensor("w2_in", [D, HID], F32, kind="ExternalInput").ap()
    gsel_c = nc.dram_tensor("gsel_c", [NUM_BUCKETS, GN], F32, kind="ExternalInput").ap()
    jmat_c = nc.dram_tensor("jmat_c", [128, 128], BF16, kind="ExternalInput").ap()
    sel2_c = nc.dram_tensor("sel2_c", [2, 128], F32, kind="ExternalInput").ap()
    eblo_c = nc.dram_tensor("eblo_c", [NUM_BUCKETS, 1], F32, kind="ExternalInput").ap()
    y_out = nc.dram_tensor("y_out", [TOK, D], F32, kind="ExternalOutput").ap()

    KD = "ExternalOutput" if debug else "Internal"
    g_dram = nc.dram_tensor("g_dram", [N_HEADS, GN], BF16, kind=KD).ap()
    wqkv_dram = [nc.dram_tensor(f"w{n}_bf", [D, D], BF16, kind=KD).ap()
                 for n in ("q", "k", "v", "o")]
    w1q_dram = nc.dram_tensor("w1q_dram", [HID, D], BF16, kind=KD).ap()
    w2q_dram = nc.dram_tensor("w2q_dram", [D, HID], BF16, kind=KD).ap()
    ln1_dram = nc.dram_tensor("ln1_dram", [S_LEN, D], BF16, kind=KD).ap()
    vt_dram = nc.dram_tensor("vt_dram", [N_HEADS * 80, S_LEN], BF16, kind=KD).ap()
    ot_dram = nc.dram_tensor("ot_dram", [D, S_LEN], BF16, kind=KD).ap()
    yat_dram = nc.dram_tensor("yat_dram", [D, S_LEN], BF16, kind=KD).ap()
    yat2_dram = nc.dram_tensor("yat2_dram", [D, S_LEN], BF16, kind=KD).ap()
    x1_dram = nc.dram_tensor("x1_dram", [S_LEN, D], F32, kind=KD).ap()
    xq_dram = nc.dram_tensor("xq_dram", [S_LEN, D], BF16, kind=KD).ap()
    hq_dram = nc.dram_tensor("hq_dram", [S_LEN, HID], BF16, kind=KD).ap()

    import contextlib
    es = contextlib.ExitStack()
    with es:
        def sb(name, shape, dt):
            return es.enter_context(nc.sbuf_tensor(name, shape, dt))

        neg40c = sb("neg40c", [128, 1], F32)
        halfc = sb("halfc", [128, 1], F32)
        zeroc = sb("zeroc", [128, 1], F32)
        epscl = sb("epscl", [128, 1], F32)
        onec = sb("onec", [128, 1], F32)
        onesrow = sb("onesrow", [1, 128], F32)
        j_sb = sb("j_sb", [128, 128], BF16)
        sel2_sb = sb("sel2_sb", [2, 128], F32)
        rel_sb = sb("rel_sb", [NUM_BUCKETS, N_HEADS], F32)
        s1col = sb("s1col", [128, 1], F32)
        s2col = sb("s2col", [128, 1], F32)
        rscol = sb("rscol", [128, 1], F32)
        sxcols = sb("sxcols", [128, NT], F32)
        sc2cols = sb("sc2cols", [128, NT], F32)
        relrow = sb("relrow", [1, 2 * N_HEADS], F32)
        bcol_sb = sb("bcol_sb", [128, 2 * N_HEADS], F32)
        nrel15 = sb("nrel15", [128, 1], F32)
        eblo_sb = sb("eblo_sb", [NUM_BUCKETS, 1], F32)
        t11 = sb("t11", [1, 8], F32)
        t11b = sb("t11b", [1, 8], F32)
        t11c = sb("t11c", [1, 8], F32)
        stc = {n: sb("stc_" + n, [128, 1], F32) for n in
               ("s", "q", "mneg", "vq", "m2", "var", "veps", "rec", "r", "mr",
                "mcol", "mclip", "invm", "s127", "tmp", "tmp2")}
        qst_t = sb("qst_t", [128, 1024], F32)
        qst_c = sb("qst_c", [128, 1024], I32)
        qst_b = sb("qst_b", [128, 1024], F32)
        qst_o = sb("qst_o", [128, 1024], BF16)
        qst_u = sb("qst_u", [128, 1024], F32)
        xtile = [sb(f"xtile{i}", [128, D], F32) for i in range(2)]
        yatile = [sb(f"yatile{i}", [128, D], BF16) for i in range(2)]
        x1t = sb("x1t", [128, D], F32)
        lnt = sb("lnt", [128, D], BF16)
        lntf = sb("lntf", [128, D], F32)
        evst = [sb(f"evst{i}", [128, 512], BF16) for i in range(2)]
        wsum = sb("wsum", [128, 32], F32)
        wsumt = sb("wsumt", [128, 1], F32)
        AR_N = 61440
        AR = sb("AR", [128, AR_N], BF16)
        ARF_N = 9216
        ARF = sb("ARF", [128, ARF_N], F32)

        def carve(off, p, n, dt=BF16):
            if dt == BF16:
                return AR[0:p, off:off + n]
            return ARF[0:p, off:off + n]

        ln1T = [carve(2048 * i, 128, 2048) for i in range(8)]
        vtok = [carve(16384 + 1280 * i, 128, 1280) for i in range(16)]
        m_e = carve(36864, 128, MU)
        m_o = carve(40832, 128, MU)
        qTp = carve(44800, 128, 2048)
        kTp = carve(46848, 128, 2048)
        oTp = carve(48896, 128, 2048)
        ebuf = [carve(50944 + 512 * i, 128, 512) for i in range(4)]
        wslab = [carve(52992 + 1024 * i, 128, 1024) for i in range(2)]
        mpp = carve(50944, 128, MU)
        ew = [carve(50944, 128, 1024), carve(52992, 128, 1024),
              carve(56576, 128, 1024)]
        gstage = carve(55040, N_HEADS, 512)
        ones_big = carve(55552, 128, 1024)
        onesr2k = carve(56576, 16, 2048)
        w1qT = [carve(4096 * i, 128, 4096) for i in range(8)]
        xqT = [carve(36864 + 2048 * i, 128, 2048) for i in range(8)]
        hbuf = carve(53248, 128, 4096)
        hqbuf = carve(57344, 128, 4096)
        w2qT = [carve(1024 * i, 128, 1024) for i in range(32)]
        hqT = [carve(36864 + 512 * i, 128, 512) for i in range(32)]
        woslab = [carve(36864 + 1024 * i, 128, 1024) for i in range(8)]
        oTld = [carve(45056 + 512 * i, 128, 512) for i in range(8)]
        oTu = carve(0, 128, 2048, F32)
        recB = carve(2048, 128, 2048, F32)
        denst = carve(4096, 128, 1024, F32)
        pairdenP = carve(5120, 128, 32, F32)
        pairrecP = carve(5184, 128, 32, F32)
        pairrec = carve(7168, 2, 2048, F32)
        gselch = carve(0, NUM_BUCKETS, 512, F32)
        wchunk = carve(512, 128, 2048, F32)
        wchunk2 = carve(4096, 128, 2048, F32)
        qw_b0 = carve(0, 128, 2048)
        qw_u0 = carve(2048, 128, 2048)
        qw_b1 = carve(4096, 128, 2048)
        qw_u1 = carve(6144, 128, 2048)
        ones2k = carve(8192, 128, 2048)

        PS = es.enter_context(nc.psum_tensor("PS", [128, 4096], F32))
        ps = [PS[:, 512 * i:512 * (i + 1)] for i in range(8)]

        for sname in ("dS", "dG", "V", "S", "T") + tuple("ph_" + e for e in ENGINES):
            r.add_sem(sname, es.enter_context(nc.semaphore(sname)))

        E = r.emit

        def VD():
            E("vector", lambda V: V.drain())

        def tinc():
            s = r.sem["T"]
            s[1] += 1
            return s[0], s[1]


        def floor_chain(t_ap, out_ap, K, post=None):
            E("vector", lambda V, t_ap=t_ap: V.scalar_tensor_tensor(
                out=qst_c[:], in0=t_ap, scalar=0.0, in1=t_ap,
                op0=ALU.add, op1=ALU.max))
            VD()
            for j in range(1, K + 1):
                E("vector", lambda V, t_ap=t_ap, j=j: V.scalar_tensor_tensor(
                    out=qst_b[:], in0=qst_c[:], scalar=float(-j + 1),
                    in1=t_ap, op0=ALU.add, op1=ALU.is_gt))
                if j == 1:
                    E("vector", lambda V, t_ap=t_ap, j=j: V.scalar_tensor_tensor(
                        out=qst_u[:], in0=qst_c[:], scalar=float(j),
                        in1=t_ap, op0=ALU.add, op1=ALU.is_le))
                    VD()
                    E("vector", lambda V: V.tensor_tensor(
                        out=qst_u[:], in0=qst_u[:], in1=qst_b[:], op=ALU.subtract))
                else:
                    VD()
                    E("vector", lambda V: V.tensor_tensor(
                        out=qst_u[:], in0=qst_u[:], in1=qst_b[:], op=ALU.subtract))
                    VD()
                    E("vector", lambda V, t_ap=t_ap, j=j: V.scalar_tensor_tensor(
                        out=qst_b[:], in0=qst_c[:], scalar=float(j),
                        in1=t_ap, op0=ALU.add, op1=ALU.is_le))
                    VD()
                    E("vector", lambda V: V.tensor_tensor(
                        out=qst_u[:], in0=qst_u[:], in1=qst_b[:], op=ALU.add))
                VD()
            E("vector", lambda V, out_ap=out_ap: V.tensor_tensor(
                out=out_ap, in0=qst_c[:], in1=qst_u[:], op=ALU.add))

        E("vector", lambda V: V.memset(neg40c[:], -SHIFT))
        E("vector", lambda V: V.memset(halfc[:], 0.5))
        E("vector", lambda V: V.memset(zeroc[:], 0.0))
        E("vector", lambda V: V.memset(epscl[:], EPS))
        E("vector", lambda V: V.memset(onec[:], 1.0))
        E("vector", lambda V: V.memset(onesrow[:], 1.0))
        E("vector", lambda V: V.memset(ones_big, 1.0))
        E("vector", lambda V: V.memset(onesr2k, 1.0))
        E("vector", lambda V: V.memset(ones2k, 1.0))
        v0 = r.inc("vector", "V")
        r.dma("sync", "dS", j_sb[:], jmat_c)
        r.dma("sync", "dS", sel2_sb[:], sel2_c)
        ld0 = r.dma("sync", "dS", rel_sb[:], rel_in)
        r.wait("sync", "V", v0)
        for h in range(N_HEADS):
            r.dma("sync", "dS", vt_dram[80 * h + 64:80 * h + 80, :], onesr2k)

        for c in range(GN // 512):
            gl = r.dma("sync", "dS", gselch, gsel_c[:, 512 * c:512 * (c + 1)])
            r.wait("tensor", "dS", gl)
            if c == 0:
                r.wait("tensor", "dS", ld0)
            E("tensor", lambda T: T.matmul(PS[0:N_HEADS, 0:512], rel_sb[:], gselch,
                                           start=True, stop=True))
            tv = r.inc("tensor", "T")
            r.wait("scalar", "T", tv)
            E("scalar", lambda S: S.activation(out=gstage, in_=PS[0:N_HEADS, 0:512],
                                               func=AF.Identity, bias=zeroc[0:N_HEADS],
                                               scale=1.0))
            sc = r.inc("scalar", "S")
            r.wait("tensor", "S", sc)
            r.wait("sync", "S", sc)
            st_ = r.dma("sync", "dS", g_dram[:, 512 * c:512 * (c + 1)], gstage)
            r.wait("scalar", "dS", st_)
            r.wait("sync", "dS", st_)

        for w_src, w_dst in zip((wq_in, wk_in, wv_in, wo_in), wqkv_dram):
            for dc in range(8):
                sl = slice(128 * dc, 128 * (dc + 1))
                st = wslab[dc % 2]
                s_ = r.sem["dG"]
                s_[1] += 16
                E("gpsimd", lambda G, w_src=w_src, sl=sl, st=st, h=s_[0]:
                  G.dma_start(out=st, in_=w_src[sl, :]).then_inc(h, 16))
                r.wait("sync", "dG", s_[1])
                sv = r.dma("sync", "dS", w_dst[sl, :], st)
                r.wait("gpsimd", "dS", sv)

        wbufs = (wchunk, wchunk2)
        wbg = {0: None, 1: None}
        obufs = ((qw_b0, qw_u0), (qw_b1, qw_u1))
        obg = {0: None, 1: None}
        wq_state = {"g": 0, "o": 0}

        def wquant(w_src, w_dst, nrows, ncols, scol):
            CW = 2048 if ncols % 2048 == 0 else 1024
            RB = 2048 // CW
            nch = nrows // 128
            ncc = ncols // CW
            n_ig = nch // RB

            def loads(buf, ch, cc):
                for rb in range(RB):
                    ld = r.dma("sync", "dS", buf[:, CW * rb:CW * (rb + 1)],
                               w_src[128 * (ch + rb):128 * (ch + rb + 1),
                                     CW * cc:CW * (cc + 1)])
                return ld

            E("vector", lambda V: V.memset(wsum[:], 0.0))
            r.inc("vector", "V")
            for ig in range(n_ig):
                for cc in range(ncc):
                    bi = wq_state["g"] % 2
                    wq_state["g"] += 1
                    buf = wbufs[bi]
                    if wbg[bi]:
                        for sn_, v_ in wbg[bi]:
                            r.wait("sync", sn_, v_)
                    ld = loads(buf, RB * ig, cc)
                    r.wait("vector", "dS", ld)
                    E("vector", lambda V, buf=buf: V.reduce_sum(
                        stc["tmp"][:], buf[:, 0:2048], axis=mybir.AxisListType.X,
                        apply_absolute_value=True))
                    VD()
                    if cc == 0:
                        E("vector", lambda V, ig=ig: V.tensor_tensor(
                            out=wsum[:, ig:ig + 1], in0=stc["tmp"][:], in1=zeroc[:],
                            op=ALU.add))
                    else:
                        E("vector", lambda V, ig=ig: V.tensor_tensor(
                            out=wsum[:, ig:ig + 1], in0=wsum[:, ig:ig + 1],
                            in1=stc["tmp"][:], op=ALU.add))
                    vv = r.inc("vector", "V")
                    wbg[bi] = [("V", vv)]
            E("vector", lambda V: V.drain())
            E("vector", lambda V, n=n_ig: V.reduce_sum(
                wsumt[:], wsum[:, 0:n], axis=mybir.AxisListType.X))
            vv = r.inc("vector", "V")
            r.wait("tensor", "V", vv)
            E("tensor", lambda T: T.matmul(PS[0:1, 0:1], wsumt[:], onec[:],
                                           start=True, stop=True))
            tv = r.inc("tensor", "T")
            r.wait("scalar", "T", tv)
            E("scalar", lambda S, n=nrows * ncols: S.activation(
                out=t11[0:1, 0:1], in_=PS[0:1, 0:1], func=AF.Identity,
                bias=zeroc[0:1], scale=1.0 / n))
            sc = r.inc("scalar", "S")
            r.wait("vector", "S", sc)
            E("vector", lambda V: V.scalar_tensor_tensor(
                out=t11b[0:1, 0:1], in0=t11[0:1, 0:1], scalar=EPS,
                in1=onec[0:1], op0=ALU.max, op1=ALU.mult))
            VD()
            E("vector", lambda V: V.reciprocal(t11[0:1, 0:1], t11b[0:1, 0:1]))
            vv = r.inc("vector", "V")
            r.wait("tensor", "V", vv)
            E("tensor", lambda T: T.matmul(PS[:, 512:513], onesrow[:], t11[0:1, 0:1],
                                           start=True, stop=True))
            tv = r.inc("tensor", "T")
            r.wait("scalar", "T", tv)
            E("scalar", lambda S: S.copy(rscol[:], PS[:, 512:513]))
            sc = r.inc("scalar", "S")
            r.wait("tensor", "S", sc)
            E("tensor", lambda T: T.matmul(PS[:, 512:513], onesrow[:], t11b[0:1, 0:1],
                                           start=True, stop=True))
            tv = r.inc("tensor", "T")
            r.wait("scalar", "T", tv)
            E("scalar", lambda S, scol=scol: S.copy(scol[:], PS[:, 512:513]))
            sc = r.inc("scalar", "S")
            r.wait("vector", "S", sc)
            for ig in range(n_ig):
                for cc in range(ncc):
                    bi = wq_state["g"] % 2
                    wq_state["g"] += 1
                    buf = wbufs[bi]
                    if wbg[bi]:
                        for sn_, v_ in wbg[bi]:
                            r.wait("sync", sn_, v_)
                    ld = loads(buf, RB * ig, cc)
                    r.wait("scalar", "dS", ld)
                    E("scalar", lambda S, buf=buf: S.activation(
                        out=buf[:, 0:2048], in_=buf[:, 0:2048], func=AF.Identity,
                        bias=zeroc[:], scale=rscol[:]))
                    sc = r.inc("scalar", "S")
                    r.wait("vector", "S", sc)
                    oi = wq_state["o"] % 2
                    wq_state["o"] += 1
                    qb, qu = obufs[oi]
                    if obg[oi] is not None:
                        r.wait("vector", "dS", obg[oi])
                    E("vector", lambda V, buf=buf, qb=qb: V.scalar_tensor_tensor(
                        out=qb[:], in0=buf[:, 0:2048], scalar=0.5,
                        in1=ones2k, op0=ALU.is_ge, op1=ALU.mult))
                    E("vector", lambda V, buf=buf, qu=qu: V.scalar_tensor_tensor(
                        out=qu[:], in0=buf[:, 0:2048], scalar=-0.5,
                        in1=ones2k, op0=ALU.is_lt, op1=ALU.mult))
                    VD()
                    E("vector", lambda V, qb=qb, qu=qu: V.tensor_tensor(
                        out=qb[:], in0=qb[:], in1=qu[:], op=ALU.subtract))
                    vv = r.inc("vector", "V")
                    r.wait("sync", "V", vv)
                    for rb in range(RB):
                        sv = r.dma("sync", "dS",
                                   w_dst[128 * (RB * ig + rb):128 * (RB * ig + rb + 1),
                                         CW * cc:CW * (cc + 1)],
                                   qb[:, CW * rb:CW * (rb + 1)])
                    obg[oi] = sv
                    wbg[bi] = [("S", sc), ("V", vv)]

        wquant(w1_in, w1q_dram, HID, D, s1col)
        wquant(w2_in, w2q_dram, D, HID, s2col)
        r.barrier()

        def ln_tile(x_src_ap, dst_dram, tt, quant):
            xt = xtile[tt % 2]
            r.guard("sync", f"xtile{tt % 2}")
            ld = r.dma("sync", "dS", xt[:], x_src_ap)
            r.wait("vector", "dS", ld)
            r.guard("vector", "stc")
            E("vector", lambda V, xt=xt: V.reduce_sum(
                stc["s"][:], xt[:], axis=mybir.AxisListType.X))
            E("vector", lambda V, xt=xt: V.tensor_tensor(
                out=qst_b[:], in0=xt[:], in1=xt[:], op=ALU.mult))
            E("vector", lambda V: V.drain())
            E("vector", lambda V: V.reduce_sum(
                stc["q"][:], qst_b[:], axis=mybir.AxisListType.X))
            VD()
            E("vector", lambda V: V.scalar_tensor_tensor(
                out=stc["mneg"][:], in0=stc["s"][:], scalar=-1.0 / D,
                in1=zeroc[:], op0=ALU.mult, op1=ALU.add))
            E("vector", lambda V: V.scalar_tensor_tensor(
                out=stc["vq"][:], in0=stc["q"][:], scalar=1.0 / D,
                in1=zeroc[:], op0=ALU.mult, op1=ALU.add))
            VD()
            E("vector", lambda V: V.tensor_tensor(
                out=stc["m2"][:], in0=stc["mneg"][:], in1=stc["mneg"][:], op=ALU.mult))
            VD()
            E("vector", lambda V: V.tensor_tensor(
                out=stc["var"][:], in0=stc["vq"][:], in1=stc["m2"][:], op=ALU.subtract))
            VD()
            E("vector", lambda V: V.tensor_tensor(
                out=stc["veps"][:], in0=stc["var"][:], in1=epscl[:], op=ALU.add))
            E("vector", lambda V: V.drain())
            E("vector", lambda V: V.reciprocal(
                stc["rec"][:], stc["veps"][:]))
            vv = r.inc("vector", "V")
            r.wait("scalar", "V", vv)
            E("scalar", lambda S: S.activation(
                out=stc["r"][:], in_=stc["rec"][:], func=AF.Sqrt,
                bias=zeroc[:], scale=1.0))
            sc = r.inc("scalar", "S")
            r.wait("vector", "S", sc)
            VD()
            E("vector", lambda V: V.tensor_tensor(
                out=stc["mr"][:], in0=stc["mneg"][:], in1=stc["r"][:], op=ALU.mult))
            vv = r.inc("vector", "V")
            r.wait("scalar", "V", vv)
            if not quant:
                E("scalar", lambda S, xt=xt: S.activation(
                    out=lnt[:], in_=xt[:], func=AF.Identity,
                    bias=stc["mr"][:], scale=stc["r"][:]))
                sc = r.inc("scalar", "S")
                r.set_guard(f"xtile{tt % 2}", "S", sc)
                r.set_guard("stc", "S", sc)
                r.wait("sync", "S", sc)
                sv = r.dma("sync", "dS", dst_dram[128 * tt:128 * (tt + 1), :], lnt[:])
                r.wait("scalar", "dS", sv)
                return
            E("scalar", lambda S, xt=xt: S.activation(
                out=lntf[:], in_=xt[:], func=AF.Identity,
                bias=stc["mr"][:], scale=stc["r"][:]))
            sc = r.inc("scalar", "S")
            r.set_guard(f"xtile{tt % 2}", "S", sc)
            r.wait("vector", "S", sc)
            E("vector", lambda V: V.reduce_max(
                stc["mcol"][:], lntf[:], axis=mybir.AxisListType.X,
                apply_absolute_value=True))
            VD()
            E("vector", lambda V: V.scalar_tensor_tensor(
                out=stc["mclip"][:], in0=stc["mcol"][:], scalar=EPS,
                in1=onec[:], op0=ALU.max, op1=ALU.mult))
            VD()
            E("vector", lambda V: V.reciprocal(
                stc["invm"][:], stc["mclip"][:]))
            VD()
            E("vector", lambda V: V.scalar_tensor_tensor(
                out=stc["s127"][:], in0=stc["invm"][:], scalar=127.0,
                in1=zeroc[:], op0=ALU.mult, op1=ALU.add))
            E("vector", lambda V: V.tensor_tensor(
                out=stc["tmp2"][:], in0=stc["mclip"][:], in1=s1col[:], op=ALU.mult))
            VD()
            E("vector", lambda V, tt=tt: V.scalar_tensor_tensor(
                out=sxcols[:, tt:tt + 1], in0=stc["tmp2"][:], scalar=1.0 / 127.0,
                in1=zeroc[:], op0=ALU.mult, op1=ALU.add))
            vv = r.inc("vector", "V")
            r.wait("scalar", "V", vv)
            E("scalar", lambda S: S.activation(
                out=qst_t[:], in_=lntf[:], func=AF.Identity,
                bias=halfc[:], scale=stc["s127"][:]))
            sc = r.inc("scalar", "S")
            r.set_guard("stc", "S", sc)
            r.wait("vector", "S", sc)
            floor_chain(qst_t[:], qst_o[:], 2)
            vv = r.inc("vector", "V")
            r.wait("sync", "V", vv)
            sv = r.dma("sync", "dS", dst_dram[128 * tt:128 * (tt + 1), :], qst_o[:])
            r.wait("vector", "dS", sv)
            r.wait("scalar", "dS", sv)

        def batch_program(b):
            def x_rows(tt):
                return x_in[b * S_LEN + 128 * tt: b * S_LEN + 128 * (tt + 1), :]

            for tt in range(NT):
                ln_tile(x_rows(tt), ln1_dram, tt, False)
            r.barrier()

            for dc in range(8):
                for tg in range(4):
                    r.dma("sync", "dS", ln1T[dc][:, 512 * tg:512 * (tg + 1)],
                          ln1_dram[512 * tg:512 * (tg + 1), 128 * dc:128 * (dc + 1)],
                          transpose=True)
            r.wait("sync", "dS")
            ln_ld = r.val("dS")

            r.wait("tensor", "dS", ln_ld)
            for hp in range(8):
                kv0 = 128 * hp
                wl = r.dma("sync", "dS",
                           wslab[hp % 2].rearrange("p (dc c) -> p dc c", c=128),
                           wqkv_dram[2][:, kv0:kv0 + 128].rearrange(
                               "(dc p) c -> p dc c", p=128))
                r.wait("tensor", "dS", wl)
                for tg in range(4):
                    bk = tg % 2
                    h_, tv = tinc()
                    for dc in range(8):
                        if dc < 7:
                            E("tensor", lambda T, hp=hp, tg=tg, dc=dc, bk=bk: T.matmul(
                                ps[bk], wslab[hp % 2][:, 128 * dc:128 * (dc + 1)],
                                ln1T[dc][:, 512 * tg:512 * (tg + 1)],
                                start=(dc == 0), stop=False))
                        else:
                            E("tensor", lambda T, hp=hp, tg=tg, dc=dc, bk=bk, h_=h_:
                              T.matmul(ps[bk], wslab[hp % 2][:, 128 * dc:128 * (dc + 1)],
                                       ln1T[dc][:, 512 * tg:512 * (tg + 1)],
                                       start=False, stop=True).then_inc(h_, 1))
                    r.wait("scalar", "T", tv)
                    st = evst[tg % 2]
                    E("scalar", lambda S, st=st, bk=bk: S.copy(st[:], ps[bk]))
                    sc = r.inc("scalar", "S")
                    r.wait("tensor", "S", sc)
                    r.wait("sync", "S", sc)
                    r.dma("sync", "dS",
                          bass.AP(tensor=vt_dram.tensor,
                                  offset=(160 * hp) * S_LEN + 512 * tg,
                                  ap=[[S_LEN, 64], [1, 512]]),
                          st[0:64, :])
                    sv = r.dma("sync", "dS",
                               bass.AP(tensor=vt_dram.tensor,
                                       offset=(160 * hp + 80) * S_LEN + 512 * tg,
                                       ap=[[S_LEN, 64], [1, 512]]),
                               st[64:128, :])
                    r.wait("scalar", "dS", sv)
                r.wait("sync", "T", r.val("T"))
            r.barrier()

            for tt in range(NT):
                r.dma("sync", "dS", vtok[tt],
                      vt_dram[:, 128 * tt:128 * (tt + 1)], transpose=True)
            vtok_ld = r.val("dS")
            r.wait("tensor", "dS", vtok_ld)

            for hp in range(8):
                kv0 = 128 * hp
                wlq = r.dma("sync", "dS",
                            wslab[0].rearrange("p (dc c) -> p dc c", c=128),
                            wqkv_dram[0][:, kv0:kv0 + 128].rearrange(
                                "(dc p) c -> p dc c", p=128))
                wlk = r.dma("sync", "dS",
                            wslab[1].rearrange("p (dc c) -> p dc c", c=128),
                            wqkv_dram[1][:, kv0:kv0 + 128].rearrange(
                                "(dc p) c -> p dc c", p=128))
                r.wait("tensor", "dS", wlk)
                for wi, dstp in ((0, qTp), (1, kTp)):
                    for tg in range(4):
                        bk = tg % 2
                        h_, tv = tinc()
                        for dc in range(8):
                            if dc < 7:
                                E("tensor", lambda T, wi=wi, tg=tg, dc=dc, bk=bk: T.matmul(
                                    ps[bk], wslab[wi][:, 128 * dc:128 * (dc + 1)],
                                    ln1T[dc][:, 512 * tg:512 * (tg + 1)],
                                    start=(dc == 0), stop=False))
                            else:
                                E("tensor", lambda T, wi=wi, tg=tg, dc=dc, bk=bk, h_=h_:
                                  T.matmul(ps[bk], wslab[wi][:, 128 * dc:128 * (dc + 1)],
                                           ln1T[dc][:, 512 * tg:512 * (tg + 1)],
                                           start=False, stop=True).then_inc(h_, 1))
                        r.wait("scalar", "T", tv)
                        E("scalar", lambda S, dstp=dstp, tg=tg, bk=bk: S.copy(
                            dstp[:, 512 * tg:512 * (tg + 1)], ps[bk]))
                        sc = r.inc("scalar", "S")
                        r.wait("tensor", "S", sc)

                r.wait("sync", "T", r.val("T"))
                for side, mbuf in ((0, m_e), (1, m_o)):
                    h = 2 * hp + side
                    ld = r.dma("sync", "dS", mbuf,
                               bass.AP(tensor=g_dram.tensor, offset=h * GN,
                                       ap=[[1, 128], [1, MU]]))
                r.wait("tensor", "dS", ld)

                sc_hist = {}
                vv_hist = {}
                pv_hist = {}
                pv_guard = None
                dminus_, dplus_ = _sat_range()
                def tclass(kt, qg):
                    dlo = 128 * kt - 512 * qg - 511
                    dhi = 128 * kt - 512 * qg + 127
                    if dhi <= dminus_:
                        return "lo"
                    if dlo >= dplus_:
                        return "hi"
                    return "mid"
                for qg in range(4):
                    q_sl = slice(512 * qg, 512 * (qg + 1))
                    for kt in range(NT + 2):
                        if kt < NT:
                            j = kt % 3
                            if (qg, kt - 3) in sc_hist:
                                r.wait("tensor", "S", sc_hist[(qg, kt - 3)])
                            off = min(max(U0 - (128 * kt - 512 * qg), 0), MU - 512)
                            E("tensor", lambda T, kt=kt, q_sl=q_sl, j=j: T.matmul(
                                ps[2 * j], kTp[0:64, 128 * kt:128 * (kt + 1)],
                                qTp[0:64, q_sl], start=True, stop=False,
                                tile_position=(0, 0)))
                            E("tensor", lambda T, kt=kt, q_sl=q_sl, j=j: T.matmul(
                                ps[2 * j + 1], kTp[64:128, 128 * kt:128 * (kt + 1)],
                                qTp[64:128, q_sl], start=True, stop=False,
                                tile_position=(64, 0)))
                            E("tensor", lambda T, j=j, off=off: T.matmul(
                                ps[2 * j], j_sb[:], m_e[:, off:off + 512],
                                start=False, stop=True))
                            h_, tv_s = tinc()
                            E("tensor", lambda T, j=j, off=off, h_=h_: T.matmul(
                                ps[2 * j + 1], j_sb[:], m_o[:, off:off + 512],
                                start=False, stop=True).then_inc(h_, 1))
                            ewg = ew[kt % 3]
                            r.wait("scalar", "T", tv_s)
                            if (qg, kt - 3) in pv_hist:
                                r.wait("scalar", "T", pv_hist[(qg, kt - 3)])
                            E("scalar", lambda S, ewg=ewg, j=j: S.activation(
                                out=ewg[:, 0:1024],
                                in_=PS[:, 1024 * j:1024 * (j + 1)],
                                func=AF.Exp, bias=neg40c[:], scale=1.0))
                            sc_hist[(qg, kt)] = r.inc("scalar", "S")
                        if kt >= 2:
                            pkt = kt - 2
                            ewp = ew[pkt % 3]
                            if pkt == 0 and pv_guard is not None:
                                r.wait("tensor", "S", pv_guard)
                            r.wait("tensor", "S", sc_hist[(qg, pkt)])
                            E("tensor", lambda T, hp=hp, pkt=pkt, ewp=ewp: T.matmul(
                                PS[0:65, 3072:3584],
                                vtok[pkt][:, 160 * hp:160 * hp + 65],
                                ewp[:, 0:512],
                                start=(pkt == 0), stop=(pkt == NT - 1)))
                            h_, pvv = tinc()
                            E("tensor", lambda T, hp=hp, pkt=pkt, ewp=ewp, h_=h_:
                              T.matmul(PS[0:65, 3584:4096],
                                       vtok[pkt][:, 160 * hp + 80:160 * hp + 145],
                                       ewp[:, 512:1024],
                                       start=(pkt == 0), stop=(pkt == NT - 1))
                              .then_inc(h_, 1))
                            pv_hist[(qg, pkt)] = pvv
                    r.wait("scalar", "T", pv_hist[(qg, NT - 1)])
                    r.wait("scalar", "dG", r.val("dG"))
                    E("scalar", lambda S, q_sl=q_sl: S.copy(oTu[0:64, q_sl],
                                                            PS[0:64, 3072:3584]))
                    E("scalar", lambda S, q_sl=q_sl: S.copy(oTu[64:128, q_sl],
                                                            PS[0:64, 3584:4096]))
                    E("scalar", lambda S: S.copy(denst[64:65, 0:512], PS[64:65, 3072:3584]))
                    E("scalar", lambda S: S.copy(denst[64:65, 512:1024], PS[64:65, 3584:4096]))
                    sc = r.inc("scalar", "S")
                    sc_hist[(qg, NT - 1)] = sc
                    sc_hist[(qg, NT - 2)] = sc
                    sc_hist[(qg, NT - 3)] = sc
                    pv_guard = sc
                    r.wait("gpsimd", "S", sc)
                    if qg == 0:
                        r.wait("gpsimd", "V", r.val("V"))
                    s_ = r.sem["dG"]
                    s_[1] += 32
                    E("gpsimd", lambda G, qg=qg, h=s_[0]: (
                        G.dma_start(out=pairdenP[:, 8 * qg:8 * qg + 4],
                                    in_=denst[64:65, 0:512]).then_inc(h, 16),
                        G.dma_start(out=pairdenP[:, 8 * qg + 4:8 * qg + 8],
                                    in_=denst[64:65, 512:1024]).then_inc(h, 16)))
                r.wait("vector", "dG", r.val("dG"))
                r.wait("vector", "dS")
                E("vector", lambda V: V.drain())
                E("vector", lambda V: V.reciprocal(
                    pairrecP[:], pairdenP[:]))
                vv = r.inc("vector", "V")
                r.wait("sync", "V", vv)
                for qg in range(4):
                    r.dma("sync", "dS", pairrec[0:1, 512 * qg:512 * (qg + 1)],
                          pairrecP[:, 8 * qg:8 * qg + 4])
                    sv_sc = r.dma("sync", "dS",
                                  pairrec[1:2, 512 * qg:512 * (qg + 1)],
                                  pairrecP[:, 8 * qg + 4:8 * qg + 8])
                r.wait("tensor", "dS", sv_sc)
                for qg in range(4):
                    E("tensor", lambda T, qg=qg: T.matmul(
                        ps[6], sel2_sb[:], pairrec[:, 512 * qg:512 * (qg + 1)],
                        start=True, stop=True))
                    tv = r.inc("tensor", "T")
                    r.wait("scalar", "T", tv)
                    E("scalar", lambda S, qg=qg: S.copy(
                        recB[:, 512 * qg:512 * (qg + 1)], ps[6]))
                    sc = r.inc("scalar", "S")
                    r.wait("tensor", "S", sc)
                r.wait("vector", "S", r.val("S"))
                E("vector", lambda V: V.tensor_tensor(
                    out=oTp, in0=oTu, in1=recB[:], op=ALU.mult))
                vv = r.inc("vector", "V")
                r.wait("scalar", "V", vv)
                r.wait("sync", "V", vv)
                sv = r.dma("sync", "dS", ot_dram[128 * hp:128 * (hp + 1), :], oTp)
                r.wait("vector", "dS", sv)
            r.barrier()

            for kvc in range(8):
                r.dma("sync", "dS",
                      woslab[kvc].rearrange("p (dc c) -> p dc c", c=128),
                      wqkv_dram[3][128 * kvc:128 * (kvc + 1), :].rearrange(
                          "p (dc c) -> p dc c", c=128))
            for tg in range(4):
                for kvc in range(8):
                    r.dma("sync", "dS", oTld[kvc],
                          ot_dram[128 * kvc:128 * (kvc + 1), 512 * tg:512 * (tg + 1)])
                ld = r.val("dS")
                r.wait("tensor", "dS", ld)
                for dct in range(8):
                    bk = dct % 2
                    h_, tv = tinc()
                    for kvc in range(8):
                        if kvc < 7:
                            E("tensor", lambda T, dct=dct, kvc=kvc, bk=bk: T.matmul(
                                ps[bk], woslab[kvc][:, 128 * dct:128 * (dct + 1)],
                                oTld[kvc], start=(kvc == 0), stop=False))
                        else:
                            E("tensor", lambda T, dct=dct, kvc=kvc, bk=bk, h_=h_:
                              T.matmul(ps[bk], woslab[kvc][:, 128 * dct:128 * (dct + 1)],
                                       oTld[kvc], start=False, stop=True).then_inc(h_, 1))
                    r.wait("scalar", "T", tv)
                    st = evst[dct % 2]
                    E("scalar", lambda S, st=st, bk=bk: S.copy(st[:], ps[bk]))
                    sc = r.inc("scalar", "S")
                    r.wait("vector", "S", sc)
                    E("vector", lambda V, st=st, bk=bk: V.tensor_tensor(
                        out=qst_o[:, 0:512], in0=ps[bk], in1=st[:],
                        op=ALU.subtract))
                    vv = r.inc("vector", "V")
                    r.wait("tensor", "V", vv)
                    r.wait("sync", "V", vv)
                    r.dma("sync", "dS",
                          yat_dram[128 * dct:128 * (dct + 1),
                                   512 * tg:512 * (tg + 1)], st[:])
                    sv = r.dma("sync", "dS",
                               yat2_dram[128 * dct:128 * (dct + 1),
                                         512 * tg:512 * (tg + 1)], qst_o[:, 0:512])
                    r.wait("scalar", "dS", sv)
                    r.wait("vector", "dS", sv)
                r.wait("sync", "T", r.val("T"))
            r.barrier()

            for tt in range(NT):
                xt, ya = xtile[tt % 2], yatile[tt % 2]
                r.guard("sync", f"xtile{tt % 2}")
                r.guard("sync", f"yatile{tt % 2}")
                r.guard("sync", "lntbuf")
                r.dma("sync", "dS", xt[:], x_rows(tt))
                ld = r.dma("sync", "dS", ya[:],
                           yat_dram[:, 128 * tt:128 * (tt + 1)], transpose=True)
                ld = r.dma("sync", "dS", lnt[:],
                           yat2_dram[:, 128 * tt:128 * (tt + 1)], transpose=True)
                r.wait("vector", "dS", ld)
                E("vector", lambda V, xt=xt, ya=ya: V.tensor_tensor(
                    out=x1t[:], in0=xt[:], in1=ya[:], op=ALU.add))
                VD()
                E("vector", lambda V: V.tensor_tensor(
                    out=x1t[:], in0=x1t[:], in1=lnt[:], op=ALU.add))
                vv = r.inc("vector", "V")
                r.set_guard(f"xtile{tt % 2}", "V", vv)
                r.set_guard(f"yatile{tt % 2}", "V", vv)
                r.set_guard("lntbuf", "V", vv)
                r.wait("sync", "V", vv)
                sv = r.dma("sync", "dS", x1_dram[128 * tt:128 * (tt + 1), :], x1t[:])
                r.wait("vector", "dS", sv)
            r.barrier()

            for tt in range(NT):
                ln_tile(x1_dram[128 * tt:128 * (tt + 1), :], xq_dram, tt, True)
            r.barrier()

            for dc in range(8):
                for hg in range(8):
                    r.dma("sync", "dS", w1qT[dc][:, 512 * hg:512 * (hg + 1)],
                          w1q_dram[512 * hg:512 * (hg + 1), 128 * dc:128 * (dc + 1)],
                          transpose=True)
            for dc in range(8):
                for tg in range(4):
                    r.dma("sync", "dS", xqT[dc][:, 512 * tg:512 * (tg + 1)],
                          xq_dram[512 * tg:512 * (tg + 1), 128 * dc:128 * (dc + 1)],
                          transpose=True)
            mm1_ld = r.val("dS")
            r.wait("tensor", "dS", mm1_ld)
            for tt in range(NT):
                h_, tv = tinc()
                for dc in range(8):
                    for hg in range(8):
                        if dc == 7 and hg == 7:
                            E("tensor", lambda T, tt=tt, dc=dc, hg=hg, h_=h_: T.matmul(
                                ps[hg], xqT[dc][:, 128 * tt:128 * (tt + 1)],
                                w1qT[dc][:, 512 * hg:512 * (hg + 1)],
                                start=False, stop=True).then_inc(h_, 1))
                        else:
                            E("tensor", lambda T, tt=tt, dc=dc, hg=hg: T.matmul(
                                ps[hg], xqT[dc][:, 128 * tt:128 * (tt + 1)],
                                w1qT[dc][:, 512 * hg:512 * (hg + 1)],
                                start=(dc == 0), stop=(dc == 7)))
                r.wait("scalar", "T", tv)
                for hg in range(8):
                    E("scalar", lambda S, tt=tt, hg=hg: S.activation(
                        out=hbuf[:, 512 * hg:512 * (hg + 1)], in_=ps[hg],
                        func=AF.Gelu, bias=zeroc[:], scale=sxcols[:, tt:tt + 1]))
                sc = r.inc("scalar", "S")
                r.wait("tensor", "S", sc)
                r.wait("vector", "S", sc)
                r.guard("vector", "stc")
                E("vector", lambda V: V.reduce_max(
                    stc["mcol"][:], hbuf, axis=mybir.AxisListType.X,
                    apply_absolute_value=True))
                VD()
                E("vector", lambda V: V.scalar_tensor_tensor(
                    out=stc["mclip"][:], in0=stc["mcol"][:], scalar=EPS,
                    in1=onec[:], op0=ALU.max, op1=ALU.mult))
                VD()
                E("vector", lambda V: V.reciprocal(
                    stc["invm"][:], stc["mclip"][:]))
                VD()
                E("vector", lambda V: V.scalar_tensor_tensor(
                    out=stc["s127"][:], in0=stc["invm"][:], scalar=127.0,
                    in1=zeroc[:], op0=ALU.mult, op1=ALU.add))
                E("vector", lambda V: V.tensor_tensor(
                    out=stc["tmp2"][:], in0=stc["mclip"][:], in1=s2col[:], op=ALU.mult))
                VD()
                E("vector", lambda V, tt=tt: V.scalar_tensor_tensor(
                    out=sc2cols[:, tt:tt + 1], in0=stc["tmp2"][:], scalar=1.0 / 127.0,
                    in1=zeroc[:], op0=ALU.mult, op1=ALU.add))
                vv = r.inc("vector", "V")
                r.wait("scalar", "V", vv)
                for cc in range(4):
                    csl = slice(1024 * cc, 1024 * (cc + 1))
                    E("scalar", lambda S, csl=csl: S.activation(
                        out=qst_t[:], in_=hbuf[:, csl], func=AF.Identity,
                        bias=halfc[:], scale=stc["s127"][:]))
                    sc = r.inc("scalar", "S")
                    if csl.stop == HID:
                        r.set_guard("stc", "S", sc)
                    r.wait("vector", "S", sc)
                    floor_chain(qst_t[:], hqbuf[:, csl], 1)
                    vv = r.inc("vector", "V")
                    r.wait("scalar", "V", vv)
                r.wait("sync", "V", r.val("V"))
                sv = r.dma("sync", "dS", hq_dram[128 * tt:128 * (tt + 1), :], hqbuf)
                r.wait("vector", "dS", sv)
                r.wait("scalar", "dS", sv)
            r.barrier()

            for hc in range(32):
                for dg in range(2):
                    r.dma("sync", "dS", w2qT[hc][:, 512 * dg:512 * (dg + 1)],
                          w2q_dram[512 * dg:512 * (dg + 1), 128 * hc:128 * (hc + 1)],
                          transpose=True)
            w2_ld = r.val("dS")
            for tg in range(4):
                for hc in range(32):
                    r.dma("sync", "dS", hqT[hc],
                          hq_dram[512 * tg:512 * (tg + 1), 128 * hc:128 * (hc + 1)],
                          transpose=True)
                slab_ld = r.val("dS")
                r.wait("tensor", "dS", max(w2_ld, slab_ld))
                for ttl in range(4):
                    tt = 4 * tg + ttl
                    for dg in range(2):
                        bk = (2 * ttl + dg) % 2
                        h_, tv = tinc()
                        for hc in range(32):
                            if hc < 31:
                                E("tensor", lambda T, ttl=ttl, dg=dg, hc=hc, bk=bk:
                                  T.matmul(ps[bk],
                                           hqT[hc][:, 128 * ttl:128 * (ttl + 1)],
                                           w2qT[hc][:, 512 * dg:512 * (dg + 1)],
                                           start=(hc == 0), stop=False))
                            else:
                                E("tensor", lambda T, ttl=ttl, dg=dg, hc=hc, bk=bk, h_=h_:
                                  T.matmul(ps[bk],
                                           hqT[hc][:, 128 * ttl:128 * (ttl + 1)],
                                           w2qT[hc][:, 512 * dg:512 * (dg + 1)],
                                           start=False, stop=True).then_inc(h_, 1))
                        xt = xtile[(2 * ttl + dg) % 2]
                        r.guard("sync", f"xtile{(2 * ttl + dg) % 2}")
                        ld = r.dma("sync", "dS", xt[:, 0:512],
                                   x1_dram[128 * tt:128 * (tt + 1),
                                           512 * dg:512 * (dg + 1)])
                        r.wait("vector", "dS", ld)
                        r.wait("vector", "T", tv)
                        E("vector", lambda V, tt=tt, bk=bk, xt=xt:
                          V.scalar_tensor_tensor(
                              out=qst_t[:, 0:512], in0=ps[bk],
                              scalar=sc2cols[:, tt:tt + 1], in1=xt[:, 0:512],
                              op0=ALU.mult, op1=ALU.add))
                        vv = r.inc("vector", "V")
                        r.set_guard(f"xtile{(2 * ttl + dg) % 2}", "V", vv)
                        r.wait("tensor", "V", vv)
                        r.wait("sync", "V", vv)
                        sv = r.dma("sync", "dS",
                                   y_out[b * S_LEN + 128 * tt:
                                         b * S_LEN + 128 * (tt + 1),
                                         512 * dg:512 * (dg + 1)], qst_t[:, 0:512])
                        r.wait("vector", "dS", sv)
                r.wait("sync", "T", r.val("T"))
            r.barrier()

        for b in range(B_CORE):
            batch_program(b)

        r.wait("sync", "dS")
        r.wait("sync", "dG")

        with nc.Block() as block:
            @block.sync
            def _(eng):
                for fn in r.ops["sync"]:
                    fn(eng)

            @block.vector
            def _(eng):
                for fn in r.ops["vector"]:
                    fn(eng)

            @block.scalar
            def _(eng):
                for fn in r.ops["scalar"]:
                    fn(eng)

            @block.tensor
            def _(eng):
                for fn in r.ops["tensor"]:
                    fn(eng)

            @block.gpsimd
            def _(eng):
                for fn in r.ops["gpsimd"]:
                    fn(eng)

    return nc


_NC_CACHE = None


def kernel(L0_attn_inputs, L0_attn_Wq, L0_attn_Wk, L0_attn_Wv, L0_attn_Wo,
           L0_attn_rel_emb, L0_ffn_w1, L0_ffn_w2):
    global _NC_CACHE
    x = np.ascontiguousarray(np.asarray(L0_attn_inputs, dtype=np.float32))
    consts = host_constants()
    if _NC_CACHE is None:
        _NC_CACHE = build_nc()
    nc = _NC_CACHE
    shared = {
        "wq_in": np.ascontiguousarray(L0_attn_Wq, dtype=np.float32),
        "wk_in": np.ascontiguousarray(L0_attn_Wk, dtype=np.float32),
        "wv_in": np.ascontiguousarray(L0_attn_Wv, dtype=np.float32),
        "wo_in": np.ascontiguousarray(L0_attn_Wo, dtype=np.float32),
        "rel_in": np.ascontiguousarray(L0_attn_rel_emb, dtype=np.float32),
        "w1_in": np.ascontiguousarray(L0_ffn_w1, dtype=np.float32),
        "w2_in": np.ascontiguousarray(L0_ffn_w2, dtype=np.float32),
        **consts,
    }
    in_maps = []
    for c in range(N_CORES):
        xi = x[B_CORE * c: B_CORE * (c + 1)].reshape(TOK, D)
        in_maps.append({"x_in": np.ascontiguousarray(xi), **shared})
    res = run_bass_kernel_spmd(nc, in_maps, core_ids=list(range(N_CORES)))
    out = np.concatenate(
        [np.asarray(r["y_out"]).reshape(B_CORE, S_LEN, D) for r in res.results],
        axis=0)
    return out.astype(np.float32)


# version 25 (final)
# speedup vs baseline: 1.6467x; 1.0156x over previous
import numpy as np
import ml_dtypes
import concourse.bass as bass
import concourse.mybir as mybir
from concourse.bass_utils import run_bass_kernel_spmd

F32 = mybir.dt.float32
BF16 = mybir.dt.bfloat16
I32 = mybir.dt.int32
AF = mybir.ActivationFunctionType
ALU = mybir.AluOpType

B_TOTAL, S_LEN, D = 16, 2048, 1024
N_HEADS, D_KV = 16, 64
HID = 4096
N_CORES = 8
B_CORE = B_TOTAL // N_CORES
TOK = B_CORE * S_LEN
NT = S_LEN // 128
EPS = 1e-5
SHIFT = 40.0
NUM_BUCKETS, MAX_DIST = 32, 128
U0 = 1920
MU = 3968
GN = 4096

ENGINES = ("sync", "vector", "scalar", "tensor", "gpsimd")


def _bucket_table():
    d = np.arange(-2048, 2048)
    n = -d
    ret = np.where(n < 0, 16, 0).astype(np.int64)
    n = np.abs(n)
    is_small = n < 8
    nf = np.maximum(n, 1).astype(np.float32)
    t = np.log(nf / np.float32(8.0)).astype(np.float32)
    t = (t / np.float32(np.log(MAX_DIST / 8.0))).astype(np.float32)
    t = (t * np.float32(8.0)).astype(np.float32)
    val = 8 + t.astype(np.int32)
    val = np.minimum(val, 15)
    return (ret + np.where(is_small, n, val)).astype(np.int64)


def _sat_range():
    bt = _bucket_table()
    lo_val = bt[0]
    d = -2048
    while bt[d + 1 + 2048] == lo_val:
        d += 1
    dminus = d
    hi_val = bt[2047 + 2048]
    d = 2047
    while bt[d - 1 + 2048] == hi_val:
        d -= 1
    dplus = d
    return dminus, dplus


def host_constants():
    bt = _bucket_table()
    gsel = np.zeros((NUM_BUCKETS, GN), dtype=np.float32)
    i = np.arange(GN)
    dd = 2047 - i
    gsel[bt[dd + 2048], i] = 1.0
    jmat = np.zeros((128, 128), dtype=ml_dtypes.bfloat16)
    jmat[np.arange(128), 127 - np.arange(128)] = 1.0
    sel2 = np.zeros((2, 128), dtype=np.float32)
    sel2[0, 0:64] = 1.0
    sel2[1, 64:128] = 1.0
    eblo = np.zeros((NUM_BUCKETS, 1), dtype=np.float32)
    eblo[bt[0], 0] = 1.0
    return {"gsel_c": gsel, "jmat_c": jmat, "sel2_c": sel2, "eblo_c": eblo}


class Rec:
    def __init__(self):
        self.ops = {e: [] for e in ENGINES}
        self.sem = {}

    def add_sem(self, name, handle):
        self.sem[name] = [handle, 0]

    def emit(self, eng, fn):
        self.ops[eng].append(fn)

    def inc(self, eng, sname):
        s = self.sem[sname]
        s[1] += 1
        h = s[0]
        if eng in ("vector", "scalar", "tensor"):
            self.emit(eng, lambda E, h=h: (E.drain(), E.nop().then_inc(h, 1)))
        else:
            self.emit(eng, lambda E, h=h: E.nop().then_inc(h, 1))
        return s[1]

    def dma(self, eng, sname, out, in_, transpose=False):
        s = self.sem[sname]
        s[1] += 16
        h = s[0]
        if transpose:
            self.emit(eng, lambda E, h=h, out=out, in_=in_:
                      E.dma_start_transpose(out, in_).then_inc(h, 16))
        else:
            self.emit(eng, lambda E, h=h, out=out, in_=in_:
                      E.dma_start(out=out, in_=in_).then_inc(h, 16))
        return s[1]

    def wait(self, eng, sname, val=None):
        s = self.sem[sname]
        v = s[1] if val is None else val
        if sname in ("dS", "dG"):
            v = s[1]
            issuer = "sync" if sname == "dS" else "gpsimd"
            if not hasattr(self, "_sw"):
                self._sw = {}
            if self._sw.get(sname, -1) < v:
                self._sw[sname] = v
                if eng != issuer:
                    self.emit(issuer, lambda E, h=s[0], v=v: E.wait_ge(h, v))
        self.emit(eng, lambda E, h=s[0], v=v: E.wait_ge(h, v))

    def val(self, sname):
        return self.sem[sname][1]

    def guard(self, eng, key):
        if key in getattr(self, "bguard", {}):
            sname, v = self.bguard[key]
            self.wait(eng, sname, v)

    def set_guard(self, key, sname, v):
        if not hasattr(self, "bguard"):
            self.bguard = {}
        self.bguard[key] = (sname, v)

    def barrier(self):
        self.wait("sync", "dS")
        self.wait("gpsimd", "dG")
        for e in ENGINES:
            self.inc(e, "ph_" + e)
        for e in ENGINES:
            for o in ENGINES:
                if o != e:
                    self.wait(e, "ph_" + o)


def build_nc(debug=False):
    nc = bass.Bass("TRN2", target_bir_lowering=False, debug=False)
    r = Rec()

    x_in = nc.dram_tensor("x_in", [TOK, D], F32, kind="ExternalInput").ap()
    wq_in = nc.dram_tensor("wq_in", [D, D], F32, kind="ExternalInput").ap()
    wk_in = nc.dram_tensor("wk_in", [D, D], F32, kind="ExternalInput").ap()
    wv_in = nc.dram_tensor("wv_in", [D, D], F32, kind="ExternalInput").ap()
    wo_in = nc.dram_tensor("wo_in", [D, D], F32, kind="ExternalInput").ap()
    rel_in = nc.dram_tensor("rel_in", [NUM_BUCKETS, N_HEADS], F32, kind="ExternalInput").ap()
    w1_in = nc.dram_tensor("w1_in", [HID, D], F32, kind="ExternalInput").ap()
    w2_in = nc.dram_tensor("w2_in", [D, HID], F32, kind="ExternalInput").ap()
    gsel_c = nc.dram_tensor("gsel_c", [NUM_BUCKETS, GN], F32, kind="ExternalInput").ap()
    jmat_c = nc.dram_tensor("jmat_c", [128, 128], BF16, kind="ExternalInput").ap()
    sel2_c = nc.dram_tensor("sel2_c", [2, 128], F32, kind="ExternalInput").ap()
    eblo_c = nc.dram_tensor("eblo_c", [NUM_BUCKETS, 1], F32, kind="ExternalInput").ap()
    y_out = nc.dram_tensor("y_out", [TOK, D], F32, kind="ExternalOutput").ap()

    KD = "ExternalOutput" if debug else "Internal"
    g_dram = nc.dram_tensor("g_dram", [N_HEADS, GN], BF16, kind=KD).ap()
    wqkv_dram = [nc.dram_tensor(f"w{n}_bf", [D, D], BF16, kind=KD).ap()
                 for n in ("q", "k", "v", "o")]
    w1q_dram = nc.dram_tensor("w1q_dram", [HID, D], BF16, kind=KD).ap()
    w2q_dram = nc.dram_tensor("w2q_dram", [D, HID], BF16, kind=KD).ap()
    ln1_dram = nc.dram_tensor("ln1_dram", [S_LEN, D], BF16, kind=KD).ap()
    vt_dram = nc.dram_tensor("vt_dram", [N_HEADS * 80, S_LEN], BF16, kind=KD).ap()
    ot_dram = nc.dram_tensor("ot_dram", [D, S_LEN], BF16, kind=KD).ap()
    yat_dram = nc.dram_tensor("yat_dram", [D, S_LEN], BF16, kind=KD).ap()
    yat2_dram = nc.dram_tensor("yat2_dram", [D, S_LEN], BF16, kind=KD).ap()
    x1_dram = nc.dram_tensor("x1_dram", [S_LEN, D], F32, kind=KD).ap()
    xq_dram = nc.dram_tensor("xq_dram", [S_LEN, D], BF16, kind=KD).ap()
    hq_dram = nc.dram_tensor("hq_dram", [S_LEN, HID], BF16, kind=KD).ap()

    import contextlib
    es = contextlib.ExitStack()
    with es:
        def sb(name, shape, dt):
            return es.enter_context(nc.sbuf_tensor(name, shape, dt))

        neg40c = sb("neg40c", [128, 1], F32)
        halfc = sb("halfc", [128, 1], F32)
        zeroc = sb("zeroc", [128, 1], F32)
        epscl = sb("epscl", [128, 1], F32)
        onec = sb("onec", [128, 1], F32)
        onesrow = sb("onesrow", [1, 128], F32)
        j_sb = sb("j_sb", [128, 128], BF16)
        sel2_sb = sb("sel2_sb", [2, 128], F32)
        rel_sb = sb("rel_sb", [NUM_BUCKETS, N_HEADS], F32)
        s1col = sb("s1col", [128, 1], F32)
        s2col = sb("s2col", [128, 1], F32)
        rscol = sb("rscol", [128, 1], F32)
        sxcols = sb("sxcols", [128, NT], F32)
        sc2cols = sb("sc2cols", [128, NT], F32)
        relrow = sb("relrow", [1, 2 * N_HEADS], F32)
        bcol_sb = sb("bcol_sb", [128, 2 * N_HEADS], F32)
        nrel15 = sb("nrel15", [128, 1], F32)
        eblo_sb = sb("eblo_sb", [NUM_BUCKETS, 1], F32)
        t11 = sb("t11", [1, 8], F32)
        t11b = sb("t11b", [1, 8], F32)
        t11c = sb("t11c", [1, 8], F32)
        stc = {n: sb("stc_" + n, [128, 1], F32) for n in
               ("s", "q", "mneg", "vq", "m2", "var", "veps", "rec", "r", "mr",
                "mcol", "mclip", "invm", "s127", "tmp", "tmp2")}
        qst_t = sb("qst_t", [128, 1024], F32)
        qst_c = sb("qst_c", [128, 1024], I32)
        qst_b = sb("qst_b", [128, 1024], F32)
        qst_o = sb("qst_o", [128, 1024], BF16)
        qst_u = sb("qst_u", [128, 1024], F32)
        xtile = [sb(f"xtile{i}", [128, D], F32) for i in range(2)]
        yatile = [sb(f"yatile{i}", [128, D], BF16) for i in range(2)]
        x1t = sb("x1t", [128, D], F32)
        lnt = sb("lnt", [128, D], BF16)
        lntf = sb("lntf", [128, D], F32)
        evst = [sb(f"evst{i}", [128, 512], BF16) for i in range(2)]
        wsum = sb("wsum", [128, 32], F32)
        wsumt = sb("wsumt", [128, 1], F32)
        AR_N = 61440
        AR = sb("AR", [128, AR_N], BF16)
        ARF_N = 9216
        ARF = sb("ARF", [128, ARF_N], F32)

        def carve(off, p, n, dt=BF16):
            if dt == BF16:
                return AR[0:p, off:off + n]
            return ARF[0:p, off:off + n]

        ln1T = [carve(2048 * i, 128, 2048) for i in range(8)]
        vtok = [carve(16384 + 1280 * i, 128, 1280) for i in range(16)]
        m_e = carve(36864, 128, MU)
        m_o = carve(40832, 128, MU)
        qTp = carve(44800, 128, 2048)
        kTp = carve(46848, 128, 2048)
        oTp = carve(48896, 128, 2048)
        ebuf = [carve(50944 + 512 * i, 128, 512) for i in range(4)]
        wslab = [carve(52992 + 1024 * i, 128, 1024) for i in range(2)]
        mpp = carve(50944, 128, MU)
        ew = [carve(50944, 128, 1024), carve(52992, 128, 1024),
              carve(56576, 128, 1024)]
        gstage = carve(55040, N_HEADS, 512)
        ones_big = carve(55552, 128, 1024)
        onesr2k = carve(56576, 16, 2048)
        w1qT = [carve(4096 * i, 128, 4096) for i in range(8)]
        xqT = [carve(36864 + 2048 * i, 128, 2048) for i in range(8)]
        hbuf = carve(53248, 128, 4096)
        hqbuf = carve(57344, 128, 4096)
        w2qT = [carve(1024 * i, 128, 1024) for i in range(32)]
        hqT = [carve(36864 + 512 * i, 128, 512) for i in range(32)]
        woslab = [carve(36864 + 1024 * i, 128, 1024) for i in range(8)]
        oTld = [carve(45056 + 512 * i, 128, 512) for i in range(8)]
        oTu = carve(0, 128, 2048, F32)
        recB = carve(2048, 128, 2048, F32)
        denst = carve(4096, 128, 1024, F32)
        pairdenP = carve(5120, 128, 32, F32)
        pairrecP = carve(5184, 128, 32, F32)
        pairrec = carve(7168, 2, 2048, F32)
        gselch = carve(0, NUM_BUCKETS, 512, F32)
        wchunk = carve(512, 128, 2048, F32)
        wchunk2 = carve(4096, 128, 2048, F32)
        qw_b0 = carve(0, 128, 2048)
        qw_u0 = carve(2048, 128, 2048)
        qw_b1 = carve(4096, 128, 2048)
        qw_u1 = carve(6144, 128, 2048)
        ones2k = carve(8192, 128, 2048)

        PS = es.enter_context(nc.psum_tensor("PS", [128, 4096], F32))
        ps = [PS[:, 512 * i:512 * (i + 1)] for i in range(8)]

        for sname in ("dS", "dG", "V", "S", "T") + tuple("ph_" + e for e in ENGINES):
            r.add_sem(sname, es.enter_context(nc.semaphore(sname)))

        E = r.emit

        def VD():
            E("vector", lambda V: V.drain())

        def tinc():
            s = r.sem["T"]
            s[1] += 1
            return s[0], s[1]


        def floor_chain(t_ap, out_ap, K, post=None):
            E("vector", lambda V, t_ap=t_ap: V.scalar_tensor_tensor(
                out=qst_c[:], in0=t_ap, scalar=0.0, in1=t_ap,
                op0=ALU.add, op1=ALU.max))
            VD()
            for j in range(1, K + 1):
                E("vector", lambda V, t_ap=t_ap, j=j: V.scalar_tensor_tensor(
                    out=qst_b[:], in0=qst_c[:], scalar=float(-j + 1),
                    in1=t_ap, op0=ALU.add, op1=ALU.is_gt))
                if j == 1:
                    E("vector", lambda V, t_ap=t_ap, j=j: V.scalar_tensor_tensor(
                        out=qst_u[:], in0=qst_c[:], scalar=float(j),
                        in1=t_ap, op0=ALU.add, op1=ALU.is_le))
                    VD()
                    E("vector", lambda V: V.tensor_tensor(
                        out=qst_u[:], in0=qst_u[:], in1=qst_b[:], op=ALU.subtract))
                else:
                    VD()
                    E("vector", lambda V: V.tensor_tensor(
                        out=qst_u[:], in0=qst_u[:], in1=qst_b[:], op=ALU.subtract))
                    VD()
                    E("vector", lambda V, t_ap=t_ap, j=j: V.scalar_tensor_tensor(
                        out=qst_b[:], in0=qst_c[:], scalar=float(j),
                        in1=t_ap, op0=ALU.add, op1=ALU.is_le))
                    VD()
                    E("vector", lambda V: V.tensor_tensor(
                        out=qst_u[:], in0=qst_u[:], in1=qst_b[:], op=ALU.add))
                VD()
            E("vector", lambda V, out_ap=out_ap: V.tensor_tensor(
                out=out_ap, in0=qst_c[:], in1=qst_u[:], op=ALU.add))

        E("vector", lambda V: V.memset(neg40c[:], -SHIFT))
        E("vector", lambda V: V.memset(halfc[:], 0.5))
        E("vector", lambda V: V.memset(zeroc[:], 0.0))
        E("vector", lambda V: V.memset(epscl[:], EPS))
        E("vector", lambda V: V.memset(onec[:], 1.0))
        E("vector", lambda V: V.memset(onesrow[:], 1.0))
        E("vector", lambda V: V.memset(ones_big, 1.0))
        E("vector", lambda V: V.memset(onesr2k, 1.0))
        E("vector", lambda V: V.memset(ones2k, 1.0))
        v0 = r.inc("vector", "V")
        r.dma("sync", "dS", j_sb[:], jmat_c)
        r.dma("sync", "dS", sel2_sb[:], sel2_c)
        ld0 = r.dma("sync", "dS", rel_sb[:], rel_in)
        r.wait("sync", "V", v0)
        for h in range(N_HEADS):
            r.dma("sync", "dS", vt_dram[80 * h + 64:80 * h + 80, :], onesr2k)

        for c in range(GN // 512):
            gl = r.dma("sync", "dS", gselch, gsel_c[:, 512 * c:512 * (c + 1)])
            r.wait("tensor", "dS", gl)
            if c == 0:
                r.wait("tensor", "dS", ld0)
            E("tensor", lambda T: T.matmul(PS[0:N_HEADS, 0:512], rel_sb[:], gselch,
                                           start=True, stop=True))
            tv = r.inc("tensor", "T")
            r.wait("scalar", "T", tv)
            E("scalar", lambda S: S.activation(out=gstage, in_=PS[0:N_HEADS, 0:512],
                                               func=AF.Identity, bias=zeroc[0:N_HEADS],
                                               scale=1.0))
            sc = r.inc("scalar", "S")
            r.wait("tensor", "S", sc)
            r.wait("sync", "S", sc)
            st_ = r.dma("sync", "dS", g_dram[:, 512 * c:512 * (c + 1)], gstage)
            r.wait("scalar", "dS", st_)
            r.wait("sync", "dS", st_)

        for w_src, w_dst in zip((wq_in, wk_in, wv_in, wo_in), wqkv_dram):
            for dc in range(8):
                sl = slice(128 * dc, 128 * (dc + 1))
                st = wslab[dc % 2]
                s_ = r.sem["dG"]
                s_[1] += 16
                E("gpsimd", lambda G, w_src=w_src, sl=sl, st=st, h=s_[0]:
                  G.dma_start(out=st, in_=w_src[sl, :]).then_inc(h, 16))
                r.wait("sync", "dG", s_[1])
                sv = r.dma("sync", "dS", w_dst[sl, :], st)
                r.wait("gpsimd", "dS", sv)

        wbufs = (wchunk, wchunk2)
        wbg = {0: None, 1: None}
        obufs = ((qw_b0, qw_u0), (qw_b1, qw_u1))
        obg = {0: None, 1: None}
        wq_state = {"g": 0, "o": 0}

        def wquant(w_src, w_dst, nrows, ncols, scol):
            CW = 2048 if ncols % 2048 == 0 else 1024
            RB = 2048 // CW
            nch = nrows // 128
            ncc = ncols // CW
            n_ig = nch // RB

            def loads(buf, ch, cc):
                for rb in range(RB):
                    ld = r.dma("sync", "dS", buf[:, CW * rb:CW * (rb + 1)],
                               w_src[128 * (ch + rb):128 * (ch + rb + 1),
                                     CW * cc:CW * (cc + 1)])
                return ld

            E("vector", lambda V: V.memset(wsum[:], 0.0))
            r.inc("vector", "V")
            for ig in range(n_ig):
                for cc in range(ncc):
                    bi = wq_state["g"] % 2
                    wq_state["g"] += 1
                    buf = wbufs[bi]
                    if wbg[bi]:
                        for sn_, v_ in wbg[bi]:
                            r.wait("sync", sn_, v_)
                    ld = loads(buf, RB * ig, cc)
                    r.wait("vector", "dS", ld)
                    E("vector", lambda V, buf=buf: V.reduce_sum(
                        stc["tmp"][:], buf[:, 0:2048], axis=mybir.AxisListType.X,
                        apply_absolute_value=True))
                    VD()
                    if cc == 0:
                        E("vector", lambda V, ig=ig: V.tensor_tensor(
                            out=wsum[:, ig:ig + 1], in0=stc["tmp"][:], in1=zeroc[:],
                            op=ALU.add))
                    else:
                        E("vector", lambda V, ig=ig: V.tensor_tensor(
                            out=wsum[:, ig:ig + 1], in0=wsum[:, ig:ig + 1],
                            in1=stc["tmp"][:], op=ALU.add))
                    vv = r.inc("vector", "V")
                    wbg[bi] = [("V", vv)]
            E("vector", lambda V: V.drain())
            E("vector", lambda V, n=n_ig: V.reduce_sum(
                wsumt[:], wsum[:, 0:n], axis=mybir.AxisListType.X))
            vv = r.inc("vector", "V")
            r.wait("tensor", "V", vv)
            E("tensor", lambda T: T.matmul(PS[0:1, 0:1], wsumt[:], onec[:],
                                           start=True, stop=True))
            tv = r.inc("tensor", "T")
            r.wait("scalar", "T", tv)
            E("scalar", lambda S, n=nrows * ncols: S.activation(
                out=t11[0:1, 0:1], in_=PS[0:1, 0:1], func=AF.Identity,
                bias=zeroc[0:1], scale=1.0 / n))
            sc = r.inc("scalar", "S")
            r.wait("vector", "S", sc)
            E("vector", lambda V: V.scalar_tensor_tensor(
                out=t11b[0:1, 0:1], in0=t11[0:1, 0:1], scalar=EPS,
                in1=onec[0:1], op0=ALU.max, op1=ALU.mult))
            VD()
            E("vector", lambda V: V.reciprocal(t11[0:1, 0:1], t11b[0:1, 0:1]))
            vv = r.inc("vector", "V")
            r.wait("tensor", "V", vv)
            E("tensor", lambda T: T.matmul(PS[:, 512:513], onesrow[:], t11[0:1, 0:1],
                                           start=True, stop=True))
            tv = r.inc("tensor", "T")
            r.wait("scalar", "T", tv)
            E("scalar", lambda S: S.copy(rscol[:], PS[:, 512:513]))
            sc = r.inc("scalar", "S")
            r.wait("tensor", "S", sc)
            E("tensor", lambda T: T.matmul(PS[:, 512:513], onesrow[:], t11b[0:1, 0:1],
                                           start=True, stop=True))
            tv = r.inc("tensor", "T")
            r.wait("scalar", "T", tv)
            E("scalar", lambda S, scol=scol: S.copy(scol[:], PS[:, 512:513]))
            sc = r.inc("scalar", "S")
            r.wait("vector", "S", sc)
            for ig in range(n_ig):
                for cc in range(ncc):
                    bi = wq_state["g"] % 2
                    wq_state["g"] += 1
                    buf = wbufs[bi]
                    if wbg[bi]:
                        for sn_, v_ in wbg[bi]:
                            r.wait("sync", sn_, v_)
                    ld = loads(buf, RB * ig, cc)
                    r.wait("scalar", "dS", ld)
                    E("scalar", lambda S, buf=buf: S.activation(
                        out=buf[:, 0:2048], in_=buf[:, 0:2048], func=AF.Identity,
                        bias=zeroc[:], scale=rscol[:]))
                    sc = r.inc("scalar", "S")
                    r.wait("vector", "S", sc)
                    oi = wq_state["o"] % 2
                    wq_state["o"] += 1
                    qb, qu = obufs[oi]
                    if obg[oi] is not None:
                        r.wait("vector", "dS", obg[oi])
                    E("vector", lambda V, buf=buf, qb=qb: V.scalar_tensor_tensor(
                        out=qb[:], in0=buf[:, 0:2048], scalar=0.5,
                        in1=ones2k, op0=ALU.is_ge, op1=ALU.mult))
                    E("vector", lambda V, buf=buf, qu=qu: V.scalar_tensor_tensor(
                        out=qu[:], in0=buf[:, 0:2048], scalar=-0.5,
                        in1=ones2k, op0=ALU.is_lt, op1=ALU.mult))
                    VD()
                    E("vector", lambda V, qb=qb, qu=qu: V.tensor_tensor(
                        out=qb[:], in0=qb[:], in1=qu[:], op=ALU.subtract))
                    vv = r.inc("vector", "V")
                    r.wait("sync", "V", vv)
                    for rb in range(RB):
                        sv = r.dma("sync", "dS",
                                   w_dst[128 * (RB * ig + rb):128 * (RB * ig + rb + 1),
                                         CW * cc:CW * (cc + 1)],
                                   qb[:, CW * rb:CW * (rb + 1)])
                    obg[oi] = sv
                    wbg[bi] = [("S", sc), ("V", vv)]

        wquant(w1_in, w1q_dram, HID, D, s1col)
        wquant(w2_in, w2q_dram, D, HID, s2col)
        r.barrier()

        def ln_tile(x_src_ap, dst_dram, tt, quant):
            xt = xtile[tt % 2]
            r.guard("sync", f"xtile{tt % 2}")
            ld = r.dma("sync", "dS", xt[:], x_src_ap)
            r.wait("vector", "dS", ld)
            r.guard("vector", "stc")
            E("vector", lambda V, xt=xt: V.reduce_sum(
                stc["s"][:], xt[:], axis=mybir.AxisListType.X))
            E("vector", lambda V, xt=xt: V.tensor_tensor(
                out=qst_b[:], in0=xt[:], in1=xt[:], op=ALU.mult))
            E("vector", lambda V: V.drain())
            E("vector", lambda V: V.reduce_sum(
                stc["q"][:], qst_b[:], axis=mybir.AxisListType.X))
            VD()
            E("vector", lambda V: V.scalar_tensor_tensor(
                out=stc["mneg"][:], in0=stc["s"][:], scalar=-1.0 / D,
                in1=zeroc[:], op0=ALU.mult, op1=ALU.add))
            E("vector", lambda V: V.scalar_tensor_tensor(
                out=stc["vq"][:], in0=stc["q"][:], scalar=1.0 / D,
                in1=zeroc[:], op0=ALU.mult, op1=ALU.add))
            VD()
            E("vector", lambda V: V.tensor_tensor(
                out=stc["m2"][:], in0=stc["mneg"][:], in1=stc["mneg"][:], op=ALU.mult))
            VD()
            E("vector", lambda V: V.tensor_tensor(
                out=stc["var"][:], in0=stc["vq"][:], in1=stc["m2"][:], op=ALU.subtract))
            VD()
            E("vector", lambda V: V.tensor_tensor(
                out=stc["veps"][:], in0=stc["var"][:], in1=epscl[:], op=ALU.add))
            E("vector", lambda V: V.drain())
            E("vector", lambda V: V.reciprocal(
                stc["rec"][:], stc["veps"][:]))
            vv = r.inc("vector", "V")
            r.wait("scalar", "V", vv)
            E("scalar", lambda S: S.activation(
                out=stc["r"][:], in_=stc["rec"][:], func=AF.Sqrt,
                bias=zeroc[:], scale=1.0))
            sc = r.inc("scalar", "S")
            r.wait("vector", "S", sc)
            VD()
            E("vector", lambda V: V.tensor_tensor(
                out=stc["mr"][:], in0=stc["mneg"][:], in1=stc["r"][:], op=ALU.mult))
            vv = r.inc("vector", "V")
            r.wait("scalar", "V", vv)
            if not quant:
                E("scalar", lambda S, xt=xt: S.activation(
                    out=lnt[:], in_=xt[:], func=AF.Identity,
                    bias=stc["mr"][:], scale=stc["r"][:]))
                sc = r.inc("scalar", "S")
                r.set_guard(f"xtile{tt % 2}", "S", sc)
                r.set_guard("stc", "S", sc)
                r.wait("sync", "S", sc)
                sv = r.dma("sync", "dS", dst_dram[128 * tt:128 * (tt + 1), :], lnt[:])
                r.wait("scalar", "dS", sv)
                return
            E("scalar", lambda S, xt=xt: S.activation(
                out=lntf[:], in_=xt[:], func=AF.Identity,
                bias=stc["mr"][:], scale=stc["r"][:]))
            sc = r.inc("scalar", "S")
            r.set_guard(f"xtile{tt % 2}", "S", sc)
            r.wait("vector", "S", sc)
            E("vector", lambda V: V.reduce_max(
                stc["mcol"][:], lntf[:], axis=mybir.AxisListType.X,
                apply_absolute_value=True))
            VD()
            E("vector", lambda V: V.scalar_tensor_tensor(
                out=stc["mclip"][:], in0=stc["mcol"][:], scalar=EPS,
                in1=onec[:], op0=ALU.max, op1=ALU.mult))
            VD()
            E("vector", lambda V: V.reciprocal(
                stc["invm"][:], stc["mclip"][:]))
            VD()
            E("vector", lambda V: V.scalar_tensor_tensor(
                out=stc["s127"][:], in0=stc["invm"][:], scalar=127.0,
                in1=zeroc[:], op0=ALU.mult, op1=ALU.add))
            E("vector", lambda V: V.tensor_tensor(
                out=stc["tmp2"][:], in0=stc["mclip"][:], in1=s1col[:], op=ALU.mult))
            VD()
            E("vector", lambda V, tt=tt: V.scalar_tensor_tensor(
                out=sxcols[:, tt:tt + 1], in0=stc["tmp2"][:], scalar=1.0 / 127.0,
                in1=zeroc[:], op0=ALU.mult, op1=ALU.add))
            vv = r.inc("vector", "V")
            r.wait("scalar", "V", vv)
            E("scalar", lambda S: S.activation(
                out=qst_t[:], in_=lntf[:], func=AF.Identity,
                bias=halfc[:], scale=stc["s127"][:]))
            sc = r.inc("scalar", "S")
            r.set_guard("stc", "S", sc)
            r.wait("vector", "S", sc)
            floor_chain(qst_t[:], qst_o[:], 2)
            vv = r.inc("vector", "V")
            r.wait("sync", "V", vv)
            sv = r.dma("sync", "dS", dst_dram[128 * tt:128 * (tt + 1), :], qst_o[:])
            r.wait("vector", "dS", sv)
            r.wait("scalar", "dS", sv)

        def batch_program(b):
            def x_rows(tt):
                return x_in[b * S_LEN + 128 * tt: b * S_LEN + 128 * (tt + 1), :]

            for tt in range(NT):
                ln_tile(x_rows(tt), ln1_dram, tt, False)
            r.barrier()

            for dc in range(8):
                for tg in range(4):
                    r.dma("sync", "dS", ln1T[dc][:, 512 * tg:512 * (tg + 1)],
                          ln1_dram[512 * tg:512 * (tg + 1), 128 * dc:128 * (dc + 1)],
                          transpose=True)
            r.wait("sync", "dS")
            ln_ld = r.val("dS")

            r.wait("tensor", "dS", ln_ld)
            for hp in range(8):
                kv0 = 128 * hp
                wl = r.dma("sync", "dS",
                           wslab[hp % 2].rearrange("p (dc c) -> p dc c", c=128),
                           wqkv_dram[2][:, kv0:kv0 + 128].rearrange(
                               "(dc p) c -> p dc c", p=128))
                r.wait("tensor", "dS", wl)
                for tg in range(4):
                    bk = tg % 2
                    h_, tv = tinc()
                    for dc in range(8):
                        if dc < 7:
                            E("tensor", lambda T, hp=hp, tg=tg, dc=dc, bk=bk: T.matmul(
                                ps[bk], wslab[hp % 2][:, 128 * dc:128 * (dc + 1)],
                                ln1T[dc][:, 512 * tg:512 * (tg + 1)],
                                start=(dc == 0), stop=False))
                        else:
                            E("tensor", lambda T, hp=hp, tg=tg, dc=dc, bk=bk, h_=h_:
                              T.matmul(ps[bk], wslab[hp % 2][:, 128 * dc:128 * (dc + 1)],
                                       ln1T[dc][:, 512 * tg:512 * (tg + 1)],
                                       start=False, stop=True).then_inc(h_, 1))
                    r.wait("scalar", "T", tv)
                    st = evst[tg % 2]
                    E("scalar", lambda S, st=st, bk=bk: S.copy(st[:], ps[bk]))
                    sc = r.inc("scalar", "S")
                    r.wait("tensor", "S", sc)
                    r.wait("sync", "S", sc)
                    r.dma("sync", "dS",
                          bass.AP(tensor=vt_dram.tensor,
                                  offset=(160 * hp) * S_LEN + 512 * tg,
                                  ap=[[S_LEN, 64], [1, 512]]),
                          st[0:64, :])
                    sv = r.dma("sync", "dS",
                               bass.AP(tensor=vt_dram.tensor,
                                       offset=(160 * hp + 80) * S_LEN + 512 * tg,
                                       ap=[[S_LEN, 64], [1, 512]]),
                               st[64:128, :])
                    r.wait("scalar", "dS", sv)
                r.wait("sync", "T", r.val("T"))
            r.barrier()

            for tt in range(NT):
                r.dma("sync", "dS", vtok[tt],
                      vt_dram[:, 128 * tt:128 * (tt + 1)], transpose=True)
            vtok_ld = r.val("dS")
            r.wait("tensor", "dS", vtok_ld)

            for hp in range(8):
                kv0 = 128 * hp
                wlq = r.dma("sync", "dS",
                            wslab[0].rearrange("p (dc c) -> p dc c", c=128),
                            wqkv_dram[0][:, kv0:kv0 + 128].rearrange(
                                "(dc p) c -> p dc c", p=128))
                wlk = r.dma("sync", "dS",
                            wslab[1].rearrange("p (dc c) -> p dc c", c=128),
                            wqkv_dram[1][:, kv0:kv0 + 128].rearrange(
                                "(dc p) c -> p dc c", p=128))
                r.wait("tensor", "dS", wlk)
                qkg = {}
                for wi, dstp in ((0, qTp), (1, kTp)):
                    for tg in range(4):
                        gidx = 4 * wi + tg
                        bk = gidx % 4
                        if gidx - 4 in qkg:
                            r.wait("tensor", "S", qkg[gidx - 4])
                        h_, tv = tinc()
                        for dc in range(8):
                            if dc < 7:
                                E("tensor", lambda T, wi=wi, tg=tg, dc=dc, bk=bk: T.matmul(
                                    ps[bk], wslab[wi][:, 128 * dc:128 * (dc + 1)],
                                    ln1T[dc][:, 512 * tg:512 * (tg + 1)],
                                    start=(dc == 0), stop=False))
                            else:
                                E("tensor", lambda T, wi=wi, tg=tg, dc=dc, bk=bk, h_=h_:
                                  T.matmul(ps[bk], wslab[wi][:, 128 * dc:128 * (dc + 1)],
                                           ln1T[dc][:, 512 * tg:512 * (tg + 1)],
                                           start=False, stop=True).then_inc(h_, 1))
                        r.wait("scalar", "T", tv)
                        E("scalar", lambda S, dstp=dstp, tg=tg, bk=bk: S.copy(
                            dstp[:, 512 * tg:512 * (tg + 1)], ps[bk]))
                        sc = r.inc("scalar", "S")
                        qkg[gidx] = sc
                r.wait("tensor", "S", sc)

                r.wait("sync", "T", r.val("T"))
                for side, mbuf in ((0, m_e), (1, m_o)):
                    h = 2 * hp + side
                    ld = r.dma("sync", "dS", mbuf,
                               bass.AP(tensor=g_dram.tensor, offset=h * GN,
                                       ap=[[1, 128], [1, MU]]))
                r.wait("tensor", "dS", ld)

                sc_hist = {}
                vv_hist = {}
                pv_hist = {}
                pv_guard = None
                dminus_, dplus_ = _sat_range()
                def tclass(kt, qg):
                    dlo = 128 * kt - 512 * qg - 511
                    dhi = 128 * kt - 512 * qg + 127
                    if dhi <= dminus_:
                        return "lo"
                    if dlo >= dplus_:
                        return "hi"
                    return "mid"
                for qg in range(4):
                    q_sl = slice(512 * qg, 512 * (qg + 1))
                    for kt in range(NT + 2):
                        if kt < NT:
                            j = kt % 3
                            if (qg, kt - 3) in sc_hist:
                                r.wait("tensor", "S", sc_hist[(qg, kt - 3)])
                            off = min(max(U0 - (128 * kt - 512 * qg), 0), MU - 512)
                            E("tensor", lambda T, kt=kt, q_sl=q_sl, j=j: T.matmul(
                                ps[2 * j], kTp[0:64, 128 * kt:128 * (kt + 1)],
                                qTp[0:64, q_sl], start=True, stop=False,
                                tile_position=(0, 0)))
                            E("tensor", lambda T, kt=kt, q_sl=q_sl, j=j: T.matmul(
                                ps[2 * j + 1], kTp[64:128, 128 * kt:128 * (kt + 1)],
                                qTp[64:128, q_sl], start=True, stop=False,
                                tile_position=(64, 0)))
                            E("tensor", lambda T, j=j, off=off: T.matmul(
                                ps[2 * j], j_sb[:], m_e[:, off:off + 512],
                                start=False, stop=True))
                            h_, tv_s = tinc()
                            E("tensor", lambda T, j=j, off=off, h_=h_: T.matmul(
                                ps[2 * j + 1], j_sb[:], m_o[:, off:off + 512],
                                start=False, stop=True).then_inc(h_, 1))
                            ewg = ew[kt % 3]
                            r.wait("scalar", "T", tv_s)
                            if (qg, kt - 3) in pv_hist:
                                r.wait("scalar", "T", pv_hist[(qg, kt - 3)])
                            E("scalar", lambda S, ewg=ewg, j=j: S.activation(
                                out=ewg[:, 0:1024],
                                in_=PS[:, 1024 * j:1024 * (j + 1)],
                                func=AF.Exp, bias=neg40c[:], scale=1.0))
                            sc_hist[(qg, kt)] = r.inc("scalar", "S")
                        if kt >= 2:
                            pkt = kt - 2
                            ewp = ew[pkt % 3]
                            if pkt == 0 and pv_guard is not None:
                                r.wait("tensor", "S", pv_guard)
                            r.wait("tensor", "S", sc_hist[(qg, pkt)])
                            E("tensor", lambda T, hp=hp, pkt=pkt, ewp=ewp: T.matmul(
                                PS[0:65, 3072:3584],
                                vtok[pkt][:, 160 * hp:160 * hp + 65],
                                ewp[:, 0:512],
                                start=(pkt == 0), stop=(pkt == NT - 1)))
                            h_, pvv = tinc()
                            E("tensor", lambda T, hp=hp, pkt=pkt, ewp=ewp, h_=h_:
                              T.matmul(PS[0:65, 3584:4096],
                                       vtok[pkt][:, 160 * hp + 80:160 * hp + 145],
                                       ewp[:, 512:1024],
                                       start=(pkt == 0), stop=(pkt == NT - 1))
                              .then_inc(h_, 1))
                            pv_hist[(qg, pkt)] = pvv
                    r.wait("scalar", "T", pv_hist[(qg, NT - 1)])
                    r.wait("scalar", "dG", r.val("dG"))
                    E("scalar", lambda S, q_sl=q_sl: S.copy(oTu[0:64, q_sl],
                                                            PS[0:64, 3072:3584]))
                    E("scalar", lambda S, q_sl=q_sl: S.copy(oTu[64:128, q_sl],
                                                            PS[0:64, 3584:4096]))
                    E("scalar", lambda S: S.copy(denst[64:65, 0:512], PS[64:65, 3072:3584]))
                    E("scalar", lambda S: S.copy(denst[64:65, 512:1024], PS[64:65, 3584:4096]))
                    sc = r.inc("scalar", "S")
                    sc_hist[(qg, NT - 1)] = sc
                    sc_hist[(qg, NT - 2)] = sc
                    sc_hist[(qg, NT - 3)] = sc
                    pv_guard = sc
                    r.wait("gpsimd", "S", sc)
                    if qg == 0:
                        r.wait("gpsimd", "V", r.val("V"))
                    s_ = r.sem["dG"]
                    s_[1] += 32
                    E("gpsimd", lambda G, qg=qg, h=s_[0]: (
                        G.dma_start(out=pairdenP[:, 8 * qg:8 * qg + 4],
                                    in_=denst[64:65, 0:512]).then_inc(h, 16),
                        G.dma_start(out=pairdenP[:, 8 * qg + 4:8 * qg + 8],
                                    in_=denst[64:65, 512:1024]).then_inc(h, 16)))
                r.wait("vector", "dG", r.val("dG"))
                r.wait("vector", "dS")
                E("vector", lambda V: V.drain())
                E("vector", lambda V: V.reciprocal(
                    pairrecP[:], pairdenP[:]))
                vv = r.inc("vector", "V")
                r.wait("sync", "V", vv)
                for qg in range(4):
                    r.dma("sync", "dS", pairrec[0:1, 512 * qg:512 * (qg + 1)],
                          pairrecP[:, 8 * qg:8 * qg + 4])
                    sv_sc = r.dma("sync", "dS",
                                  pairrec[1:2, 512 * qg:512 * (qg + 1)],
                                  pairrecP[:, 8 * qg + 4:8 * qg + 8])
                r.wait("tensor", "dS", sv_sc)
                for qg in range(4):
                    E("tensor", lambda T, qg=qg: T.matmul(
                        ps[6], sel2_sb[:], pairrec[:, 512 * qg:512 * (qg + 1)],
                        start=True, stop=True))
                    tv = r.inc("tensor", "T")
                    r.wait("scalar", "T", tv)
                    E("scalar", lambda S, qg=qg: S.copy(
                        recB[:, 512 * qg:512 * (qg + 1)], ps[6]))
                    sc = r.inc("scalar", "S")
                    r.wait("tensor", "S", sc)
                r.wait("vector", "S", r.val("S"))
                E("vector", lambda V: V.tensor_tensor(
                    out=oTp, in0=oTu, in1=recB[:], op=ALU.mult))
                vv = r.inc("vector", "V")
                r.wait("scalar", "V", vv)
                r.wait("sync", "V", vv)
                sv = r.dma("sync", "dS", ot_dram[128 * hp:128 * (hp + 1), :], oTp)
                r.wait("vector", "dS", sv)
            r.barrier()

            for kvc in range(8):
                r.dma("sync", "dS",
                      woslab[kvc].rearrange("p (dc c) -> p dc c", c=128),
                      wqkv_dram[3][128 * kvc:128 * (kvc + 1), :].rearrange(
                          "p (dc c) -> p dc c", c=128))
            for tg in range(4):
                for kvc in range(8):
                    r.dma("sync", "dS", oTld[kvc],
                          ot_dram[128 * kvc:128 * (kvc + 1), 512 * tg:512 * (tg + 1)])
                ld = r.val("dS")
                r.wait("tensor", "dS", ld)
                for dct in range(8):
                    bk = dct % 2
                    h_, tv = tinc()
                    for kvc in range(8):
                        if kvc < 7:
                            E("tensor", lambda T, dct=dct, kvc=kvc, bk=bk: T.matmul(
                                ps[bk], woslab[kvc][:, 128 * dct:128 * (dct + 1)],
                                oTld[kvc], start=(kvc == 0), stop=False))
                        else:
                            E("tensor", lambda T, dct=dct, kvc=kvc, bk=bk, h_=h_:
                              T.matmul(ps[bk], woslab[kvc][:, 128 * dct:128 * (dct + 1)],
                                       oTld[kvc], start=False, stop=True).then_inc(h_, 1))
                    r.wait("scalar", "T", tv)
                    st = evst[dct % 2]
                    E("scalar", lambda S, st=st, bk=bk: S.copy(st[:], ps[bk]))
                    sc = r.inc("scalar", "S")
                    r.wait("vector", "S", sc)
                    E("vector", lambda V, st=st, bk=bk: V.tensor_tensor(
                        out=qst_o[:, 0:512], in0=ps[bk], in1=st[:],
                        op=ALU.subtract))
                    vv = r.inc("vector", "V")
                    r.wait("tensor", "V", vv)
                    r.wait("sync", "V", vv)
                    r.dma("sync", "dS",
                          yat_dram[128 * dct:128 * (dct + 1),
                                   512 * tg:512 * (tg + 1)], st[:])
                    sv = r.dma("sync", "dS",
                               yat2_dram[128 * dct:128 * (dct + 1),
                                         512 * tg:512 * (tg + 1)], qst_o[:, 0:512])
                    r.wait("scalar", "dS", sv)
                    r.wait("vector", "dS", sv)
                r.wait("sync", "T", r.val("T"))
            r.barrier()

            for tt in range(NT):
                xt, ya = xtile[tt % 2], yatile[tt % 2]
                r.guard("sync", f"xtile{tt % 2}")
                r.guard("sync", f"yatile{tt % 2}")
                r.guard("sync", "lntbuf")
                r.dma("sync", "dS", xt[:], x_rows(tt))
                ld = r.dma("sync", "dS", ya[:],
                           yat_dram[:, 128 * tt:128 * (tt + 1)], transpose=True)
                ld = r.dma("sync", "dS", lnt[:],
                           yat2_dram[:, 128 * tt:128 * (tt + 1)], transpose=True)
                r.wait("vector", "dS", ld)
                E("vector", lambda V, xt=xt, ya=ya: V.tensor_tensor(
                    out=x1t[:], in0=xt[:], in1=ya[:], op=ALU.add))
                VD()
                E("vector", lambda V: V.tensor_tensor(
                    out=x1t[:], in0=x1t[:], in1=lnt[:], op=ALU.add))
                vv = r.inc("vector", "V")
                r.set_guard(f"xtile{tt % 2}", "V", vv)
                r.set_guard(f"yatile{tt % 2}", "V", vv)
                r.set_guard("lntbuf", "V", vv)
                r.wait("sync", "V", vv)
                sv = r.dma("sync", "dS", x1_dram[128 * tt:128 * (tt + 1), :], x1t[:])
                r.wait("vector", "dS", sv)
            r.barrier()

            for tt in range(NT):
                ln_tile(x1_dram[128 * tt:128 * (tt + 1), :], xq_dram, tt, True)
            r.barrier()

            for dc in range(8):
                for hg in range(8):
                    r.dma("sync", "dS", w1qT[dc][:, 512 * hg:512 * (hg + 1)],
                          w1q_dram[512 * hg:512 * (hg + 1), 128 * dc:128 * (dc + 1)],
                          transpose=True)
            for dc in range(8):
                for tg in range(4):
                    r.dma("sync", "dS", xqT[dc][:, 512 * tg:512 * (tg + 1)],
                          xq_dram[512 * tg:512 * (tg + 1), 128 * dc:128 * (dc + 1)],
                          transpose=True)
            mm1_ld = r.val("dS")
            r.wait("tensor", "dS", mm1_ld)
            for tt in range(NT):
                h_, tv = tinc()
                for dc in range(8):
                    for hg in range(8):
                        if dc == 7 and hg == 7:
                            E("tensor", lambda T, tt=tt, dc=dc, hg=hg, h_=h_: T.matmul(
                                ps[hg], xqT[dc][:, 128 * tt:128 * (tt + 1)],
                                w1qT[dc][:, 512 * hg:512 * (hg + 1)],
                                start=False, stop=True).then_inc(h_, 1))
                        else:
                            E("tensor", lambda T, tt=tt, dc=dc, hg=hg: T.matmul(
                                ps[hg], xqT[dc][:, 128 * tt:128 * (tt + 1)],
                                w1qT[dc][:, 512 * hg:512 * (hg + 1)],
                                start=(dc == 0), stop=(dc == 7)))
                r.wait("scalar", "T", tv)
                for hg in range(8):
                    E("scalar", lambda S, tt=tt, hg=hg: S.activation(
                        out=hbuf[:, 512 * hg:512 * (hg + 1)], in_=ps[hg],
                        func=AF.Gelu, bias=zeroc[:], scale=sxcols[:, tt:tt + 1]))
                sc = r.inc("scalar", "S")
                r.wait("tensor", "S", sc)
                r.wait("vector", "S", sc)
                r.guard("vector", "stc")
                E("vector", lambda V: V.reduce_max(
                    stc["mcol"][:], hbuf, axis=mybir.AxisListType.X,
                    apply_absolute_value=True))
                VD()
                E("vector", lambda V: V.scalar_tensor_tensor(
                    out=stc["mclip"][:], in0=stc["mcol"][:], scalar=EPS,
                    in1=onec[:], op0=ALU.max, op1=ALU.mult))
                VD()
                E("vector", lambda V: V.reciprocal(
                    stc["invm"][:], stc["mclip"][:]))
                VD()
                E("vector", lambda V: V.scalar_tensor_tensor(
                    out=stc["s127"][:], in0=stc["invm"][:], scalar=127.0,
                    in1=zeroc[:], op0=ALU.mult, op1=ALU.add))
                E("vector", lambda V: V.tensor_tensor(
                    out=stc["tmp2"][:], in0=stc["mclip"][:], in1=s2col[:], op=ALU.mult))
                VD()
                E("vector", lambda V, tt=tt: V.scalar_tensor_tensor(
                    out=sc2cols[:, tt:tt + 1], in0=stc["tmp2"][:], scalar=1.0 / 127.0,
                    in1=zeroc[:], op0=ALU.mult, op1=ALU.add))
                vv = r.inc("vector", "V")
                r.wait("scalar", "V", vv)
                for cc in range(4):
                    csl = slice(1024 * cc, 1024 * (cc + 1))
                    E("scalar", lambda S, csl=csl: S.activation(
                        out=qst_t[:], in_=hbuf[:, csl], func=AF.Identity,
                        bias=halfc[:], scale=stc["s127"][:]))
                    sc = r.inc("scalar", "S")
                    if csl.stop == HID:
                        r.set_guard("stc", "S", sc)
                    r.wait("vector", "S", sc)
                    floor_chain(qst_t[:], hqbuf[:, csl], 1)
                    vv = r.inc("vector", "V")
                    r.wait("scalar", "V", vv)
                r.wait("sync", "V", r.val("V"))
                sv = r.dma("sync", "dS", hq_dram[128 * tt:128 * (tt + 1), :], hqbuf)
                r.wait("vector", "dS", sv)
                r.wait("scalar", "dS", sv)
            r.barrier()

            for hc in range(32):
                for dg in range(2):
                    r.dma("sync", "dS", w2qT[hc][:, 512 * dg:512 * (dg + 1)],
                          w2q_dram[512 * dg:512 * (dg + 1), 128 * hc:128 * (hc + 1)],
                          transpose=True)
            w2_ld = r.val("dS")
            for tg in range(4):
                for hc in range(32):
                    r.dma("sync", "dS", hqT[hc],
                          hq_dram[512 * tg:512 * (tg + 1), 128 * hc:128 * (hc + 1)],
                          transpose=True)
                slab_ld = r.val("dS")
                r.wait("tensor", "dS", max(w2_ld, slab_ld))
                for ttl in range(4):
                    tt = 4 * tg + ttl
                    for dg in range(2):
                        bk = (2 * ttl + dg) % 2
                        h_, tv = tinc()
                        for hc in range(32):
                            if hc < 31:
                                E("tensor", lambda T, ttl=ttl, dg=dg, hc=hc, bk=bk:
                                  T.matmul(ps[bk],
                                           hqT[hc][:, 128 * ttl:128 * (ttl + 1)],
                                           w2qT[hc][:, 512 * dg:512 * (dg + 1)],
                                           start=(hc == 0), stop=False))
                            else:
                                E("tensor", lambda T, ttl=ttl, dg=dg, hc=hc, bk=bk, h_=h_:
                                  T.matmul(ps[bk],
                                           hqT[hc][:, 128 * ttl:128 * (ttl + 1)],
                                           w2qT[hc][:, 512 * dg:512 * (dg + 1)],
                                           start=False, stop=True).then_inc(h_, 1))
                        xt = xtile[(2 * ttl + dg) % 2]
                        r.guard("sync", f"xtile{(2 * ttl + dg) % 2}")
                        ld = r.dma("sync", "dS", xt[:, 0:512],
                                   x1_dram[128 * tt:128 * (tt + 1),
                                           512 * dg:512 * (dg + 1)])
                        r.wait("vector", "dS", ld)
                        r.wait("vector", "T", tv)
                        E("vector", lambda V, tt=tt, bk=bk, xt=xt:
                          V.scalar_tensor_tensor(
                              out=qst_t[:, 0:512], in0=ps[bk],
                              scalar=sc2cols[:, tt:tt + 1], in1=xt[:, 0:512],
                              op0=ALU.mult, op1=ALU.add))
                        vv = r.inc("vector", "V")
                        r.set_guard(f"xtile{(2 * ttl + dg) % 2}", "V", vv)
                        r.wait("tensor", "V", vv)
                        r.wait("sync", "V", vv)
                        sv = r.dma("sync", "dS",
                                   y_out[b * S_LEN + 128 * tt:
                                         b * S_LEN + 128 * (tt + 1),
                                         512 * dg:512 * (dg + 1)], qst_t[:, 0:512])
                        r.wait("vector", "dS", sv)
                r.wait("sync", "T", r.val("T"))
            r.barrier()

        for b in range(B_CORE):
            batch_program(b)

        r.wait("sync", "dS")
        r.wait("sync", "dG")

        with nc.Block() as block:
            @block.sync
            def _(eng):
                for fn in r.ops["sync"]:
                    fn(eng)

            @block.vector
            def _(eng):
                for fn in r.ops["vector"]:
                    fn(eng)

            @block.scalar
            def _(eng):
                for fn in r.ops["scalar"]:
                    fn(eng)

            @block.tensor
            def _(eng):
                for fn in r.ops["tensor"]:
                    fn(eng)

            @block.gpsimd
            def _(eng):
                for fn in r.ops["gpsimd"]:
                    fn(eng)

    return nc


_NC_CACHE = None


def kernel(L0_attn_inputs, L0_attn_Wq, L0_attn_Wk, L0_attn_Wv, L0_attn_Wo,
           L0_attn_rel_emb, L0_ffn_w1, L0_ffn_w2):
    global _NC_CACHE
    x = np.ascontiguousarray(np.asarray(L0_attn_inputs, dtype=np.float32))
    consts = host_constants()
    if _NC_CACHE is None:
        _NC_CACHE = build_nc()
    nc = _NC_CACHE
    shared = {
        "wq_in": np.ascontiguousarray(L0_attn_Wq, dtype=np.float32),
        "wk_in": np.ascontiguousarray(L0_attn_Wk, dtype=np.float32),
        "wv_in": np.ascontiguousarray(L0_attn_Wv, dtype=np.float32),
        "wo_in": np.ascontiguousarray(L0_attn_Wo, dtype=np.float32),
        "rel_in": np.ascontiguousarray(L0_attn_rel_emb, dtype=np.float32),
        "w1_in": np.ascontiguousarray(L0_ffn_w1, dtype=np.float32),
        "w2_in": np.ascontiguousarray(L0_ffn_w2, dtype=np.float32),
        **consts,
    }
    in_maps = []
    for c in range(N_CORES):
        xi = x[B_CORE * c: B_CORE * (c + 1)].reshape(TOK, D)
        in_maps.append({"x_in": np.ascontiguousarray(xi), **shared})
    res = run_bass_kernel_spmd(nc, in_maps, core_ids=list(range(N_CORES)))
    out = np.concatenate(
        [np.asarray(r["y_out"]).reshape(B_CORE, S_LEN, D) for r in res.results],
        axis=0)
    return out.astype(np.float32)


# version 26
# speedup vs baseline: 1.6604x; 1.0083x over previous
import numpy as np
import ml_dtypes
import concourse.bass as bass
import concourse.mybir as mybir
from concourse.bass_utils import run_bass_kernel_spmd

F32 = mybir.dt.float32
BF16 = mybir.dt.bfloat16
I32 = mybir.dt.int32
AF = mybir.ActivationFunctionType
ALU = mybir.AluOpType

B_TOTAL, S_LEN, D = 16, 2048, 1024
N_HEADS, D_KV = 16, 64
HID = 4096
N_CORES = 8
B_CORE = B_TOTAL // N_CORES
TOK = B_CORE * S_LEN
NT = S_LEN // 128
EPS = 1e-5
SHIFT = 40.0
NUM_BUCKETS, MAX_DIST = 32, 128
U0 = 1920
MU = 3968
GN = 4096

ENGINES = ("sync", "vector", "scalar", "tensor", "gpsimd")


def _bucket_table():
    d = np.arange(-2048, 2048)
    n = -d
    ret = np.where(n < 0, 16, 0).astype(np.int64)
    n = np.abs(n)
    is_small = n < 8
    nf = np.maximum(n, 1).astype(np.float32)
    t = np.log(nf / np.float32(8.0)).astype(np.float32)
    t = (t / np.float32(np.log(MAX_DIST / 8.0))).astype(np.float32)
    t = (t * np.float32(8.0)).astype(np.float32)
    val = 8 + t.astype(np.int32)
    val = np.minimum(val, 15)
    return (ret + np.where(is_small, n, val)).astype(np.int64)


def _sat_range():
    bt = _bucket_table()
    lo_val = bt[0]
    d = -2048
    while bt[d + 1 + 2048] == lo_val:
        d += 1
    dminus = d
    hi_val = bt[2047 + 2048]
    d = 2047
    while bt[d - 1 + 2048] == hi_val:
        d -= 1
    dplus = d
    return dminus, dplus


def host_constants():
    bt = _bucket_table()
    gsel = np.zeros((NUM_BUCKETS, GN), dtype=np.float32)
    i = np.arange(GN)
    dd = 2047 - i
    gsel[bt[dd + 2048], i] = 1.0
    jmat = np.zeros((128, 128), dtype=ml_dtypes.bfloat16)
    jmat[np.arange(128), 127 - np.arange(128)] = 1.0
    sel2 = np.zeros((2, 128), dtype=np.float32)
    sel2[0, 0:64] = 1.0
    sel2[1, 64:128] = 1.0
    eblo = np.zeros((NUM_BUCKETS, 1), dtype=np.float32)
    eblo[bt[0], 0] = 1.0
    return {"gsel_c": gsel, "jmat_c": jmat, "sel2_c": sel2, "eblo_c": eblo}


class Rec:
    def __init__(self):
        self.ops = {e: [] for e in ENGINES}
        self.sem = {}

    def add_sem(self, name, handle):
        self.sem[name] = [handle, 0]

    def emit(self, eng, fn):
        self.ops[eng].append(fn)

    def inc(self, eng, sname):
        s = self.sem[sname]
        s[1] += 1
        h = s[0]
        if eng in ("vector", "scalar", "tensor"):
            self.emit(eng, lambda E, h=h: (E.drain(), E.nop().then_inc(h, 1)))
        else:
            self.emit(eng, lambda E, h=h: E.nop().then_inc(h, 1))
        return s[1]

    def dma(self, eng, sname, out, in_, transpose=False):
        s = self.sem[sname]
        s[1] += 16
        h = s[0]
        if transpose:
            self.emit(eng, lambda E, h=h, out=out, in_=in_:
                      E.dma_start_transpose(out, in_).then_inc(h, 16))
        else:
            self.emit(eng, lambda E, h=h, out=out, in_=in_:
                      E.dma_start(out=out, in_=in_).then_inc(h, 16))
        return s[1]

    def wait(self, eng, sname, val=None):
        s = self.sem[sname]
        v = s[1] if val is None else val
        if sname in ("dS", "dG"):
            v = s[1]
            issuer = "sync" if sname == "dS" else "gpsimd"
            if not hasattr(self, "_sw"):
                self._sw = {}
            if self._sw.get(sname, -1) < v:
                self._sw[sname] = v
                if eng != issuer:
                    self.emit(issuer, lambda E, h=s[0], v=v: E.wait_ge(h, v))
        self.emit(eng, lambda E, h=s[0], v=v: E.wait_ge(h, v))

    def val(self, sname):
        return self.sem[sname][1]

    def guard(self, eng, key):
        if key in getattr(self, "bguard", {}):
            sname, v = self.bguard[key]
            self.wait(eng, sname, v)

    def set_guard(self, key, sname, v):
        if not hasattr(self, "bguard"):
            self.bguard = {}
        self.bguard[key] = (sname, v)

    def barrier(self):
        self.wait("sync", "dS")
        self.wait("gpsimd", "dG")
        for e in ENGINES:
            self.inc(e, "ph_" + e)
        for e in ENGINES:
            for o in ENGINES:
                if o != e:
                    self.wait(e, "ph_" + o)


def build_nc(debug=False):
    nc = bass.Bass("TRN2", target_bir_lowering=False, debug=False)
    r = Rec()

    x_in = nc.dram_tensor("x_in", [TOK, D], F32, kind="ExternalInput").ap()
    wq_in = nc.dram_tensor("wq_in", [D, D], F32, kind="ExternalInput").ap()
    wk_in = nc.dram_tensor("wk_in", [D, D], F32, kind="ExternalInput").ap()
    wv_in = nc.dram_tensor("wv_in", [D, D], F32, kind="ExternalInput").ap()
    wo_in = nc.dram_tensor("wo_in", [D, D], F32, kind="ExternalInput").ap()
    rel_in = nc.dram_tensor("rel_in", [NUM_BUCKETS, N_HEADS], F32, kind="ExternalInput").ap()
    w1_in = nc.dram_tensor("w1_in", [HID, D], F32, kind="ExternalInput").ap()
    w2_in = nc.dram_tensor("w2_in", [D, HID], F32, kind="ExternalInput").ap()
    gsel_c = nc.dram_tensor("gsel_c", [NUM_BUCKETS, GN], F32, kind="ExternalInput").ap()
    jmat_c = nc.dram_tensor("jmat_c", [128, 128], BF16, kind="ExternalInput").ap()
    sel2_c = nc.dram_tensor("sel2_c", [2, 128], F32, kind="ExternalInput").ap()
    eblo_c = nc.dram_tensor("eblo_c", [NUM_BUCKETS, 1], F32, kind="ExternalInput").ap()
    y_out = nc.dram_tensor("y_out", [TOK, D], F32, kind="ExternalOutput").ap()

    KD = "ExternalOutput" if debug else "Internal"
    g_dram = nc.dram_tensor("g_dram", [N_HEADS, GN], BF16, kind=KD).ap()
    wqkv_dram = [nc.dram_tensor(f"w{n}_bf", [D, D], BF16, kind=KD).ap()
                 for n in ("q", "k", "v", "o")]
    w1q_dram = nc.dram_tensor("w1q_dram", [HID, D], BF16, kind=KD).ap()
    w2q_dram = nc.dram_tensor("w2q_dram", [D, HID], BF16, kind=KD).ap()
    ln1_dram = nc.dram_tensor("ln1_dram", [S_LEN, D], BF16, kind=KD).ap()
    vt_dram = nc.dram_tensor("vt_dram", [N_HEADS * 80, S_LEN], BF16, kind=KD).ap()
    ot_dram = nc.dram_tensor("ot_dram", [D, S_LEN], BF16, kind=KD).ap()
    yat_dram = nc.dram_tensor("yat_dram", [D, S_LEN], BF16, kind=KD).ap()
    yat2_dram = nc.dram_tensor("yat2_dram", [D, S_LEN], BF16, kind=KD).ap()
    x1_dram = nc.dram_tensor("x1_dram", [S_LEN, D], F32, kind=KD).ap()
    xq_dram = nc.dram_tensor("xq_dram", [S_LEN, D], BF16, kind=KD).ap()
    hq_dram = nc.dram_tensor("hq_dram", [S_LEN, HID], BF16, kind=KD).ap()

    import contextlib
    es = contextlib.ExitStack()
    with es:
        def sb(name, shape, dt):
            return es.enter_context(nc.sbuf_tensor(name, shape, dt))

        neg40c = sb("neg40c", [128, 1], F32)
        halfc = sb("halfc", [128, 1], F32)
        zeroc = sb("zeroc", [128, 1], F32)
        epscl = sb("epscl", [128, 1], F32)
        onec = sb("onec", [128, 1], F32)
        onesrow = sb("onesrow", [1, 128], F32)
        j_sb = sb("j_sb", [128, 128], BF16)
        sel2_sb = sb("sel2_sb", [2, 128], F32)
        rel_sb = sb("rel_sb", [NUM_BUCKETS, N_HEADS], F32)
        s1col = sb("s1col", [128, 1], F32)
        s2col = sb("s2col", [128, 1], F32)
        rscol = sb("rscol", [128, 1], F32)
        sxcols = sb("sxcols", [128, NT], F32)
        sc2cols = sb("sc2cols", [128, NT], F32)
        relrow = sb("relrow", [1, 2 * N_HEADS], F32)
        bcol_sb = sb("bcol_sb", [128, 2 * N_HEADS], F32)
        nrel15 = sb("nrel15", [128, 1], F32)
        eblo_sb = sb("eblo_sb", [NUM_BUCKETS, 1], F32)
        t11 = sb("t11", [1, 8], F32)
        t11b = sb("t11b", [1, 8], F32)
        t11c = sb("t11c", [1, 8], F32)
        stc = {n: sb("stc_" + n, [128, 1], F32) for n in
               ("s", "q", "mneg", "vq", "m2", "var", "veps", "rec", "r", "mr",
                "mcol", "mclip", "invm", "s127", "tmp", "tmp2")}
        qst_t = sb("qst_t", [128, 1024], F32)
        qst_c = sb("qst_c", [128, 1024], I32)
        qst_b = sb("qst_b", [128, 1024], F32)
        qst_o = sb("qst_o", [128, 1024], BF16)
        qst_u = sb("qst_u", [128, 1024], F32)
        xtile = [sb(f"xtile{i}", [128, D], F32) for i in range(2)]
        yatile = [sb(f"yatile{i}", [128, D], BF16) for i in range(2)]
        x1t = sb("x1t", [128, D], F32)
        lnt = sb("lnt", [128, D], BF16)
        lntf = sb("lntf", [128, D], F32)
        evst = [sb(f"evst{i}", [128, 512], BF16) for i in range(2)]
        wsum = sb("wsum", [128, 32], F32)
        wsumt = sb("wsumt", [128, 1], F32)
        AR_N = 61440
        AR = sb("AR", [128, AR_N], BF16)
        ARF_N = 9216
        ARF = sb("ARF", [128, ARF_N], F32)

        def carve(off, p, n, dt=BF16):
            if dt == BF16:
                return AR[0:p, off:off + n]
            return ARF[0:p, off:off + n]

        ln1T = [carve(2048 * i, 128, 2048) for i in range(8)]
        vtok = [carve(16384 + 1280 * i, 128, 1280) for i in range(16)]
        m_e = carve(36864, 128, MU)
        m_o = carve(40832, 128, MU)
        qTp = carve(44800, 128, 2048)
        kTp = carve(46848, 128, 2048)
        oTp = carve(48896, 128, 2048)
        ebuf = [carve(50944 + 512 * i, 128, 512) for i in range(4)]
        wslab = [carve(52992 + 1024 * i, 128, 1024) for i in range(2)]
        mpp = carve(50944, 128, MU)
        ew = [carve(50944, 128, 1024), carve(52992, 128, 1024),
              carve(56576, 128, 1024)]
        gstage = carve(55040, N_HEADS, 512)
        ones_big = carve(55552, 128, 1024)
        onesr2k = carve(56576, 16, 2048)
        w1qT = [carve(4096 * i, 128, 4096) for i in range(8)]
        xqT = [carve(36864 + 2048 * i, 128, 2048) for i in range(8)]
        hbuf = carve(53248, 128, 4096)
        hqbuf = carve(57344, 128, 4096)
        w2qT = [carve(1024 * i, 128, 1024) for i in range(32)]
        hqT = [carve(36864 + 512 * i, 128, 512) for i in range(32)]
        woslab = [carve(36864 + 1024 * i, 128, 1024) for i in range(8)]
        oTld = [carve(45056 + 512 * i, 128, 512) for i in range(8)]
        oTu = carve(0, 128, 2048, F32)
        recB = carve(2048, 128, 2048, F32)
        denst = carve(4096, 128, 1024, F32)
        pairdenP = carve(5120, 128, 32, F32)
        pairrecP = carve(5184, 128, 32, F32)
        pairrec = carve(7168, 2, 2048, F32)
        gselch = carve(0, NUM_BUCKETS, 512, F32)
        wchunk = carve(512, 128, 2048, F32)
        wchunk2 = carve(4096, 128, 2048, F32)
        qw_b0 = carve(0, 128, 2048)
        qw_u0 = carve(2048, 128, 2048)
        qw_b1 = carve(4096, 128, 2048)
        qw_u1 = carve(6144, 128, 2048)
        ones2k = carve(8192, 128, 2048)

        PS = es.enter_context(nc.psum_tensor("PS", [128, 4096], F32))
        ps = [PS[:, 512 * i:512 * (i + 1)] for i in range(8)]

        for sname in ("dS", "dG", "V", "S", "T") + tuple("ph_" + e for e in ENGINES):
            r.add_sem(sname, es.enter_context(nc.semaphore(sname)))

        E = r.emit

        def VD():
            E("vector", lambda V: V.drain())

        def tinc():
            s = r.sem["T"]
            s[1] += 1
            return s[0], s[1]


        def floor_chain(t_ap, out_ap, K, post=None):
            E("vector", lambda V, t_ap=t_ap: V.scalar_tensor_tensor(
                out=qst_c[:], in0=t_ap, scalar=0.0, in1=t_ap,
                op0=ALU.add, op1=ALU.max))
            VD()
            for j in range(1, K + 1):
                E("vector", lambda V, t_ap=t_ap, j=j: V.scalar_tensor_tensor(
                    out=qst_b[:], in0=qst_c[:], scalar=float(-j + 1),
                    in1=t_ap, op0=ALU.add, op1=ALU.is_gt))
                if j == 1:
                    E("vector", lambda V, t_ap=t_ap, j=j: V.scalar_tensor_tensor(
                        out=qst_u[:], in0=qst_c[:], scalar=float(j),
                        in1=t_ap, op0=ALU.add, op1=ALU.is_le))
                    VD()
                    E("vector", lambda V: V.tensor_tensor(
                        out=qst_u[:], in0=qst_u[:], in1=qst_b[:], op=ALU.subtract))
                else:
                    VD()
                    E("vector", lambda V: V.tensor_tensor(
                        out=qst_u[:], in0=qst_u[:], in1=qst_b[:], op=ALU.subtract))
                    VD()
                    E("vector", lambda V, t_ap=t_ap, j=j: V.scalar_tensor_tensor(
                        out=qst_b[:], in0=qst_c[:], scalar=float(j),
                        in1=t_ap, op0=ALU.add, op1=ALU.is_le))
                    VD()
                    E("vector", lambda V: V.tensor_tensor(
                        out=qst_u[:], in0=qst_u[:], in1=qst_b[:], op=ALU.add))
                VD()
            E("vector", lambda V, out_ap=out_ap: V.tensor_tensor(
                out=out_ap, in0=qst_c[:], in1=qst_u[:], op=ALU.add))

        E("vector", lambda V: V.memset(neg40c[:], -SHIFT))
        E("vector", lambda V: V.memset(halfc[:], 0.5))
        E("vector", lambda V: V.memset(zeroc[:], 0.0))
        E("vector", lambda V: V.memset(epscl[:], EPS))
        E("vector", lambda V: V.memset(onec[:], 1.0))
        E("vector", lambda V: V.memset(onesrow[:], 1.0))
        E("vector", lambda V: V.memset(ones_big, 1.0))
        E("vector", lambda V: V.memset(onesr2k, 1.0))
        E("vector", lambda V: V.memset(ones2k, 1.0))
        v0 = r.inc("vector", "V")
        r.dma("sync", "dS", j_sb[:], jmat_c)
        r.dma("sync", "dS", sel2_sb[:], sel2_c)
        ld0 = r.dma("sync", "dS", rel_sb[:], rel_in)
        r.wait("sync", "V", v0)
        for h in range(N_HEADS):
            r.dma("sync", "dS", vt_dram[80 * h + 64:80 * h + 80, :], onesr2k)

        for c in range(GN // 512):
            gl = r.dma("sync", "dS", gselch, gsel_c[:, 512 * c:512 * (c + 1)])
            r.wait("tensor", "dS", gl)
            if c == 0:
                r.wait("tensor", "dS", ld0)
            E("tensor", lambda T: T.matmul(PS[0:N_HEADS, 0:512], rel_sb[:], gselch,
                                           start=True, stop=True))
            tv = r.inc("tensor", "T")
            r.wait("scalar", "T", tv)
            E("scalar", lambda S: S.activation(out=gstage, in_=PS[0:N_HEADS, 0:512],
                                               func=AF.Identity, bias=zeroc[0:N_HEADS],
                                               scale=1.0))
            sc = r.inc("scalar", "S")
            r.wait("tensor", "S", sc)
            r.wait("sync", "S", sc)
            st_ = r.dma("sync", "dS", g_dram[:, 512 * c:512 * (c + 1)], gstage)
            r.wait("scalar", "dS", st_)
            r.wait("sync", "dS", st_)

        for w_src, w_dst in zip((wq_in, wk_in, wv_in, wo_in), wqkv_dram):
            for dc in range(8):
                sl = slice(128 * dc, 128 * (dc + 1))
                st = wslab[dc % 2]
                s_ = r.sem["dG"]
                s_[1] += 16
                E("gpsimd", lambda G, w_src=w_src, sl=sl, st=st, h=s_[0]:
                  G.dma_start(out=st, in_=w_src[sl, :]).then_inc(h, 16))
                r.wait("sync", "dG", s_[1])
                sv = r.dma("sync", "dS", w_dst[sl, :], st)
                r.wait("gpsimd", "dS", sv)

        wbufs = (wchunk, wchunk2)
        wbg = {0: None, 1: None}
        obufs = ((qw_b0, qw_u0), (qw_b1, qw_u1))
        obg = {0: None, 1: None}
        wq_state = {"g": 0, "o": 0}

        def wquant(w_src, w_dst, nrows, ncols, scol):
            CW = 2048 if ncols % 2048 == 0 else 1024
            RB = 2048 // CW
            nch = nrows // 128
            ncc = ncols // CW
            n_ig = nch // RB

            def loads(buf, ch, cc):
                for rb in range(RB):
                    ld = r.dma("sync", "dS", buf[:, CW * rb:CW * (rb + 1)],
                               w_src[128 * (ch + rb):128 * (ch + rb + 1),
                                     CW * cc:CW * (cc + 1)])
                return ld

            E("vector", lambda V: V.memset(wsum[:], 0.0))
            r.inc("vector", "V")
            for ig in range(n_ig):
                for cc in range(ncc):
                    bi = wq_state["g"] % 2
                    wq_state["g"] += 1
                    buf = wbufs[bi]
                    if wbg[bi]:
                        for sn_, v_ in wbg[bi]:
                            r.wait("sync", sn_, v_)
                    ld = loads(buf, RB * ig, cc)
                    r.wait("vector", "dS", ld)
                    E("vector", lambda V, buf=buf: V.reduce_sum(
                        stc["tmp"][:], buf[:, 0:2048], axis=mybir.AxisListType.X,
                        apply_absolute_value=True))
                    VD()
                    if cc == 0:
                        E("vector", lambda V, ig=ig: V.tensor_tensor(
                            out=wsum[:, ig:ig + 1], in0=stc["tmp"][:], in1=zeroc[:],
                            op=ALU.add))
                    else:
                        E("vector", lambda V, ig=ig: V.tensor_tensor(
                            out=wsum[:, ig:ig + 1], in0=wsum[:, ig:ig + 1],
                            in1=stc["tmp"][:], op=ALU.add))
                    vv = r.inc("vector", "V")
                    wbg[bi] = [("V", vv)]
            E("vector", lambda V: V.drain())
            E("vector", lambda V, n=n_ig: V.reduce_sum(
                wsumt[:], wsum[:, 0:n], axis=mybir.AxisListType.X))
            vv = r.inc("vector", "V")
            r.wait("tensor", "V", vv)
            E("tensor", lambda T: T.matmul(PS[0:1, 0:1], wsumt[:], onec[:],
                                           start=True, stop=True))
            tv = r.inc("tensor", "T")
            r.wait("scalar", "T", tv)
            E("scalar", lambda S, n=nrows * ncols: S.activation(
                out=t11[0:1, 0:1], in_=PS[0:1, 0:1], func=AF.Identity,
                bias=zeroc[0:1], scale=1.0 / n))
            sc = r.inc("scalar", "S")
            r.wait("vector", "S", sc)
            E("vector", lambda V: V.scalar_tensor_tensor(
                out=t11b[0:1, 0:1], in0=t11[0:1, 0:1], scalar=EPS,
                in1=onec[0:1], op0=ALU.max, op1=ALU.mult))
            VD()
            E("vector", lambda V: V.reciprocal(t11[0:1, 0:1], t11b[0:1, 0:1]))
            vv = r.inc("vector", "V")
            r.wait("tensor", "V", vv)
            E("tensor", lambda T: T.matmul(PS[:, 512:513], onesrow[:], t11[0:1, 0:1],
                                           start=True, stop=True))
            tv = r.inc("tensor", "T")
            r.wait("scalar", "T", tv)
            E("scalar", lambda S: S.copy(rscol[:], PS[:, 512:513]))
            sc = r.inc("scalar", "S")
            r.wait("tensor", "S", sc)
            E("tensor", lambda T: T.matmul(PS[:, 512:513], onesrow[:], t11b[0:1, 0:1],
                                           start=True, stop=True))
            tv = r.inc("tensor", "T")
            r.wait("scalar", "T", tv)
            E("scalar", lambda S, scol=scol: S.copy(scol[:], PS[:, 512:513]))
            sc = r.inc("scalar", "S")
            r.wait("vector", "S", sc)
            for ig in range(n_ig):
                for cc in range(ncc):
                    bi = wq_state["g"] % 2
                    wq_state["g"] += 1
                    buf = wbufs[bi]
                    if wbg[bi]:
                        for sn_, v_ in wbg[bi]:
                            r.wait("sync", sn_, v_)
                    ld = loads(buf, RB * ig, cc)
                    r.wait("scalar", "dS", ld)
                    E("scalar", lambda S, buf=buf: S.activation(
                        out=buf[:, 0:2048], in_=buf[:, 0:2048], func=AF.Identity,
                        bias=zeroc[:], scale=rscol[:]))
                    sc = r.inc("scalar", "S")
                    r.wait("vector", "S", sc)
                    oi = wq_state["o"] % 2
                    wq_state["o"] += 1
                    qb, qu = obufs[oi]
                    if obg[oi] is not None:
                        r.wait("vector", "dS", obg[oi])
                    E("vector", lambda V, buf=buf, qb=qb: V.scalar_tensor_tensor(
                        out=qb[:], in0=buf[:, 0:2048], scalar=0.5,
                        in1=ones2k, op0=ALU.is_ge, op1=ALU.mult))
                    E("vector", lambda V, buf=buf, qu=qu: V.scalar_tensor_tensor(
                        out=qu[:], in0=buf[:, 0:2048], scalar=-0.5,
                        in1=ones2k, op0=ALU.is_lt, op1=ALU.mult))
                    VD()
                    E("vector", lambda V, qb=qb, qu=qu: V.tensor_tensor(
                        out=qb[:], in0=qb[:], in1=qu[:], op=ALU.subtract))
                    vv = r.inc("vector", "V")
                    r.wait("sync", "V", vv)
                    for rb in range(RB):
                        sv = r.dma("sync", "dS",
                                   w_dst[128 * (RB * ig + rb):128 * (RB * ig + rb + 1),
                                         CW * cc:CW * (cc + 1)],
                                   qb[:, CW * rb:CW * (rb + 1)])
                    obg[oi] = sv
                    wbg[bi] = [("S", sc), ("V", vv)]

        wquant(w1_in, w1q_dram, HID, D, s1col)
        wquant(w2_in, w2q_dram, D, HID, s2col)
        r.barrier()

        def ln_tile(x_src_ap, dst_dram, tt, quant):
            xt = xtile[tt % 2]
            r.guard("sync", f"xtile{tt % 2}")
            ld = r.dma("sync", "dS", xt[:], x_src_ap)
            r.wait("vector", "dS", ld)
            r.guard("vector", "stc")
            E("vector", lambda V, xt=xt: V.reduce_sum(
                stc["s"][:], xt[:], axis=mybir.AxisListType.X))
            E("vector", lambda V, xt=xt: V.tensor_tensor(
                out=qst_b[:], in0=xt[:], in1=xt[:], op=ALU.mult))
            E("vector", lambda V: V.drain())
            E("vector", lambda V: V.reduce_sum(
                stc["q"][:], qst_b[:], axis=mybir.AxisListType.X))
            VD()
            E("vector", lambda V: V.scalar_tensor_tensor(
                out=stc["mneg"][:], in0=stc["s"][:], scalar=-1.0 / D,
                in1=zeroc[:], op0=ALU.mult, op1=ALU.add))
            E("vector", lambda V: V.scalar_tensor_tensor(
                out=stc["vq"][:], in0=stc["q"][:], scalar=1.0 / D,
                in1=zeroc[:], op0=ALU.mult, op1=ALU.add))
            VD()
            E("vector", lambda V: V.tensor_tensor(
                out=stc["m2"][:], in0=stc["mneg"][:], in1=stc["mneg"][:], op=ALU.mult))
            VD()
            E("vector", lambda V: V.tensor_tensor(
                out=stc["var"][:], in0=stc["vq"][:], in1=stc["m2"][:], op=ALU.subtract))
            VD()
            E("vector", lambda V: V.tensor_tensor(
                out=stc["veps"][:], in0=stc["var"][:], in1=epscl[:], op=ALU.add))
            E("vector", lambda V: V.drain())
            E("vector", lambda V: V.reciprocal(
                stc["rec"][:], stc["veps"][:]))
            vv = r.inc("vector", "V")
            r.wait("scalar", "V", vv)
            E("scalar", lambda S: S.activation(
                out=stc["r"][:], in_=stc["rec"][:], func=AF.Sqrt,
                bias=zeroc[:], scale=1.0))
            sc = r.inc("scalar", "S")
            r.wait("vector", "S", sc)
            VD()
            E("vector", lambda V: V.tensor_tensor(
                out=stc["mr"][:], in0=stc["mneg"][:], in1=stc["r"][:], op=ALU.mult))
            vv = r.inc("vector", "V")
            r.wait("scalar", "V", vv)
            if not quant:
                E("scalar", lambda S, xt=xt: S.activation(
                    out=lnt[:], in_=xt[:], func=AF.Identity,
                    bias=stc["mr"][:], scale=stc["r"][:]))
                sc = r.inc("scalar", "S")
                r.set_guard(f"xtile{tt % 2}", "S", sc)
                r.set_guard("stc", "S", sc)
                r.wait("sync", "S", sc)
                sv = r.dma("sync", "dS", dst_dram[128 * tt:128 * (tt + 1), :], lnt[:])
                r.wait("scalar", "dS", sv)
                return
            E("scalar", lambda S, xt=xt: S.activation(
                out=lntf[:], in_=xt[:], func=AF.Identity,
                bias=stc["mr"][:], scale=stc["r"][:]))
            sc = r.inc("scalar", "S")
            r.set_guard(f"xtile{tt % 2}", "S", sc)
            r.wait("vector", "S", sc)
            E("vector", lambda V: V.reduce_max(
                stc["mcol"][:], lntf[:], axis=mybir.AxisListType.X,
                apply_absolute_value=True))
            VD()
            E("vector", lambda V: V.scalar_tensor_tensor(
                out=stc["mclip"][:], in0=stc["mcol"][:], scalar=EPS,
                in1=onec[:], op0=ALU.max, op1=ALU.mult))
            VD()
            E("vector", lambda V: V.reciprocal(
                stc["invm"][:], stc["mclip"][:]))
            VD()
            E("vector", lambda V: V.scalar_tensor_tensor(
                out=stc["s127"][:], in0=stc["invm"][:], scalar=127.0,
                in1=zeroc[:], op0=ALU.mult, op1=ALU.add))
            E("vector", lambda V: V.tensor_tensor(
                out=stc["tmp2"][:], in0=stc["mclip"][:], in1=s1col[:], op=ALU.mult))
            VD()
            E("vector", lambda V, tt=tt: V.scalar_tensor_tensor(
                out=sxcols[:, tt:tt + 1], in0=stc["tmp2"][:], scalar=1.0 / 127.0,
                in1=zeroc[:], op0=ALU.mult, op1=ALU.add))
            vv = r.inc("vector", "V")
            r.wait("scalar", "V", vv)
            E("scalar", lambda S: S.activation(
                out=qst_t[:], in_=lntf[:], func=AF.Identity,
                bias=halfc[:], scale=stc["s127"][:]))
            sc = r.inc("scalar", "S")
            r.set_guard("stc", "S", sc)
            r.wait("vector", "S", sc)
            floor_chain(qst_t[:], qst_o[:], 2)
            vv = r.inc("vector", "V")
            r.wait("sync", "V", vv)
            sv = r.dma("sync", "dS", dst_dram[128 * tt:128 * (tt + 1), :], qst_o[:])
            r.wait("vector", "dS", sv)
            r.wait("scalar", "dS", sv)

        def batch_program(b):
            def x_rows(tt):
                return x_in[b * S_LEN + 128 * tt: b * S_LEN + 128 * (tt + 1), :]

            for tt in range(NT):
                ln_tile(x_rows(tt), ln1_dram, tt, False)
            r.barrier()

            for dc in range(8):
                for tg in range(4):
                    r.dma("sync", "dS", ln1T[dc][:, 512 * tg:512 * (tg + 1)],
                          ln1_dram[512 * tg:512 * (tg + 1), 128 * dc:128 * (dc + 1)],
                          transpose=True)
            r.wait("sync", "dS")
            ln_ld = r.val("dS")

            r.wait("tensor", "dS", ln_ld)
            avg = {}
            wsg = {}
            evg = {}
            for hp in range(8):
                kv0 = 128 * hp
                if hp % 2 in wsg:
                    r.wait("sync", "T", wsg[hp % 2])
                wl = r.dma("sync", "dS",
                           wslab[hp % 2].rearrange("p (dc c) -> p dc c", c=128),
                           wqkv_dram[2][:, kv0:kv0 + 128].rearrange(
                               "(dc p) c -> p dc c", p=128))
                r.wait("tensor", "dS", wl)
                for tg in range(4):
                    g = 4 * hp + tg
                    bk = g % 4
                    if g - 4 in avg:
                        r.wait("tensor", "S", avg[g - 4])
                    h_, tv = tinc()
                    for dc in range(8):
                        if dc < 7:
                            E("tensor", lambda T, hp=hp, tg=tg, dc=dc, bk=bk: T.matmul(
                                ps[bk], wslab[hp % 2][:, 128 * dc:128 * (dc + 1)],
                                ln1T[dc][:, 512 * tg:512 * (tg + 1)],
                                start=(dc == 0), stop=False))
                        else:
                            E("tensor", lambda T, hp=hp, tg=tg, dc=dc, bk=bk, h_=h_:
                              T.matmul(ps[bk], wslab[hp % 2][:, 128 * dc:128 * (dc + 1)],
                                       ln1T[dc][:, 512 * tg:512 * (tg + 1)],
                                       start=False, stop=True).then_inc(h_, 1))
                    r.wait("scalar", "T", tv)
                    st = evst[tg % 2]
                    if g - 2 in evg:
                        r.wait("scalar", "dS", evg[g - 2])
                    E("scalar", lambda S, st=st, bk=bk: S.copy(st[:], ps[bk]))
                    sc = r.inc("scalar", "S")
                    avg[g] = sc
                    r.wait("sync", "S", sc)
                    r.dma("sync", "dS",
                          bass.AP(tensor=vt_dram.tensor,
                                  offset=(160 * hp) * S_LEN + 512 * tg,
                                  ap=[[S_LEN, 64], [1, 512]]),
                          st[0:64, :])
                    sv = r.dma("sync", "dS",
                               bass.AP(tensor=vt_dram.tensor,
                                       offset=(160 * hp + 80) * S_LEN + 512 * tg,
                                       ap=[[S_LEN, 64], [1, 512]]),
                               st[64:128, :])
                    evg[g] = sv
                wsg[hp % 2] = tv
            r.barrier()

            for tt in range(NT):
                r.dma("sync", "dS", vtok[tt],
                      vt_dram[:, 128 * tt:128 * (tt + 1)], transpose=True)
            vtok_ld = r.val("dS")
            r.wait("tensor", "dS", vtok_ld)

            for hp in range(8):
                kv0 = 128 * hp
                wlq = r.dma("sync", "dS",
                            wslab[0].rearrange("p (dc c) -> p dc c", c=128),
                            wqkv_dram[0][:, kv0:kv0 + 128].rearrange(
                                "(dc p) c -> p dc c", p=128))
                wlk = r.dma("sync", "dS",
                            wslab[1].rearrange("p (dc c) -> p dc c", c=128),
                            wqkv_dram[1][:, kv0:kv0 + 128].rearrange(
                                "(dc p) c -> p dc c", p=128))
                r.wait("tensor", "dS", wlk)
                qkg = {}
                for wi, dstp in ((0, qTp), (1, kTp)):
                    for tg in range(4):
                        gidx = 4 * wi + tg
                        bk = gidx % 4
                        if gidx - 4 in qkg:
                            r.wait("tensor", "S", qkg[gidx - 4])
                        h_, tv = tinc()
                        for dc in range(8):
                            if dc < 7:
                                E("tensor", lambda T, wi=wi, tg=tg, dc=dc, bk=bk: T.matmul(
                                    ps[bk], wslab[wi][:, 128 * dc:128 * (dc + 1)],
                                    ln1T[dc][:, 512 * tg:512 * (tg + 1)],
                                    start=(dc == 0), stop=False))
                            else:
                                E("tensor", lambda T, wi=wi, tg=tg, dc=dc, bk=bk, h_=h_:
                                  T.matmul(ps[bk], wslab[wi][:, 128 * dc:128 * (dc + 1)],
                                           ln1T[dc][:, 512 * tg:512 * (tg + 1)],
                                           start=False, stop=True).then_inc(h_, 1))
                        r.wait("scalar", "T", tv)
                        E("scalar", lambda S, dstp=dstp, tg=tg, bk=bk: S.copy(
                            dstp[:, 512 * tg:512 * (tg + 1)], ps[bk]))
                        sc = r.inc("scalar", "S")
                        qkg[gidx] = sc
                r.wait("tensor", "S", sc)

                r.wait("sync", "T", r.val("T"))
                for side, mbuf in ((0, m_e), (1, m_o)):
                    h = 2 * hp + side
                    ld = r.dma("sync", "dS", mbuf,
                               bass.AP(tensor=g_dram.tensor, offset=h * GN,
                                       ap=[[1, 128], [1, MU]]))
                r.wait("tensor", "dS", ld)

                sc_hist = {}
                vv_hist = {}
                pv_hist = {}
                pv_guard = None
                dminus_, dplus_ = _sat_range()
                def tclass(kt, qg):
                    dlo = 128 * kt - 512 * qg - 511
                    dhi = 128 * kt - 512 * qg + 127
                    if dhi <= dminus_:
                        return "lo"
                    if dlo >= dplus_:
                        return "hi"
                    return "mid"
                for qg in range(4):
                    q_sl = slice(512 * qg, 512 * (qg + 1))
                    for kt in range(NT + 2):
                        if kt < NT:
                            j = kt % 3
                            if (qg, kt - 3) in sc_hist:
                                r.wait("tensor", "S", sc_hist[(qg, kt - 3)])
                            off = min(max(U0 - (128 * kt - 512 * qg), 0), MU - 512)
                            E("tensor", lambda T, kt=kt, q_sl=q_sl, j=j: T.matmul(
                                ps[2 * j], kTp[0:64, 128 * kt:128 * (kt + 1)],
                                qTp[0:64, q_sl], start=True, stop=False,
                                tile_position=(0, 0)))
                            E("tensor", lambda T, kt=kt, q_sl=q_sl, j=j: T.matmul(
                                ps[2 * j + 1], kTp[64:128, 128 * kt:128 * (kt + 1)],
                                qTp[64:128, q_sl], start=True, stop=False,
                                tile_position=(64, 0)))
                            E("tensor", lambda T, j=j, off=off: T.matmul(
                                ps[2 * j], j_sb[:], m_e[:, off:off + 512],
                                start=False, stop=True))
                            h_, tv_s = tinc()
                            E("tensor", lambda T, j=j, off=off, h_=h_: T.matmul(
                                ps[2 * j + 1], j_sb[:], m_o[:, off:off + 512],
                                start=False, stop=True).then_inc(h_, 1))
                            ewg = ew[kt % 3]
                            r.wait("scalar", "T", tv_s)
                            if (qg, kt - 3) in pv_hist:
                                r.wait("scalar", "T", pv_hist[(qg, kt - 3)])
                            E("scalar", lambda S, ewg=ewg, j=j: S.activation(
                                out=ewg[:, 0:1024],
                                in_=PS[:, 1024 * j:1024 * (j + 1)],
                                func=AF.Exp, bias=neg40c[:], scale=1.0))
                            sc_hist[(qg, kt)] = r.inc("scalar", "S")
                        if kt >= 2:
                            pkt = kt - 2
                            ewp = ew[pkt % 3]
                            if pkt == 0 and pv_guard is not None:
                                r.wait("tensor", "S", pv_guard)
                            r.wait("tensor", "S", sc_hist[(qg, pkt)])
                            E("tensor", lambda T, hp=hp, pkt=pkt, ewp=ewp: T.matmul(
                                PS[0:65, 3072:3584],
                                vtok[pkt][:, 160 * hp:160 * hp + 65],
                                ewp[:, 0:512],
                                start=(pkt == 0), stop=(pkt == NT - 1)))
                            h_, pvv = tinc()
                            E("tensor", lambda T, hp=hp, pkt=pkt, ewp=ewp, h_=h_:
                              T.matmul(PS[0:65, 3584:4096],
                                       vtok[pkt][:, 160 * hp + 80:160 * hp + 145],
                                       ewp[:, 512:1024],
                                       start=(pkt == 0), stop=(pkt == NT - 1))
                              .then_inc(h_, 1))
                            pv_hist[(qg, pkt)] = pvv
                    r.wait("scalar", "T", pv_hist[(qg, NT - 1)])
                    r.wait("scalar", "dG", r.val("dG"))
                    E("scalar", lambda S, q_sl=q_sl: S.copy(oTu[0:64, q_sl],
                                                            PS[0:64, 3072:3584]))
                    E("scalar", lambda S, q_sl=q_sl: S.copy(oTu[64:128, q_sl],
                                                            PS[0:64, 3584:4096]))
                    E("scalar", lambda S: S.copy(denst[64:65, 0:512], PS[64:65, 3072:3584]))
                    E("scalar", lambda S: S.copy(denst[64:65, 512:1024], PS[64:65, 3584:4096]))
                    sc = r.inc("scalar", "S")
                    sc_hist[(qg, NT - 1)] = sc
                    sc_hist[(qg, NT - 2)] = sc
                    sc_hist[(qg, NT - 3)] = sc
                    pv_guard = sc
                    r.wait("gpsimd", "S", sc)
                    if qg == 0:
                        r.wait("gpsimd", "V", r.val("V"))
                    s_ = r.sem["dG"]
                    s_[1] += 32
                    E("gpsimd", lambda G, qg=qg, h=s_[0]: (
                        G.dma_start(out=pairdenP[:, 8 * qg:8 * qg + 4],
                                    in_=denst[64:65, 0:512]).then_inc(h, 16),
                        G.dma_start(out=pairdenP[:, 8 * qg + 4:8 * qg + 8],
                                    in_=denst[64:65, 512:1024]).then_inc(h, 16)))
                r.wait("vector", "dG", r.val("dG"))
                r.wait("vector", "dS")
                E("vector", lambda V: V.drain())
                E("vector", lambda V: V.reciprocal(
                    pairrecP[:], pairdenP[:]))
                vv = r.inc("vector", "V")
                r.wait("sync", "V", vv)
                for qg in range(4):
                    r.dma("sync", "dS", pairrec[0:1, 512 * qg:512 * (qg + 1)],
                          pairrecP[:, 8 * qg:8 * qg + 4])
                    sv_sc = r.dma("sync", "dS",
                                  pairrec[1:2, 512 * qg:512 * (qg + 1)],
                                  pairrecP[:, 8 * qg + 4:8 * qg + 8])
                r.wait("tensor", "dS", sv_sc)
                for qg in range(4):
                    E("tensor", lambda T, qg=qg: T.matmul(
                        ps[6], sel2_sb[:], pairrec[:, 512 * qg:512 * (qg + 1)],
                        start=True, stop=True))
                    tv = r.inc("tensor", "T")
                    r.wait("scalar", "T", tv)
                    E("scalar", lambda S, qg=qg: S.copy(
                        recB[:, 512 * qg:512 * (qg + 1)], ps[6]))
                    sc = r.inc("scalar", "S")
                    r.wait("tensor", "S", sc)
                r.wait("vector", "S", r.val("S"))
                E("vector", lambda V: V.tensor_tensor(
                    out=oTp, in0=oTu, in1=recB[:], op=ALU.mult))
                vv = r.inc("vector", "V")
                r.wait("scalar", "V", vv)
                r.wait("sync", "V", vv)
                sv = r.dma("sync", "dS", ot_dram[128 * hp:128 * (hp + 1), :], oTp)
                r.wait("vector", "dS", sv)
            r.barrier()

            for kvc in range(8):
                r.dma("sync", "dS",
                      woslab[kvc].rearrange("p (dc c) -> p dc c", c=128),
                      wqkv_dram[3][128 * kvc:128 * (kvc + 1), :].rearrange(
                          "p (dc c) -> p dc c", c=128))
            for tg in range(4):
                for kvc in range(8):
                    r.dma("sync", "dS", oTld[kvc],
                          ot_dram[128 * kvc:128 * (kvc + 1), 512 * tg:512 * (tg + 1)])
                ld = r.val("dS")
                r.wait("tensor", "dS", ld)
                for dct in range(8):
                    bk = dct % 2
                    h_, tv = tinc()
                    for kvc in range(8):
                        if kvc < 7:
                            E("tensor", lambda T, dct=dct, kvc=kvc, bk=bk: T.matmul(
                                ps[bk], woslab[kvc][:, 128 * dct:128 * (dct + 1)],
                                oTld[kvc], start=(kvc == 0), stop=False))
                        else:
                            E("tensor", lambda T, dct=dct, kvc=kvc, bk=bk, h_=h_:
                              T.matmul(ps[bk], woslab[kvc][:, 128 * dct:128 * (dct + 1)],
                                       oTld[kvc], start=False, stop=True).then_inc(h_, 1))
                    r.wait("scalar", "T", tv)
                    st = evst[dct % 2]
                    E("scalar", lambda S, st=st, bk=bk: S.copy(st[:], ps[bk]))
                    sc = r.inc("scalar", "S")
                    r.wait("vector", "S", sc)
                    E("vector", lambda V, st=st, bk=bk: V.tensor_tensor(
                        out=qst_o[:, 0:512], in0=ps[bk], in1=st[:],
                        op=ALU.subtract))
                    vv = r.inc("vector", "V")
                    r.wait("tensor", "V", vv)
                    r.wait("sync", "V", vv)
                    r.dma("sync", "dS",
                          yat_dram[128 * dct:128 * (dct + 1),
                                   512 * tg:512 * (tg + 1)], st[:])
                    sv = r.dma("sync", "dS",
                               yat2_dram[128 * dct:128 * (dct + 1),
                                         512 * tg:512 * (tg + 1)], qst_o[:, 0:512])
                    r.wait("scalar", "dS", sv)
                    r.wait("vector", "dS", sv)
                r.wait("sync", "T", r.val("T"))
            r.barrier()

            for tt in range(NT):
                xt, ya = xtile[tt % 2], yatile[tt % 2]
                r.guard("sync", f"xtile{tt % 2}")
                r.guard("sync", f"yatile{tt % 2}")
                r.guard("sync", "lntbuf")
                r.dma("sync", "dS", xt[:], x_rows(tt))
                ld = r.dma("sync", "dS", ya[:],
                           yat_dram[:, 128 * tt:128 * (tt + 1)], transpose=True)
                ld = r.dma("sync", "dS", lnt[:],
                           yat2_dram[:, 128 * tt:128 * (tt + 1)], transpose=True)
                r.wait("vector", "dS", ld)
                E("vector", lambda V, xt=xt, ya=ya: V.tensor_tensor(
                    out=x1t[:], in0=xt[:], in1=ya[:], op=ALU.add))
                VD()
                E("vector", lambda V: V.tensor_tensor(
                    out=x1t[:], in0=x1t[:], in1=lnt[:], op=ALU.add))
                vv = r.inc("vector", "V")
                r.set_guard(f"xtile{tt % 2}", "V", vv)
                r.set_guard(f"yatile{tt % 2}", "V", vv)
                r.set_guard("lntbuf", "V", vv)
                r.wait("sync", "V", vv)
                sv = r.dma("sync", "dS", x1_dram[128 * tt:128 * (tt + 1), :], x1t[:])
                r.wait("vector", "dS", sv)
            r.barrier()

            for tt in range(NT):
                ln_tile(x1_dram[128 * tt:128 * (tt + 1), :], xq_dram, tt, True)
            r.barrier()

            for dc in range(8):
                for hg in range(8):
                    r.dma("sync", "dS", w1qT[dc][:, 512 * hg:512 * (hg + 1)],
                          w1q_dram[512 * hg:512 * (hg + 1), 128 * dc:128 * (dc + 1)],
                          transpose=True)
            for dc in range(8):
                for tg in range(4):
                    r.dma("sync", "dS", xqT[dc][:, 512 * tg:512 * (tg + 1)],
                          xq_dram[512 * tg:512 * (tg + 1), 128 * dc:128 * (dc + 1)],
                          transpose=True)
            mm1_ld = r.val("dS")
            r.wait("tensor", "dS", mm1_ld)
            for tt in range(NT):
                h_, tv = tinc()
                for dc in range(8):
                    for hg in range(8):
                        if dc == 7 and hg == 7:
                            E("tensor", lambda T, tt=tt, dc=dc, hg=hg, h_=h_: T.matmul(
                                ps[hg], xqT[dc][:, 128 * tt:128 * (tt + 1)],
                                w1qT[dc][:, 512 * hg:512 * (hg + 1)],
                                start=False, stop=True).then_inc(h_, 1))
                        else:
                            E("tensor", lambda T, tt=tt, dc=dc, hg=hg: T.matmul(
                                ps[hg], xqT[dc][:, 128 * tt:128 * (tt + 1)],
                                w1qT[dc][:, 512 * hg:512 * (hg + 1)],
                                start=(dc == 0), stop=(dc == 7)))
                r.wait("scalar", "T", tv)
                for hg in range(8):
                    E("scalar", lambda S, tt=tt, hg=hg: S.activation(
                        out=hbuf[:, 512 * hg:512 * (hg + 1)], in_=ps[hg],
                        func=AF.Gelu, bias=zeroc[:], scale=sxcols[:, tt:tt + 1]))
                sc = r.inc("scalar", "S")
                r.wait("tensor", "S", sc)
                r.wait("vector", "S", sc)
                r.guard("vector", "stc")
                E("vector", lambda V: V.reduce_max(
                    stc["mcol"][:], hbuf, axis=mybir.AxisListType.X,
                    apply_absolute_value=True))
                VD()
                E("vector", lambda V: V.scalar_tensor_tensor(
                    out=stc["mclip"][:], in0=stc["mcol"][:], scalar=EPS,
                    in1=onec[:], op0=ALU.max, op1=ALU.mult))
                VD()
                E("vector", lambda V: V.reciprocal(
                    stc["invm"][:], stc["mclip"][:]))
                VD()
                E("vector", lambda V: V.scalar_tensor_tensor(
                    out=stc["s127"][:], in0=stc["invm"][:], scalar=127.0,
                    in1=zeroc[:], op0=ALU.mult, op1=ALU.add))
                E("vector", lambda V: V.tensor_tensor(
                    out=stc["tmp2"][:], in0=stc["mclip"][:], in1=s2col[:], op=ALU.mult))
                VD()
                E("vector", lambda V, tt=tt: V.scalar_tensor_tensor(
                    out=sc2cols[:, tt:tt + 1], in0=stc["tmp2"][:], scalar=1.0 / 127.0,
                    in1=zeroc[:], op0=ALU.mult, op1=ALU.add))
                vv = r.inc("vector", "V")
                r.wait("scalar", "V", vv)
                for cc in range(4):
                    csl = slice(1024 * cc, 1024 * (cc + 1))
                    E("scalar", lambda S, csl=csl: S.activation(
                        out=qst_t[:], in_=hbuf[:, csl], func=AF.Identity,
                        bias=halfc[:], scale=stc["s127"][:]))
                    sc = r.inc("scalar", "S")
                    if csl.stop == HID:
                        r.set_guard("stc", "S", sc)
                    r.wait("vector", "S", sc)
                    floor_chain(qst_t[:], hqbuf[:, csl], 1)
                    vv = r.inc("vector", "V")
                    r.wait("scalar", "V", vv)
                r.wait("sync", "V", r.val("V"))
                sv = r.dma("sync", "dS", hq_dram[128 * tt:128 * (tt + 1), :], hqbuf)
                r.wait("vector", "dS", sv)
                r.wait("scalar", "dS", sv)
            r.barrier()

            for hc in range(32):
                for dg in range(2):
                    r.dma("sync", "dS", w2qT[hc][:, 512 * dg:512 * (dg + 1)],
                          w2q_dram[512 * dg:512 * (dg + 1), 128 * hc:128 * (hc + 1)],
                          transpose=True)
            w2_ld = r.val("dS")
            for tg in range(4):
                for hc in range(32):
                    r.dma("sync", "dS", hqT[hc],
                          hq_dram[512 * tg:512 * (tg + 1), 128 * hc:128 * (hc + 1)],
                          transpose=True)
                slab_ld = r.val("dS")
                r.wait("tensor", "dS", max(w2_ld, slab_ld))
                for ttl in range(4):
                    tt = 4 * tg + ttl
                    for dg in range(2):
                        bk = (2 * ttl + dg) % 2
                        h_, tv = tinc()
                        for hc in range(32):
                            if hc < 31:
                                E("tensor", lambda T, ttl=ttl, dg=dg, hc=hc, bk=bk:
                                  T.matmul(ps[bk],
                                           hqT[hc][:, 128 * ttl:128 * (ttl + 1)],
                                           w2qT[hc][:, 512 * dg:512 * (dg + 1)],
                                           start=(hc == 0), stop=False))
                            else:
                                E("tensor", lambda T, ttl=ttl, dg=dg, hc=hc, bk=bk, h_=h_:
                                  T.matmul(ps[bk],
                                           hqT[hc][:, 128 * ttl:128 * (ttl + 1)],
                                           w2qT[hc][:, 512 * dg:512 * (dg + 1)],
                                           start=False, stop=True).then_inc(h_, 1))
                        xt = xtile[(2 * ttl + dg) % 2]
                        r.guard("sync", f"xtile{(2 * ttl + dg) % 2}")
                        ld = r.dma("sync", "dS", xt[:, 0:512],
                                   x1_dram[128 * tt:128 * (tt + 1),
                                           512 * dg:512 * (dg + 1)])
                        r.wait("vector", "dS", ld)
                        r.wait("vector", "T", tv)
                        E("vector", lambda V, tt=tt, bk=bk, xt=xt:
                          V.scalar_tensor_tensor(
                              out=qst_t[:, 0:512], in0=ps[bk],
                              scalar=sc2cols[:, tt:tt + 1], in1=xt[:, 0:512],
                              op0=ALU.mult, op1=ALU.add))
                        vv = r.inc("vector", "V")
                        r.set_guard(f"xtile{(2 * ttl + dg) % 2}", "V", vv)
                        r.wait("tensor", "V", vv)
                        r.wait("sync", "V", vv)
                        sv = r.dma("sync", "dS",
                                   y_out[b * S_LEN + 128 * tt:
                                         b * S_LEN + 128 * (tt + 1),
                                         512 * dg:512 * (dg + 1)], qst_t[:, 0:512])
                        r.wait("vector", "dS", sv)
                r.wait("sync", "T", r.val("T"))
            r.barrier()

        for b in range(B_CORE):
            batch_program(b)

        r.wait("sync", "dS")
        r.wait("sync", "dG")

        with nc.Block() as block:
            @block.sync
            def _(eng):
                for fn in r.ops["sync"]:
                    fn(eng)

            @block.vector
            def _(eng):
                for fn in r.ops["vector"]:
                    fn(eng)

            @block.scalar
            def _(eng):
                for fn in r.ops["scalar"]:
                    fn(eng)

            @block.tensor
            def _(eng):
                for fn in r.ops["tensor"]:
                    fn(eng)

            @block.gpsimd
            def _(eng):
                for fn in r.ops["gpsimd"]:
                    fn(eng)

    return nc


_NC_CACHE = None


def kernel(L0_attn_inputs, L0_attn_Wq, L0_attn_Wk, L0_attn_Wv, L0_attn_Wo,
           L0_attn_rel_emb, L0_ffn_w1, L0_ffn_w2):
    global _NC_CACHE
    x = np.ascontiguousarray(np.asarray(L0_attn_inputs, dtype=np.float32))
    consts = host_constants()
    if _NC_CACHE is None:
        _NC_CACHE = build_nc()
    nc = _NC_CACHE
    shared = {
        "wq_in": np.ascontiguousarray(L0_attn_Wq, dtype=np.float32),
        "wk_in": np.ascontiguousarray(L0_attn_Wk, dtype=np.float32),
        "wv_in": np.ascontiguousarray(L0_attn_Wv, dtype=np.float32),
        "wo_in": np.ascontiguousarray(L0_attn_Wo, dtype=np.float32),
        "rel_in": np.ascontiguousarray(L0_attn_rel_emb, dtype=np.float32),
        "w1_in": np.ascontiguousarray(L0_ffn_w1, dtype=np.float32),
        "w2_in": np.ascontiguousarray(L0_ffn_w2, dtype=np.float32),
        **consts,
    }
    in_maps = []
    for c in range(N_CORES):
        xi = x[B_CORE * c: B_CORE * (c + 1)].reshape(TOK, D)
        in_maps.append({"x_in": np.ascontiguousarray(xi), **shared})
    res = run_bass_kernel_spmd(nc, in_maps, core_ids=list(range(N_CORES)))
    out = np.concatenate(
        [np.asarray(r["y_out"]).reshape(B_CORE, S_LEN, D) for r in res.results],
        axis=0)
    return out.astype(np.float32)
